# version 02
# baseline (speedup 1.0000x reference)
import numpy as np
from contextlib import ExitStack

import concourse.bass as bass
import concourse.mybir as mybir
from concourse.bass_utils import run_bass_kernel_spmd
from concourse.ap import AP as RawAP

F32 = mybir.dt.float32
BF16 = mybir.dt.bfloat16
AF = mybir.ActivationFunctionType
ALU = mybir.AluOpType
AX = mybir.AxisListType

BS = 2
S = 2048
D = 1024
H = 16
E = 8
F = 4096
F2 = 2048
NT = 16
NTB = 4
TSUB = 8


class Planner:
    def __init__(self):
        self.ops = {k: [] for k in ("W", "A", "T", "S", "V")}
        self.ctr = {k: 0 for k in ("W", "A", "T", "S", "V")}
        self.lastw = {}
        self.vspacer = None

    def dma(self, q, out, in_):
        self.ctr[q] += 16

        def f(e, out=out, in_=in_):
            e.dma_start(out=out, in_=in_).then_inc(f.sem, 16)
        f.semname = q
        self.ops[q].append(f)
        return self.ctr[q]

    def op(self, eng, fn, inc=False):
        if eng == "V":
            inc = True
        ret = None
        if inc:
            self.ctr[eng] += 1
            ret = self.ctr[eng]

            def g(e, fn=fn):
                fn(e).then_inc(g.sem, 1)
            g.semname = eng
            self.ops[eng].append(g)
        else:
            self.ops[eng].append(fn)
        if eng == "V":
            self.wait("V", "V", ret)
        return ret

    def vred(self, fn):
        v = self.op("V", fn, inc=True)
        self.wait("V", "V", v)
        return v

    def wait(self, eng, sem, val):
        if not val:
            return
        key = (eng, sem)
        if self.lastw.get(key, 0) >= val:
            return
        self.lastw[key] = val

        def w(e, val=val):
            e.wait_ge(w.sem, val)
        w.semname = sem
        self.ops[eng].append(w)

    def waitp(self, eng, pair):
        if pair:
            self.wait(eng, pair[0], pair[1])


def build_program():
    nc = bass.Bass("TRN2", target_bir_lowering=False)

    def din(name, shape, dt=F32):
        return nc.dram_tensor(name, shape, dt, kind="ExternalInput").ap()

    x_in = din("x", [BS, S, D])
    Ww = din("L0_attn_W_w", [3 * D, D])
    Wb = din("L0_attn_W_b", [3 * D])
    pw = din("L0_attn_proj_w", [D, D])
    pb = din("L0_attn_proj_b", [D])
    lnAg = din("L0_attn_ln_g", [D])
    lnAb = din("L0_attn_ln_b", [D])
    rtw = din("L0_ffn_router_w", [D, E])
    rtb = din("L0_ffn_router_b", [E])
    w1 = din("L0_ffn_w1", [E, D, F])
    b1 = din("L0_ffn_b1", [E, F])
    gt = din("L0_ffn_gate", [E, F])
    w2 = din("L0_ffn_w2", [E, F, D])
    b2 = din("L0_ffn_b2", [E, D])
    sw1 = din("L0_ffn_sw1", [D, F2])
    sb1 = din("L0_ffn_sb1", [F2])
    sgt = din("L0_ffn_sgate", [F2])
    sw2 = din("L0_ffn_sw2", [F2, D])
    sb2 = din("L0_ffn_sb2", [D])
    shw = din("L0_ffn_shared_weight", [1])
    lnFg = din("L0_ffn_ln_g", [D])
    lnFb = din("L0_ffn_ln_b", [D])
    biasz = din("biasz", [H, 128, 2432])
    identf = din("identf", [128, 128])
    epsc = din("epsc", [4])

    out_d = nc.dram_tensor("out", [BS, S, D], F32, kind="ExternalOutput").ap()

    def scr(name, shape, dt):
        return nc.dram_tensor(name, shape, dt, kind="Internal").ap()
    xnT_sc = scr("xnT_sc", [BS, 8, 128, S], BF16)
    qT_sc = scr("qT_sc", [BS, 8, 128, S], BF16)
    kT_sc = scr("kT_sc", [BS, 8, 128, S], BF16)
    v_sc = scr("v_sc", [BS, NT, 128, D], BF16)
    s_sc = scr("s_sc", [BS, NT, 128, D], F32)
    w1b = scr("w1b", [E, D, F], BF16)
    w2b = scr("w2b", [E, F, D], BF16)
    sw1b = scr("sw1b", [D, F2], BF16)
    sw2b = scr("sw2b", [F2, D], BF16)
    sig_rt = scr("sig_rt", [1], F32)

    pl = Planner()
    QW, QA = "W", "A"

    with ExitStack() as ctx:
        sb = lambda n, s, d: ctx.enter_context(nc.sbuf_tensor(n, s, d))
        idf = sb("idf", [128, 128], F32)
        qkb = sb("qkb", [128, 16], F32)
        vb64 = sb("vb64", [64, 16], F32)
        bcA = sb("bcA", [128, 3072], F32)
        rb_b = sb("rb_b", [128, 8], F32)
        rw_f = sb("rw_f", [128, 64], F32)
        rw_sb = sb("rw_sb", [128, 64], BF16)
        b2_bf = sb("b2_bf", [8, 1024], BF16)
        bg_all = sb("bg_all", [128, 256], F32)
        gt_all = sb("gt_all", [128, 256], F32)
        bgs = sb("bgs", [128, 16], F32)
        gts = sb("gts", [128, 16], F32)
        nm_st = sb("nm_st", [128, 32], F32)
        rs_st = sb("rs_st", [128, 32], F32)
        epss = sb("epss", [128, 4], F32)
        sig1 = sb("sig1", [1, 1], F32)
        sigc = sb("sigc", [128, 1], F32)
        AR1 = sb("AR1", [128, 16384], BF16)
        AR2 = sb("AR2", [128, 24576], BF16)
        AR3 = sb("AR3", [128, 8192], F32)
        qtb = [sb("qtb0", [64, 2048], BF16), sb("qtb1", [64, 2048], BF16)]
        ktb = [sb("ktb0", [64, 2048], BF16), sb("ktb1", [64, 2048], BF16)]
        vsb = [sb("vsb0", [128, 1024], BF16), sb("vsb1", [128, 1024], BF16)]
        bias_h = sb("bias_h", [128, 2432], F32)
        oTstage = sb("oTstage", [64, 2048], BF16)
        pTstage = sb("pTstage", [64, 1024], BF16)
        w1t = [sb("w1t0", [128, 1024], BF16), sb("w1t1", [128, 1024], BF16),
               sb("w1t2", [128, 1024], BF16)]
        qkev = [sb("qkev0", [128, 1024], BF16), sb("qkev1", [128, 1024], BF16)]
        xfb = [sb("xfb0", [128, 1024], F32), sb("xfb1", [128, 1024], F32)]
        sfb = [sb("sfb0", [128, 1024], F32), sb("sfb1", [128, 1024], F32)]
        wk = [sb("wk0", [128, 1024], F32), sb("wk1", [128, 1024], F32),
              sb("wk2", [128, 1024], F32)]
        col = sb("colb", [128, 16], F32)
        dum = sb("dum", [1, 1], F32)
        dum2 = sb("dum2", [1, 1], F32)
        pr0 = sb("pr0", [128, 8], F32)
        pre = sb("pre", [128, 8], F32)
        prb = sb("prb", [128, 8], F32)
        msk8 = sb("msk8", [128, 8], F32)
        comb_sb = sb("comb_sb", [128, 64], F32)
        combT = pTstage[0:8, :]
        b2f = bcA[0:8, 0:1024]

        P0 = ctx.enter_context(nc.psum_tensor("P0", [128, 2048], F32))
        P1 = ctx.enter_context(nc.psum_tensor("P1", [128, 2048], F32))

        sems = {k: ctx.enter_context(nc.semaphore("sem" + k))
                for k in ("W", "A", "T", "S", "V")}
        block = ctx.enter_context(nc.Block())

        def BAR():
            pl.op("S", lambda e: e.copy(dum[:, :], dum[:, :]), inc=True)
            pl.op("V", lambda e: e.tensor_copy(dum[:, :], dum[:, :]), inc=True)
            vals = dict(pl.ctr)
            for eng in ("T", "S", "V", "W", "A"):
                for sem_, v_ in vals.items():
                    if sem_ != eng and v_ > 0:
                        pl.wait(eng, sem_, v_)

        xnT_sb = AR1
        h2h = AR1
        WT = AR2
        oTpack = AR2[:, 0:16384]
        pT = AR2[:, 16384:24576]
        w2h = AR2[:, 0:16384]
        xnT_tb = AR2[:, 16384:24576]
        sc_sb = AR3[:, 0:2048]
        pe_sb = AR3[:, 2048:4096]
        pn_sb = AR3[:, 4096:6144]
        cvst = [AR3[:, 6144:7168], AR3[:, 7168:8192]]
        y_acc = AR3

        pl.dma(QA, idf[:, :], identf[:, :])
        a_qkb = pl.dma(QA, qkb[:, :], RawAP(Wb.tensor, 0, [[1, 128], [128, 16]]))
        a_vb = pl.dma(QA, vb64[:, :], RawAP(Wb.tensor, 2048, [[1, 64], [64, 16]]))
        a_rb = pl.dma(QA, rb_b[:, :], RawAP(rtb.tensor, 0, [[0, 128], [1, 8]]))
        a_rw = pl.dma(QA, rw_f[:, :], RawAP(rtw.tensor, 0, [[8, 128], [1024, 8], [1, 8]]))
        a_b2 = pl.dma(QA, b2f[:, :], b2[:, :])
        for e_ in range(E):
            a_bg = pl.dma(QA, bg_all[:, e_ * 32:(e_ + 1) * 32],
                          RawAP(b1.tensor, e_ * 4096, [[1, 128], [128, 32]]))
            a_gt = pl.dma(QA, gt_all[:, e_ * 32:(e_ + 1) * 32],
                          RawAP(gt.tensor, e_ * 4096, [[1, 128], [128, 32]]))
        a_bgs = pl.dma(QA, bgs[:, :], RawAP(sb1.tensor, 0, [[1, 128], [128, 16]]))
        a_gts = pl.dma(QA, gts[:, :], RawAP(sgt.tensor, 0, [[1, 128], [128, 16]]))
        a_eps = pl.dma(QA, epss[:, :], RawAP(epsc.tensor, 0, [[0, 128], [1, 4]]))
        a_sh = pl.dma(QA, sig1[:, :], shw[None, :])
        pl.wait("S", QA, a_rw)
        pl.op("S", lambda e: e.copy(rw_sb[:, :], rw_f[:, :]))
        pl.wait("S", QA, a_b2)
        s_b2 = pl.op("S", lambda e: e.copy(b2_bf[:, :], b2f[:, :]), inc=True)
        pl.wait("S", QA, a_sh)
        s_sig = pl.op("S", lambda e: e.activation(sig1[:, :], sig1[:, :], AF.Sigmoid), inc=True)
        pl.wait(QA, "S", s_sig)
        a_sr = pl.dma(QA, sig_rt[None, :], sig1[:, :])
        pl.wait(QA, QA, a_sr)
        a_sigc = pl.dma(QA, sigc[:, :], RawAP(sig_rt.tensor, 0, [[0, 128], [1, 1]]))
        pl.wait("V", QA, max(a_bg, a_gt))
        v_bg = pl.op("V", lambda e: e.tensor_mul(bg_all[:, :], bg_all[:, :], gt_all[:, :]), inc=True)
        pl.wait("V", QA, max(a_bgs, a_gts))
        v_bgs = pl.op("V", lambda e: e.tensor_mul(bgs[:, :], bgs[:, :], gts[:, :]), inc=True)

        cvst_free = [None, None]
        qkev_free = [None, None]
        xfb_free = [None, None]
        sfb_free = [None, None]
        wk_free = [None] * 3

        s_wt = None
        for oc in range(24):
            BAR()
            par = oc % 2
            pl.waitp(QA, cvst_free[par])
            a_w = pl.dma(QA, cvst[par][:, 0:1024], Ww[oc * 128:(oc + 1) * 128, :])
            for dc in range(8):
                slot = dc % 2
                psl = P0[:, slot * 512:slot * 512 + 128]
                pl.wait("T", QA, a_w)
                if dc >= 2 or oc > 0:
                    pl.wait("T", "S", (s_wt or 0) - 1 if dc >= 2 else s_wt)
                t_wt = pl.op("T", lambda e, p=psl, src=cvst[par][:, dc * 128:(dc + 1) * 128]:
                             e.transpose(p, src, idf[:, :]), inc=True)
                dst = WT[:, dc * 3072 + oc * 128: dc * 3072 + (oc + 1) * 128]
                pl.wait("S", "T", t_wt)
                s_wt = pl.op("S", lambda e, d=dst, p=psl: e.copy(d, p), inc=True)
            cvst_free[par] = ("T", pl.ctr["T"])

        s_xnT_last = {}
        last_t_tr = None
        last_v_xn = None
        xnt_store_cnt = {}
        qkv_store_cnt = {}
        for b in range(BS):
            for tile in range(NT):
                BAR()
                colix = b * 16 + tile
                par = tile % 2
                pl.waitp(QA, xfb_free[par])
                a_x = pl.dma(QA, xfb[par][:, :], x_in[b, tile * 128:(tile + 1) * 128, :])
                pl.wait("V", QA, a_x)
                pl.wait("V", QA, a_eps)
                pl.vred(lambda e, xp=xfb[par]: e.reduce_sum(col[:, 0:1], xp[:, :], axis=AX.X))
                v_nm = pl.op("V", lambda e, c=colix: e.tensor_scalar_mul(
                    nm_st[:, c:c + 1], col[:, 0:1], -1.0 / D), inc=True)
                pl.wait("S", "V", v_nm)
                s_xm = pl.op("S", lambda e, c=colix, xp=xfb[par]: e.add(
                    wk[0][:, :], xp[:, :], nm_st[:, c:c + 1]), inc=True)
                xfb_free[par] = ("S", s_xm)
                pl.wait("V", "S", s_xm)
                pl.op("V", lambda e: e.tensor_mul(wk[1][:, :], wk[0][:, :], wk[0][:, :]))
                v_sq = pl.op("V", lambda e: e.reduce_sum(col[:, 1:2], wk[1][:, :], axis=AX.X), inc=True)
                pl.wait("S", "V", v_sq)
                s_si = pl.op("S", lambda e: e.activation(
                    col[:, 2:3], col[:, 1:2], AF.Sqrt, bias=epss[:, 0:1], scale=1.0 / D), inc=True)
                pl.wait("V", "S", s_si)
                pl.op("V", lambda e, c=colix: e.reciprocal(rs_st[:, c:c + 1], col[:, 2:3]))
                pl.wait("V", "T", last_t_tr)
                last_v_xn = pl.op("V", lambda e, c=colix: e.tensor_scalar_mul(
                    wk[2][:, :], wk[0][:, :], rs_st[:, c:c + 1]), inc=True)
                for dc in range(8):
                    slot = dc % 2
                    psl = P0[:, slot * 512:slot * 512 + 128]
                    pl.wait("T", "V", last_v_xn)
                    pl.wait("T", "S", s_xnT_last.get((slot,)))
                    last_t_tr = pl.op("T", lambda e, p=psl, src=wk[2][:, dc * 128:(dc + 1) * 128]:
                                      e.transpose(p, src, idf[:, :]), inc=True)
                    dst = xnT_sb[:, dc * 2048 + tile * 128: dc * 2048 + (tile + 1) * 128]
                    pl.wait("S", "T", last_t_tr)
                    s_xnT_last[(slot,)] = pl.op("S", lambda e, d=dst, p=psl: e.copy(d, p), inc=True)
            s_xnT_all = pl.ctr["S"]
            pl.wait(QA, "S", s_xnT_all)
            for dc in range(8):
                xnt_store_cnt[b] = pl.dma(QA, xnT_sc[b, dc], xnT_sb[:, dc * 2048:(dc + 1) * 2048])

            ps_free = {0: None, 1: None}
            pl.wait("T", "S", s_xnT_all)
            for oc in range(16):
                BAR()
                ps = P0 if oc % 2 == 0 else P1
                pl.waitp("T", ps_free[oc % 2])
                t_mm = None
                for dc in range(8):
                    for tc in range(4):
                        lhs = WT[:, dc * 3072 + oc * 128: dc * 3072 + (oc + 1) * 128]
                        rhs = xnT_sb[:, dc * 2048 + tc * 512: dc * 2048 + tc * 512 + 512]
                        t_mm = pl.op("T", lambda e, o=ps[:, tc * 512:tc * 512 + 512], l=lhs, r=rhs,
                                     st=(dc == 0), sp=(dc == 7): e.matmul(o, l, r, start=st, stop=sp),
                                     inc=(dc == 7 and tc == 3))
                dst_sc = qT_sc if oc < 8 else kT_sc
                occ = oc % 8
                sev = None
                for hh in range(2):
                    pl.wait("S", "T", t_mm)
                    pl.waitp("S", qkev_free[hh])
                    sev = pl.op("S", lambda e, d=qkev[hh][:, :],
                                p=ps[:, hh * 1024:hh * 1024 + 1024],
                                bcol=qkb[:, oc:oc + 1]:
                                e.add(d, p, bcol), inc=True)
                    pl.wait(QA, "S", sev)
                    adma = pl.dma(QA, dst_sc[b, occ, :, hh * 1024:hh * 1024 + 1024], qkev[hh][:, :])
                    qkev_free[hh] = (QA, adma)
                ps_free[oc % 2] = ("S", sev)
            for tile in range(NT):
                BAR()
                ps = P0 if tile % 2 == 0 else P1
                pl.waitp("T", ps_free[tile % 2])
                for dc in range(8):
                    for oc2 in range(2):
                        lhs = xnT_sb[:, dc * 2048 + tile * 128: dc * 2048 + (tile + 1) * 128]
                        rhs = WT[:, dc * 3072 + 2048 + oc2 * 512: dc * 3072 + 2048 + oc2 * 512 + 512]
                        t_mm = pl.op("T", lambda e, o=ps[:, oc2 * 512:oc2 * 512 + 512], l=lhs, r=rhs,
                                     st=(dc == 0), sp=(dc == 7): e.matmul(o, l, r, start=st, stop=sp),
                                     inc=(dc == 7 and oc2 == 1))
                hh = tile % 2
                pl.wait("S", "T", t_mm)
                pl.waitp("S", qkev_free[hh])
                sev = pl.op("S", lambda e, d=qkev[hh][:, :], p=ps[:, 0:1024]: e.copy(d, p), inc=True)
                pl.wait(QA, "S", sev)
                adma = pl.dma(QA, v_sc[b, tile], qkev[hh][:, :])
                qkev_free[hh] = (QA, adma)
                ps_free[tile % 2] = ("S", sev)
            qkv_store_cnt[b] = pl.ctr[QA]

        conv_jobs = []
        for e_ in range(E):
            for dc in range(8):
                for q4 in range(4):
                    off = e_ * D * F + dc * 128 * F + q4 * 1024
                    conv_jobs.append((RawAP(w1.tensor, off, [[F, 128], [1, 1024]]),
                                      RawAP(w1b.tensor, off, [[F, 128], [1, 1024]])))
        for e_ in range(E):
            for fc in range(32):
                off = e_ * F * D + fc * 128 * D
                conv_jobs.append((RawAP(w2.tensor, off, [[D, 128], [1, 1024]]),
                                  RawAP(w2b.tensor, off, [[D, 128], [1, 1024]])))
        for dc in range(8):
            for hh in range(2):
                off = dc * 128 * F2 + hh * 1024
                conv_jobs.append((RawAP(sw1.tensor, off, [[F2, 128], [1, 1024]]),
                                  RawAP(sw1b.tensor, off, [[F2, 128], [1, 1024]])))
        for fc in range(16):
            off = fc * 128 * D
            conv_jobs.append((RawAP(sw2.tensor, off, [[D, 128], [1, 1024]]),
                              RawAP(sw2b.tensor, off, [[D, 128], [1, 1024]])))
        conv = {"i": 0, "load": [None, None], "cast": [None, None]}

        def emit_conv(njobs):
            for _ in range(njobs):
                i = conv["i"]
                if i >= len(conv_jobs):
                    return
                conv["i"] = i + 1
                par = i % 2
                eng = "S" if par == 0 else "V"
                src, dst = conv_jobs[i]
                pl.waitp(QA, cvst_free[par])
                a_l = pl.dma(QA, cvst[par][:, 0:1024], src)
                conv["load"][par] = a_l
                pl.wait(eng, QA, a_l)
                pl.waitp(eng, qkev_free[par])
                if eng == "S":
                    cinc = pl.op("S", lambda e, d=qkev[par][:, :], s_=cvst[par][:, 0:1024]:
                                 e.copy(d, s_), inc=True)
                else:
                    cinc = pl.op("V", lambda e, d=qkev[par][:, :], s_=cvst[par][:, 0:1024]:
                                 e.tensor_copy(d, s_), inc=True)
                cvst_free[par] = (eng, cinc)
                conv["cast"][par] = (eng, cinc)
                pl.wait(QA, eng, cinc)
                a_st = pl.dma(QA, dst, qkev[par][:, :])
                qkev_free[par] = (QA, a_st)

        qk_free = [None, None]
        v_free = [None, None]
        bias_free = None
        stt_last = [None] * 4
        tr_free = [None, None]
        omm_free = None
        v_sdone_hist = []
        s_ss_cnt = {}
        t_pm_last = None
        for b in range(BS):
            s_oev = {}
            a_oTh = {}
            ost_free = None
            pl.wait(QA, QA, qkv_store_cnt[b])
            for h in range(H):
                BAR()
                par = h % 2
                pl.waitp(QA, qk_free[par])
                a_q = pl.dma(QA, qtb[par][:, :], qT_sc[b, h // 2, (h % 2) * 64:(h % 2) * 64 + 64, :])
                a_k = pl.dma(QA, ktb[par][:, :], kT_sc[b, h // 2, (h % 2) * 64:(h % 2) * 64 + 64, :])
                pl.waitp(QA, v_free[par])
                a_v = None
                for vq in range(4):
                    srcv = RawAP(v_sc.tensor, (b * NT + vq * 4) * 128 * D + h * 64,
                                 [[D, 128], [128 * D, 4], [1, 64]])
                    a_v = pl.dma(QA, vsb[par][:, vq * 256:(vq + 1) * 256], srcv)
                pl.waitp(QA, bias_free)
                a_bi = pl.dma(QA, bias_h[:, :], biasz[h, :, :])
                emit_conv(9)
                s_pn = v_pn = None
                for ic in range(4):
                    BAR()
                    w = (ic + 1) * 512
                    for qt in range(4):
                        BAR()
                        i0 = ic * 512 + qt * 128
                        boff = 1920 - i0
                        pl.wait("T", QA, a_k)
                        for jb in range(ic + 1):
                            pl.waitp("T", ("V", stt_last[jb]) if stt_last[jb] else None)
                            t_mm = pl.op("T", lambda e, o=P0[:, jb * 512:jb * 512 + 512],
                                         l=qtb[par][:, i0:i0 + 128],
                                         r=ktb[par][:, jb * 512:jb * 512 + 512]:
                                         e.matmul(o, l, r, start=True, stop=True), inc=True)
                            pl.wait("V", "T", t_mm)
                            pl.wait("V", QA, a_bi)
                            stt_last[jb] = pl.op(
                                "V", lambda e, d=sc_sb[:, jb * 512:jb * 512 + 512],
                                p=P0[:, jb * 512:jb * 512 + 512],
                                bsl=bias_h[:, boff + jb * 512: boff + jb * 512 + 512]:
                                e.scalar_tensor_tensor(d, p, 1.0 / 32.0, bsl,
                                                       op0=ALU.mult, op1=ALU.add), inc=True)
                        v_mx = pl.op("V", lambda e, w_=w: e.reduce_max(
                            col[:, 4:5], sc_sb[:, 0:w_], axis=AX.X, negate=True), inc=True)
                        pl.wait("S", "V", v_mx)
                        pl.wait("S", "V", v_pn)
                        s_ex = pl.op("S", lambda e, w_=w: e.activation(
                            pe_sb[:, 0:w_], sc_sb[:, 0:w_], AF.Exp, bias=col[:, 4:5]), inc=True)
                        pl.wait("V", "S", s_ex)
                        pl.vred(lambda e, w_=w: e.reduce_sum(col[:, 5:6], pe_sb[:, 0:w_], axis=AX.X))
                        v_rc = pl.op("V", lambda e: e.reciprocal(col[:, 6:7], col[:, 5:6]), inc=True)
                        wh = w // 2
                        tr_done_t = pl.ctr["T"]
                        pl.wait("S", "V", v_rc)
                        pl.wait("S", "T", (s_pn is not None) and tr_done_t)
                        s_pn = pl.op("S", lambda e, w_=wh: e.mul(
                            pn_sb[:, 0:w_], pe_sb[:, 0:w_], col[:, 6:7]), inc=True)
                        pl.wait("V", "T", (v_pn is not None) and tr_done_t)
                        v_pn = pl.op("V", lambda e, w_=wh, wf=w: e.tensor_scalar_mul(
                            pn_sb[:, w_:wf], pe_sb[:, w_:wf], col[:, 6:7]), inc=True)
                        njt = 4 * (ic + 1)
                        for jt in range(njt):
                            slot = jt % 2
                            psl = P1[:, slot * 512:slot * 512 + 128]
                            pl.wait("T", "S", s_pn)
                            pl.wait("T", "V", v_pn)
                            pl.waitp("T", tr_free[slot])
                            t_tr = pl.op("T", lambda e, p=psl, src=pn_sb[:, jt * 128:(jt + 1) * 128]:
                                         e.transpose(p, src, idf[:, :]), inc=True)
                            ceng = "S" if jt % 2 == 0 else "V"
                            pl.wait(ceng, "T", t_tr)
                            dstp = pT[:, jt * 512 + qt * 128: jt * 512 + (qt + 1) * 128]
                            if ceng == "S":
                                cv = pl.op("S", lambda e, d=dstp, p=psl: e.copy(d, p), inc=True)
                            else:
                                cv = pl.op("V", lambda e, d=dstp, p=psl: e.tensor_copy(d, p), inc=True)
                            tr_free[slot] = (ceng, cv)
                    pl.wait("T", "S", pl.lastw.get(("dummy",), 0) or None)
                    pl.wait("T", "S", tr_free[0][1] if tr_free[0] and tr_free[0][0] == "S" else None)
                    pl.wait("T", "S", pl.ctr["S"])
                    pl.wait("T", "V", pl.ctr["V"])
                    pl.waitp("T", omm_free)
                    pl.wait("T", QA, a_v)
                    njt = 4 * (ic + 1)
                    t_omm = None
                    for jt in range(njt):
                        t_omm = pl.op("T", lambda e, o=P1[0:64, 1024:1536],
                                      l=vsb[par][:, jt * 64:(jt + 1) * 64],
                                      r=pT[:, jt * 512:(jt + 1) * 512],
                                      st=(jt == 0), sp=(jt == njt - 1):
                                      e.matmul(o, l, r, start=st, stop=sp), inc=(jt == njt - 1))
                    qk_free[par] = ("T", pl.ctr["T"])
                    v_free[par] = ("T", pl.ctr["T"])
                    pl.wait("S", "T", t_omm)
                    pl.wait("S", QA, a_vb)
                    if h % 2 == 0:
                        dsto = oTpack[0:64, (h // 2) * 2048 + ic * 512:(h // 2) * 2048 + ic * 512 + 512]
                    else:
                        if ic == 0:
                            pl.waitp("S", ost_free)
                        dsto = oTstage[0:64, ic * 512: ic * 512 + 512]
                    s_oev[(h, ic)] = pl.op("S", lambda e, d=dsto, p=P1[0:64, 1024:1536],
                                           bcol=vb64[0:64, h:h + 1]:
                                           e.add(d, p, bcol), inc=True)
                    omm_free = ("S", s_oev[(h, ic)])
                bias_free = ("V", pl.ctr["V"])
                if h % 2 == 1:
                    pl.wait(QA, "S", s_oev[(h, 3)])
                    a_oTh[h] = pl.dma(QA, AR2[64:128, (h // 2) * 2048:(h // 2 + 1) * 2048],
                                      oTstage[0:64, :])
                    ost_free = (QA, a_oTh[h])
            s_pjc_last = [None, None]
            a_pj_last = None
            pThi3 = pT.rearrange("p (a c) -> p a c", c=1024)
            for ocp in range(8):
                BAR()
                par = ocp % 2
                pl.waitp(QA, cvst_free[par])
                a_pw = pl.dma(QA, cvst[par][:, 0:1024], pw[ocp * 128:(ocp + 1) * 128, :])
                if ocp == 0:
                    pl.wait("T", "V", pl.ctr["V"])
                for dhalf in range(16):
                    dc, hl = dhalf // 2, dhalf % 2
                    slot = dhalf % 2
                    psl = P0[0:64, slot * 512:slot * 512 + 128]
                    pl.wait("T", QA, a_pw)
                    if s_pjc_last[slot]:
                        pl.wait("T", "S", s_pjc_last[slot])
                    t_tr = pl.op("T", lambda e, p=psl,
                                 src=cvst[par][:, dc * 128 + hl * 64: dc * 128 + hl * 64 + 64]:
                                 e.transpose(p, src, idf[:, :]), inc=True)
                    pl.wait("S", "T", t_tr)
                    if hl == 0:
                        dstp = pT[0:64, dc * 1024 + ocp * 128: dc * 1024 + (ocp + 1) * 128]
                    else:
                        if a_pj_last:
                            pl.wait("S", QA, a_pj_last)
                        dstp = pTstage[0:64, dc * 128:(dc + 1) * 128]
                    s_pjc_last[slot] = pl.op("S", lambda e, d=dstp, p=psl: e.copy(d, p), inc=True)
                cvst_free[par] = ("T", pl.ctr["T"])
                pl.wait(QA, "S", s_pjc_last[1])
                dhi = pThi3[64:128, :, ocp * 128:(ocp + 1) * 128]
                shi = pTstage[0:64, :].rearrange("p (a c) -> p a c", c=128)
                a_pj_last = pl.dma(QA, dhi, shi)
            if v_sdone_hist:
                pl.wait(QA, "V", v_sdone_hist[-1])
            pl.wait(QA, "S", s_b2)
            a_lnag = pl.dma(QA, bcA[:, 0:1024], RawAP(lnAg.tensor, 0, [[0, 128], [1, 1024]]))
            a_lnab = pl.dma(QA, bcA[:, 1024:2048], RawAP(lnAb.tensor, 0, [[0, 128], [1, 1024]]))
            a_pjb = pl.dma(QA, bcA[:, 2048:3072], RawAP(pb.tensor, 0, [[0, 128], [1, 1024]]))
            ps_free4 = {0: None, 1: None}
            pl.waitp("T", tr_free[0])
            pl.waitp("T", tr_free[1])
            pl.waitp("T", omm_free)
            for tile in range(NT):
                BAR()
                par = tile % 2
                colix = b * 16 + tile
                pl.waitp(QA, xfb_free[par])
                a_x = pl.dma(QA, xfb[par][:, :], x_in[b, tile * 128:(tile + 1) * 128, :])
                ps = P0 if tile % 2 == 0 else P1
                pl.wait("T", QA, a_pj_last)
                pl.wait("T", QA, a_oTh.get(15))
                pl.wait("T", "S", s_pjc_last[1])
                pl.wait("T", "S", s_oev[(14, 3)])
                pl.waitp("T", ps_free4[tile % 2])
                t_pm = None
                for dc in range(8):
                    for ec in range(2):
                        lhs = oTpack[:, dc * 2048 + tile * 128: dc * 2048 + (tile + 1) * 128]
                        rhs = pT[:, dc * 1024 + ec * 512: dc * 1024 + ec * 512 + 512]
                        t_pm = pl.op("T", lambda e, o=ps[:, ec * 512:ec * 512 + 512], l=lhs, r=rhs,
                                     st=(dc == 0), sp=(dc == 7): e.matmul(o, l, r, start=st, stop=sp),
                                     inc=(dc == 7 and ec == 1))
                t_pm_last = t_pm
                pl.wait("V", "T", t_pm)
                pl.wait("V", QA, max(a_x, a_pjb))
                pl.waitp("V", wk_free[0])
                pl.op("V", lambda e, p=ps[:, 0:1024]: e.tensor_add(wk[0][:, :], p, bcA[:, 2048:3072]))
                pl.op("V", lambda e, c=colix, xp=xfb[par]: e.tensor_scalar(
                    wk[1][:, :], xp[:, :], nm_st[:, c:c + 1], rs_st[:, c:c + 1],
                    op0=ALU.add, op1=ALU.mult))
                v_y0 = pl.op("V", lambda e: e.tensor_add(wk[0][:, :], wk[0][:, :], wk[1][:, :]), inc=True)
                xfb_free[par] = ("V", v_y0)
                ps_free4[tile % 2] = ("V", v_y0)
                pl.vred(lambda e: e.reduce_sum(col[:, 7:8], wk[0][:, :], axis=AX.X))
                pl.op("V", lambda e: e.tensor_scalar_mul(col[:, 8:9], col[:, 7:8], -1.0 / D))
                pl.op("V", lambda e: e.tensor_scalar_add(wk[1][:, :], wk[0][:, :], col[:, 8:9]))
                pl.op("V", lambda e: e.tensor_mul(wk[2][:, :], wk[1][:, :], wk[1][:, :]))
                v_s2 = pl.op("V", lambda e: e.reduce_sum(col[:, 9:10], wk[2][:, :], axis=AX.X), inc=True)
                pl.wait("S", "V", v_s2)
                s_q2 = pl.op("S", lambda e: e.activation(
                    col[:, 10:11], col[:, 9:10], AF.Sqrt, bias=epss[:, 0:1], scale=1.0 / D), inc=True)
                pl.wait("V", "S", s_q2)
                pl.op("V", lambda e: e.reciprocal(col[:, 11:12], col[:, 10:11]))
                pl.op("V", lambda e: e.tensor_scalar_mul(wk[1][:, :], wk[1][:, :], col[:, 11:12]))
                pl.op("V", lambda e: e.tensor_mul(wk[1][:, :], wk[1][:, :], bcA[:, 0:1024]))
                pl.waitp("V", sfb_free[par])
                v_sd = pl.op("V", lambda e, d=sfb[par][:, :]: e.tensor_add(
                    d, wk[1][:, :], bcA[:, 1024:2048]), inc=True)
                wk_free[0] = ("V", v_sd)
                pl.wait(QA, "V", v_sd)
                a_ss = pl.dma(QA, s_sc[b, tile], sfb[par][:, :])
                s_ss_cnt[(b, tile)] = a_ss
                sfb_free[par] = (QA, a_ss)
                v_sdone_hist.append(v_sd)

        emit_conv(10 ** 9)
        BAR()
        conv_all_store = pl.ctr[QA]
        conv_tail_loads = max(x for x in conv["load"] if x) if any(conv["load"]) else 0
        conv_tail_casts = [conv["cast"][0], conv["cast"][1]]

        pl.wait(QA, "V", v_sdone_hist[-1])
        a_lnfg = pl.dma(QA, bcA[:, 0:1024], RawAP(lnFg.tensor, 0, [[0, 128], [1, 1024]]))
        a_lnfb = pl.dma(QA, bcA[:, 1024:2048], RawAP(lnFb.tensor, 0, [[0, 128], [1, 1024]]))
        a_sb2 = pl.dma(QA, bcA[:, 2048:3072], RawAP(sb2.tensor, 0, [[0, 128], [1, 1024]]))
        pl.wait("V", QA, max(a_sb2, a_sigc))
        v_sb2 = pl.op("V", lambda e: e.tensor_scalar_mul(
            bcA[:, 2048:3072], bcA[:, 2048:3072], sigc[:, 0:1]), inc=True)

        wcnt = {"n": 0}
        w1t_free = [None, None, None]
        psh_free = [None, None]
        psy_free = [None, None]
        w2_free = ("T", t_pm_last)
        xnt_free = ("T", t_pm_last)
        h2_free = ("T", pl.ctr["T"])
        yacc_freeV = pl.ctr["V"]
        yacc_freeS = pl.ctr["S"]
        a_out = None
        out_free = [None, None]
        v_out_hist = [None, None]
        for tb in range(NTB):
            BAR()
            b, half = tb // 2, tb % 2
            pl.waitp(QA, xnt_free)
            pl.wait(QA, QA, xnt_store_cnt[b])
            a_xt = None
            for dc in range(8):
                a_xt = pl.dma(QA, xnT_tb[:, dc * 1024:(dc + 1) * 1024],
                              xnT_sc[b, dc, :, half * 1024:half * 1024 + 1024])
            v_comb = {}
            s_ctc = {}
            for ts_ in range(TSUB):
                BAR()
                pl.wait("T", QA, a_xt)
                if tb == 0 and ts_ == 0:
                    pl.wait("T", "V", v_sdone_hist[-1])
                    pl.wait("T", "V", yacc_freeV)
                pl.waitp("T", v_comb.get(ts_ - 1) and ("V", v_comb[ts_ - 1]))
                t_rt = None
                for dc in range(8):
                    t_rt = pl.op("T", lambda e, o=P0[:, 0:8],
                                 l=xnT_tb[:, dc * 1024 + ts_ * 128: dc * 1024 + (ts_ + 1) * 128],
                                 r=rw_sb[:, dc * 8:(dc + 1) * 8],
                                 st=(dc == 0), sp=(dc == 7): e.matmul(o, l, r, start=st, stop=sp),
                                 inc=(dc == 7))
                pl.wait("V", "T", t_rt)
                pl.wait("V", QA, a_rb)
                pl.op("V", lambda e: e.tensor_add(pr0[:, :], P0[:, 0:8], rb_b[:, :]))
                v_sm = pl.op("V", lambda e: e.reduce_max(
                    col[:, 12:13], pr0[:, :], axis=AX.X, negate=True), inc=True)
                pl.wait("S", "V", v_sm)
                s_e8 = pl.op("S", lambda e: e.activation(
                    pre[:, :], pr0[:, :], AF.Exp, bias=col[:, 12:13]), inc=True)
                pl.wait("V", "S", s_e8)
                pl.vred(lambda e: e.reduce_sum(col[:, 13:14], pre[:, :], axis=AX.X))
                pl.op("V", lambda e: e.reciprocal(col[:, 13:14], col[:, 13:14]))
                pl.op("V", lambda e: e.tensor_scalar_mul(prb[:, :], pre[:, :], col[:, 13:14]))
                pl.vred(lambda e: e.reduce_max(col[:, 14:15], prb[:, :], axis=AX.X))
                pl.op("V", lambda e: e.tensor_scalar(msk8[:, :], prb[:, :], col[:, 14:15],
                                                     None, op0=ALU.is_lt))
                pl.op("V", lambda e: e.tensor_mul(msk8[:, :], msk8[:, :], prb[:, :]))
                pl.vred(lambda e: e.reduce_max(col[:, 15:16], msk8[:, :], axis=AX.X))
                pl.op("V", lambda e: e.tensor_scalar(msk8[:, :], prb[:, :], col[:, 15:16],
                                                     None, op0=ALU.is_ge))
                pl.op("V", lambda e: e.tensor_mul(msk8[:, :], msk8[:, :], prb[:, :]))
                pl.op("V", lambda e: e.tensor_tensor(col[:, 14:15], col[:, 14:15],
                                                     col[:, 15:16], op=ALU.add))
                pl.op("V", lambda e: e.reciprocal(col[:, 14:15], col[:, 14:15]))
                v_comb[ts_] = pl.op("V", lambda e, d=comb_sb[:, ts_ * 8:(ts_ + 1) * 8]:
                                    e.tensor_scalar_mul(d, msk8[:, :], col[:, 14:15]), inc=True)
                pl.wait("T", "V", v_comb[ts_])
                pl.waitp("T", s_ctc.get(ts_ - 1) and ("S", s_ctc[ts_ - 1]))
                t_ct = pl.op("T", lambda e, o=P1[0:8, 512:640],
                             src=comb_sb[:, ts_ * 8:(ts_ + 1) * 8]:
                             e.transpose(o, src, idf[:, :]), inc=True)
                pl.wait("S", "T", t_ct)
                s_ctc[ts_] = pl.op("S", lambda e, d=combT[0:8, ts_ * 128:(ts_ + 1) * 128],
                                   p=P1[0:8, 512:640]: e.copy(d, p), inc=True)
            v_yinit = None
            for ts_ in range(TSUB):
                BAR()
                pl.wait("T", "S", max(s_ctc[ts_], s_b2))
                if ts_ > 0:
                    pl.wait("T", "V", v_yinit)
                elif tb == 0:
                    pl.wait("T", "V", yacc_freeV)
                    pl.wait("T", "S", yacc_freeS)
                t_b2 = None
                for dh in range(2):
                    t_b2 = pl.op("T", lambda e, o=P1[:, dh * 512:dh * 512 + 512],
                                 l=combT[0:8, ts_ * 128:(ts_ + 1) * 128],
                                 r=b2_bf[0:8, dh * 512:dh * 512 + 512]:
                                 e.matmul(o, l, r, start=True, stop=True), inc=(dh == 1))
                pl.wait("V", "T", t_b2)
                pl.wait("V", "V", 0 if v_sb2 is None else v_sb2)
                if tb == 0 and ts_ >= 6:
                    pl.wait("V", QA, conv_tail_loads)
                    for cc in conv_tail_casts:
                        pl.waitp("V", cc)
                v_yinit = pl.op("V", lambda e, d=y_acc[:, ts_ * 1024:(ts_ + 1) * 1024]:
                                e.tensor_add(d, P1[:, 0:1024], bcA[:, 2048:3072]), inc=True)
            v_yinit_all = v_yinit
            s_h2 = {}
            v_yev_last = {}
            for e_ in range(9):
                Fh = 8 if e_ == 8 else 16
                shared = (e_ == 8)
                for fh in range(2):
                    a_w2h = None
                    w2q = 0
                    for fc in range(Fh):
                        BAR()
                        fcg = fh * Fh + fc
                        slot = wcnt["n"] % 3
                        pl.waitp(QW, w1t_free[slot])
                        if wcnt["n"] == 0:
                            pl.wait(QW, QA, conv_tail_loads)
                            for cc in conv_tail_casts:
                                pl.waitp(QW, cc)
                            pl.wait(QW, QA, pl.ctr[QA])
                        if shared:
                            srcw = RawAP(sw1b.tensor, fcg * 128, [[F2, 128], [128 * F2, 8], [1, 128]])
                        else:
                            srcw = RawAP(w1b.tensor, e_ * D * F + fcg * 128,
                                         [[F, 128], [128 * F, 8], [1, 128]])
                        a_w1 = pl.dma(QW, w1t[slot][:, :], srcw)
                        wcnt["n"] += 1
                        psh = P0[:, (fc % 2) * 1024:(fc % 2) * 1024 + 1024]
                        pl.wait("T", QW, a_w1)
                        if tb == 0 and e_ == 0 and fh == 0 and fc < 2:
                            pl.wait("T", "V", v_comb[TSUB - 1])
                        pl.waitp("T", psh_free[fc % 2])
                        if fc == 0 and fh == 0:
                            pl.waitp("T", w2_free)
                        t_h = None
                        for t5 in range(2):
                            for dc in range(8):
                                t_h = pl.op("T", lambda e, o=psh[:, t5 * 512:t5 * 512 + 512],
                                            l=w1t[slot][:, dc * 128:(dc + 1) * 128],
                                            r=xnT_tb[:, dc * 1024 + t5 * 512: dc * 1024 + t5 * 512 + 512],
                                            st=(dc == 0), sp=(dc == 7):
                                            e.matmul(o, l, r, start=st, stop=sp),
                                            inc=(dc == 7 and t5 == 1))
                        w1t_free[slot] = ("T", t_h)
                        pl.wait("S", "T", t_h)
                        pl.wait("S", "V", v_bgs if shared else v_bg)
                        if shared:
                            bcol, gcol = bgs[:, fcg:fcg + 1], gts[:, fcg:fcg + 1]
                        else:
                            bcol = bg_all[:, e_ * 32 + fcg:e_ * 32 + fcg + 1]
                            gcol = gt_all[:, e_ * 32 + fcg:e_ * 32 + fcg + 1]
                        s_h2[fc] = pl.op(
                            "S", lambda e, d=h2h[:, fc * 1024:(fc + 1) * 1024], p=psh,
                            bc=bcol, gc=gcol:
                            e.activation(d, p, AF.Gelu, bias=bc, scale=gc), inc=True)
                        psh_free[fc % 2] = ("S", s_h2[fc])
                        nw2 = Fh // 2
                        if fc < nw2:
                            for sub in range(2):
                                fcl = fc * 2 + sub
                                if tb == 0 and e_ == 0 and fh == 0:
                                    pl.wait(QA, QA, conv_all_store)
                                pl.waitp(QA, w2_free)
                                if shared:
                                    src2 = RawAP(sw2b.tensor, (fh * Fh + fcl) * 128 * D,
                                                 [[D, 128], [1, D]])
                                else:
                                    src2 = RawAP(w2b.tensor, e_ * F * D + (fh * Fh + fcl) * 128 * D,
                                                 [[D, 128], [1, D]])
                                a_w2h = pl.dma(QA, w2h[:, fcl * 1024:(fcl + 1) * 1024], src2)
                    t_y = None
                    for ts_ in range(TSUB):
                        BAR()
                        psy = P1[:, (ts_ % 2) * 1024:(ts_ % 2) * 1024 + 1024]
                        pl.wait("T", "S", s_h2[Fh - 1])
                        pl.wait("T", QA, a_w2h)
                        if tb == 0 and e_ == 0 and fh == 0 and ts_ < 2:
                            pl.wait("T", "V", v_yinit_all)
                        pl.waitp("T", psy_free[ts_ % 2])
                        for dh in range(2):
                            for fc in range(Fh):
                                t_y = pl.op("T", lambda e, o=psy[:, dh * 512:dh * 512 + 512],
                                            l=h2h[:, fc * 1024 + ts_ * 128: fc * 1024 + (ts_ + 1) * 128],
                                            r=w2h[:, fc * 1024 + dh * 512: fc * 1024 + dh * 512 + 512],
                                            st=(fc == 0), sp=(fc == Fh - 1):
                                            e.matmul(o, l, r, start=st, stop=sp),
                                            inc=(fc == Fh - 1 and dh == 1))
                        ccol = sigc[:, 0:1] if shared else comb_sb[:, ts_ * 8 + e_: ts_ * 8 + e_ + 1]
                        wkb = wk[ts_ % 2]
                        pl.wait("S", "T", t_y)
                        pl.waitp("S", wk_free[ts_ % 2])
                        s_yt = pl.op("S", lambda e, d=wkb[:, :], p=psy, c=ccol:
                                     e.mul(d, p, c), inc=True)
                        pl.wait("V", "S", s_yt)
                        v_ye = pl.op("V", lambda e, d=y_acc[:, ts_ * 1024:(ts_ + 1) * 1024],
                                     w_=wkb: e.tensor_add(d, d, w_[:, :]), inc=True)
                        wk_free[ts_ % 2] = ("V", v_ye)
                        psy_free[ts_ % 2] = ("V", v_ye)
                        v_yev_last[ts_] = v_ye
                    w2_free = ("T", t_y)
            for ts_ in range(TSUB):
                BAR()
                par = ts_ % 2
                tile = half * 8 + ts_
                pl.waitp(QA, xfb_free[par])
                a_x = pl.dma(QA, xfb[par][:, :], x_in[b, tile * 128:(tile + 1) * 128, :])
                pl.waitp(QA, sfb_free[par])
                pl.wait(QA, QA, s_ss_cnt[(b, tile)])
                a_s = pl.dma(QA, sfb[par][:, :], s_sc[b, tile])
                ysl = y_acc[:, ts_ * 1024:(ts_ + 1) * 1024]
                pl.wait("V", "V", v_yev_last[ts_])
                pl.waitp("V", wk_free[2])
                pl.waitp("V", wk_free[0])
                pl.vred(lambda e, y=ysl: e.reduce_sum(col[:, 7:8], y, axis=AX.X))
                pl.op("V", lambda e: e.tensor_scalar_mul(col[:, 8:9], col[:, 7:8], -1.0 / D))
                pl.op("V", lambda e, y=ysl: e.tensor_scalar_add(wk[2][:, :], y, col[:, 8:9]))
                pl.op("V", lambda e: e.tensor_mul(wk[0][:, :], wk[2][:, :], wk[2][:, :]))
                v_ss = pl.op("V", lambda e: e.reduce_sum(col[:, 9:10], wk[0][:, :], axis=AX.X), inc=True)
                pl.wait("S", "V", v_ss)
                s_sq = pl.op("S", lambda e: e.activation(
                    col[:, 10:11], col[:, 9:10], AF.Sqrt, bias=epss[:, 0:1], scale=1.0 / D), inc=True)
                pl.wait("V", "S", s_sq)
                pl.op("V", lambda e: e.reciprocal(col[:, 11:12], col[:, 10:11]))
                pl.op("V", lambda e: e.tensor_scalar_mul(wk[2][:, :], wk[2][:, :], col[:, 11:12]))
                pl.op("V", lambda e: e.tensor_mul(wk[2][:, :], wk[2][:, :], bcA[:, 0:1024]))
                v_t2 = pl.op("V", lambda e: e.tensor_add(wk[0][:, :], wk[2][:, :],
                                                         bcA[:, 1024:2048]), inc=True)
                pl.wait("S", "V", v_t2)
                s_g = pl.op("S", lambda e: e.activation(wk[2][:, :], wk[0][:, :], AF.Silu), inc=True)
                pl.wait("V", "S", s_g)
                pl.wait("V", QA, max(a_s, a_x))
                pl.op("V", lambda e, sp=sfb[par]: e.tensor_mul(wk[0][:, :], wk[2][:, :], sp[:, :]))
                v_out = pl.op("V", lambda e, xp=xfb[par]: e.tensor_add(
                    wk[0][:, :], wk[0][:, :], xp[:, :]), inc=True)
                xfb_free[par] = ("V", v_out)
                sfb_free[par] = ("V", v_out)
                wk_free[2] = ("V", v_out)
                pl.wait(QA, "V", v_out)
                a_out = pl.dma(QA, out_d[b, tile * 128:(tile + 1) * 128, :], wk[0][:, :])
                wk_free[0] = (QA, a_out)
        BAR()
        pl.wait(QA, QA, pl.ctr[QA])
        pl.wait(QW, QA, pl.ctr[QA])

        for oplist in pl.ops.values():
            for f in oplist:
                if hasattr(f, "semname"):
                    f.sem = sems[f.semname]
        ctx.enter_context(nc.allow_non_contiguous_dma(
            reason="small one-time column-layout constant loads"))

        @block.sync
        def _(e):
            for f in pl.ops["W"]:
                f(e)

        @block.gpsimd
        def _(e):
            for f in pl.ops["A"]:
                f(e)

        @block.tensor
        def _(e):
            for f in pl.ops["T"]:
                f(e)

        @block.scalar
        def _(e):
            for f in pl.ops["S"]:
                f(e)

        @block.vector
        def _(e):
            for f in pl.ops["V"]:
                f(e)

    return nc


_NC_CACHE = None


def kernel(**inputs):
    global _NC_CACHE
    x = np.ascontiguousarray(np.asarray(inputs["L0_attn_x"], dtype=np.float32))
    rpb = np.asarray(inputs["L0_attn_rpb"], dtype=np.float32)

    p = np.arange(128)[:, None]
    c = np.arange(2432)[None, :]
    idx = np.clip(c + 127 - p, 0, 4094)
    biasz = rpb[idx, :]
    biasz = np.transpose(biasz, (2, 0, 1)).copy()
    biasz[:, (c > p + 1920)] = -1e30
    biasz = np.ascontiguousarray(biasz.astype(np.float32))

    identf = np.eye(128, dtype=np.float32)
    epsc = np.array([1e-5, 0.0, 0.0, 0.0], dtype=np.float32)

    if _NC_CACHE is None:
        _NC_CACHE = build_program()
    nc = _NC_CACHE

    wnames = ["L0_attn_W_w", "L0_attn_W_b", "L0_attn_proj_w", "L0_attn_proj_b",
              "L0_attn_ln_g", "L0_attn_ln_b", "L0_ffn_router_w", "L0_ffn_router_b",
              "L0_ffn_w1", "L0_ffn_b1", "L0_ffn_gate", "L0_ffn_w2", "L0_ffn_b2",
              "L0_ffn_sw1", "L0_ffn_sb1", "L0_ffn_sgate", "L0_ffn_sw2", "L0_ffn_sb2",
              "L0_ffn_ln_g", "L0_ffn_ln_b"]
    base = {k: np.ascontiguousarray(np.asarray(inputs[k], dtype=np.float32)) for k in wnames}
    base["L0_ffn_shared_weight"] = np.asarray(inputs["L0_ffn_shared_weight"],
                                              dtype=np.float32).reshape(1)
    base["biasz"] = biasz
    base["identf"] = identf
    base["epsc"] = epsc

    in_maps = []
    for ci in range(8):
        m = dict(base)
        m["x"] = np.ascontiguousarray(x[ci * 2:(ci + 1) * 2])
        in_maps.append(m)

    res = run_bass_kernel_spmd(nc, in_maps, core_ids=list(range(8)))
    return np.concatenate([r["out"] for r in res.results], axis=0)



# version 03
# speedup vs baseline: 1.0103x; 1.0103x over previous
import numpy as np
from contextlib import ExitStack

import concourse.bass as bass
import concourse.mybir as mybir
from concourse.bass_utils import run_bass_kernel_spmd
from concourse.ap import AP as RawAP

F32 = mybir.dt.float32
BF16 = mybir.dt.bfloat16
AF = mybir.ActivationFunctionType
ALU = mybir.AluOpType
AX = mybir.AxisListType

BS = 2
S = 2048
D = 1024
H = 16
E = 8
F = 4096
F2 = 2048
NT = 16
NTB = 4
TSUB = 8


class Planner:
    def __init__(self):
        self.ops = {k: [] for k in ("W", "A", "T", "S", "V")}
        self.ctr = {k: 0 for k in ("W", "A", "T", "S", "V", "F")}
        self.lastw = {}
        self.vspacer = None

    def dfence(self):
        self.ctr["F"] += 1

        def f(e):
            e.drain().then_inc(f.sem, 1)
        f.semname = "F"
        self.ops["A"].append(f)
        return self.ctr["F"]

    def dma(self, q, out, in_):
        self.ctr[q] += 16

        def f(e, out=out, in_=in_):
            e.dma_start(out=out, in_=in_).then_inc(f.sem, 16)
        f.semname = q
        self.ops[q].append(f)
        return self.ctr[q]

    def op(self, eng, fn, inc=False):
        if eng == "V":
            inc = True
        ret = None
        if inc:
            self.ctr[eng] += 1
            ret = self.ctr[eng]

            def g(e, fn=fn):
                fn(e).then_inc(g.sem, 1)
            g.semname = eng
            self.ops[eng].append(g)
        else:
            self.ops[eng].append(fn)
        if eng == "V":
            self.wait("V", "V", ret)
        return ret

    def vred(self, fn):
        v = self.op("V", fn, inc=True)
        self.wait("V", "V", v)
        return v

    def wait(self, eng, sem, val):
        if not val:
            return
        key = (eng, sem)
        if self.lastw.get(key, 0) >= val:
            return
        self.lastw[key] = val

        def w(e, val=val):
            e.wait_ge(w.sem, val)
        w.semname = sem
        self.ops[eng].append(w)

    def waitp(self, eng, pair):
        if pair:
            self.wait(eng, pair[0], pair[1])


def build_program():
    nc = bass.Bass("TRN2", target_bir_lowering=False)

    def din(name, shape, dt=F32):
        return nc.dram_tensor(name, shape, dt, kind="ExternalInput").ap()

    x_in = din("x", [BS, S, D])
    Ww = din("L0_attn_W_w", [3 * D, D])
    Wb = din("L0_attn_W_b", [3 * D])
    pw = din("L0_attn_proj_w", [D, D])
    pb = din("L0_attn_proj_b", [D])
    lnAg = din("L0_attn_ln_g", [D])
    lnAb = din("L0_attn_ln_b", [D])
    rtw = din("L0_ffn_router_w", [D, E])
    rtb = din("L0_ffn_router_b", [E])
    w1 = din("L0_ffn_w1", [E, D, F])
    b1 = din("L0_ffn_b1", [E, F])
    gt = din("L0_ffn_gate", [E, F])
    w2 = din("L0_ffn_w2", [E, F, D])
    b2 = din("L0_ffn_b2", [E, D])
    sw1 = din("L0_ffn_sw1", [D, F2])
    sb1 = din("L0_ffn_sb1", [F2])
    sgt = din("L0_ffn_sgate", [F2])
    sw2 = din("L0_ffn_sw2", [F2, D])
    sb2 = din("L0_ffn_sb2", [D])
    shw = din("L0_ffn_shared_weight", [1])
    lnFg = din("L0_ffn_ln_g", [D])
    lnFb = din("L0_ffn_ln_b", [D])
    biasz = din("biasz", [H, 128, 2432])
    identf = din("identf", [128, 128])
    epsc = din("epsc", [4])

    out_d = nc.dram_tensor("out", [BS, S, D], F32, kind="ExternalOutput").ap()

    def scr(name, shape, dt):
        return nc.dram_tensor(name, shape, dt, kind="Internal").ap()
    xnT_sc = scr("xnT_sc", [BS, 8, 128, S], BF16)
    qT_sc = scr("qT_sc", [BS, 8, 128, S], BF16)
    kT_sc = scr("kT_sc", [BS, 8, 128, S], BF16)
    v_sc = scr("v_sc", [BS, NT, 128, D], BF16)
    s_sc = scr("s_sc", [BS, NT, 128, D], F32)
    w1b = scr("w1b", [E, D, F], BF16)
    w2b = scr("w2b", [E, F, D], BF16)
    sw1b = scr("sw1b", [D, F2], BF16)
    sw2b = scr("sw2b", [F2, D], BF16)
    sig_rt = scr("sig_rt", [1], F32)

    pl = Planner()
    QW, QA = "W", "A"

    with ExitStack() as ctx:
        sb = lambda n, s, d: ctx.enter_context(nc.sbuf_tensor(n, s, d))
        idf = sb("idf", [128, 128], F32)
        qkb = sb("qkb", [128, 16], F32)
        vb64 = sb("vb64", [64, 16], F32)
        bcA = sb("bcA", [128, 3072], F32)
        rb_b = sb("rb_b", [128, 8], F32)
        rw_f = sb("rw_f", [128, 64], F32)
        rw_sb = sb("rw_sb", [128, 64], BF16)
        b2_bf = sb("b2_bf", [8, 1024], BF16)
        bg_all = sb("bg_all", [128, 256], F32)
        gt_all = sb("gt_all", [128, 256], F32)
        bgs = sb("bgs", [128, 16], F32)
        gts = sb("gts", [128, 16], F32)
        nm_st = sb("nm_st", [128, 32], F32)
        rs_st = sb("rs_st", [128, 32], F32)
        epss = sb("epss", [128, 4], F32)
        sig1 = sb("sig1", [1, 1], F32)
        sigc = sb("sigc", [128, 1], F32)
        AR1 = sb("AR1", [128, 16384], BF16)
        AR2 = sb("AR2", [128, 24576], BF16)
        AR3 = sb("AR3", [128, 8192], F32)
        qtb = [sb("qtb0", [64, 2048], BF16), sb("qtb1", [64, 2048], BF16)]
        ktb = [sb("ktb0", [64, 2048], BF16), sb("ktb1", [64, 2048], BF16)]
        vsb = [sb("vsb0", [128, 1024], BF16), sb("vsb1", [128, 1024], BF16)]
        bias_h = sb("bias_h", [128, 2432], F32)
        oTstage = sb("oTstage", [64, 2048], BF16)
        pTstage = sb("pTstage", [64, 1024], BF16)
        w1t = [sb("w1t0", [128, 1024], BF16), sb("w1t1", [128, 1024], BF16),
               sb("w1t2", [128, 1024], BF16)]
        qkev = [sb("qkev0", [128, 1024], BF16), sb("qkev1", [128, 1024], BF16)]
        xfb = [sb("xfb0", [128, 1024], F32), sb("xfb1", [128, 1024], F32)]
        sfb = [sb("sfb0", [128, 1024], F32), sb("sfb1", [128, 1024], F32)]
        wk = [sb("wk0", [128, 1024], F32), sb("wk1", [128, 1024], F32),
              sb("wk2", [128, 1024], F32)]
        col = sb("colb", [128, 16], F32)
        dum = sb("dum", [1, 1], F32)
        dum2 = sb("dum2", [1, 1], F32)
        pr0 = sb("pr0", [128, 8], F32)
        pre = sb("pre", [128, 8], F32)
        prb = sb("prb", [128, 8], F32)
        msk8 = sb("msk8", [128, 8], F32)
        comb_sb = sb("comb_sb", [128, 64], F32)
        combT = pTstage[0:8, :]
        b2f = bcA[0:8, 0:1024]

        P0 = ctx.enter_context(nc.psum_tensor("P0", [128, 2048], F32))
        P1 = ctx.enter_context(nc.psum_tensor("P1", [128, 2048], F32))

        sems = {k: ctx.enter_context(nc.semaphore("sem" + k))
                for k in ("W", "A", "T", "S", "V", "F")}
        block = ctx.enter_context(nc.Block())

        def BAR():
            pl.op("S", lambda e: e.copy(dum[:, :], dum[:, :]), inc=True)
            pl.op("V", lambda e: e.tensor_copy(dum[:, :], dum[:, :]), inc=True)
            vals = dict(pl.ctr)
            for eng in ("T", "S", "V", "W", "A"):
                for sem_, v_ in vals.items():
                    if sem_ != eng and v_ > 0:
                        pl.wait(eng, sem_, v_)

        xnT_sb = AR1
        h2h = AR1
        WT = AR2
        oTpack = AR2[:, 0:16384]
        pT = AR2[:, 16384:24576]
        w2h = AR2[:, 0:16384]
        xnT_tb = AR2[:, 16384:24576]
        sc_sb = AR3[:, 0:2048]
        pe_sb = AR3[:, 2048:4096]
        pn_sb = AR3[:, 4096:6144]
        cvst = [AR3[:, 6144:7168], AR3[:, 7168:8192]]
        y_acc = AR3

        pl.dma(QA, idf[:, :], identf[:, :])
        a_qkb = pl.dma(QA, qkb[:, :], RawAP(Wb.tensor, 0, [[1, 128], [128, 16]]))
        a_vb = pl.dma(QA, vb64[:, :], RawAP(Wb.tensor, 2048, [[1, 64], [64, 16]]))
        a_rb = pl.dma(QA, rb_b[:, :], RawAP(rtb.tensor, 0, [[0, 128], [1, 8]]))
        a_rw = pl.dma(QA, rw_f[:, :], RawAP(rtw.tensor, 0, [[8, 128], [1024, 8], [1, 8]]))
        a_b2 = pl.dma(QA, b2f[:, :], b2[:, :])
        for e_ in range(E):
            a_bg = pl.dma(QA, bg_all[:, e_ * 32:(e_ + 1) * 32],
                          RawAP(b1.tensor, e_ * 4096, [[1, 128], [128, 32]]))
            a_gt = pl.dma(QA, gt_all[:, e_ * 32:(e_ + 1) * 32],
                          RawAP(gt.tensor, e_ * 4096, [[1, 128], [128, 32]]))
        a_bgs = pl.dma(QA, bgs[:, :], RawAP(sb1.tensor, 0, [[1, 128], [128, 16]]))
        a_gts = pl.dma(QA, gts[:, :], RawAP(sgt.tensor, 0, [[1, 128], [128, 16]]))
        a_eps = pl.dma(QA, epss[:, :], RawAP(epsc.tensor, 0, [[0, 128], [1, 4]]))
        a_sh = pl.dma(QA, sig1[:, :], shw[None, :])
        F1 = pl.dfence()
        pl.wait("S", "F", F1)
        pl.wait("V", "F", F1)
        pl.wait("S", QA, a_rw)
        pl.op("S", lambda e: e.copy(rw_sb[:, :], rw_f[:, :]))
        pl.wait("S", QA, a_b2)
        s_b2 = pl.op("S", lambda e: e.copy(b2_bf[:, :], b2f[:, :]), inc=True)
        pl.wait("S", QA, a_sh)
        s_sig = pl.op("S", lambda e: e.activation(sig1[:, :], sig1[:, :], AF.Sigmoid), inc=True)
        pl.wait(QA, "S", s_sig)
        a_sr = pl.dma(QA, sig_rt[None, :], sig1[:, :])
        pl.wait(QA, QA, a_sr)
        pl.dfence()
        a_sigc = pl.dma(QA, sigc[:, :], RawAP(sig_rt.tensor, 0, [[0, 128], [1, 1]]))
        pl.dfence()
        pl.wait("V", QA, max(a_bg, a_gt))
        v_bg = pl.op("V", lambda e: e.tensor_mul(bg_all[:, :], bg_all[:, :], gt_all[:, :]), inc=True)
        pl.wait("V", QA, max(a_bgs, a_gts))
        v_bgs = pl.op("V", lambda e: e.tensor_mul(bgs[:, :], bgs[:, :], gts[:, :]), inc=True)

        cvst_free = [None, None]
        qkev_free = [None, None]
        xfb_free = [None, None]
        sfb_free = [None, None]
        wk_free = [None] * 3

        s_wt = None
        for oc in range(24):
            BAR()
            par = oc % 2
            pl.waitp(QA, cvst_free[par])
            a_w = pl.dma(QA, cvst[par][:, 0:1024], Ww[oc * 128:(oc + 1) * 128, :])
            for dc in range(8):
                slot = dc % 2
                psl = P0[:, slot * 512:slot * 512 + 128]
                pl.wait("T", QA, a_w)
                if dc >= 2 or oc > 0:
                    pl.wait("T", "S", (s_wt or 0) - 1 if dc >= 2 else s_wt)
                t_wt = pl.op("T", lambda e, p=psl, src=cvst[par][:, dc * 128:(dc + 1) * 128]:
                             e.transpose(p, src, idf[:, :]), inc=True)
                dst = WT[:, dc * 3072 + oc * 128: dc * 3072 + (oc + 1) * 128]
                pl.wait("S", "T", t_wt)
                s_wt = pl.op("S", lambda e, d=dst, p=psl: e.copy(d, p), inc=True)
            cvst_free[par] = ("T", pl.ctr["T"])

        s_xnT_last = {}
        last_t_tr = None
        last_v_xn = None
        xnt_store_cnt = {}
        qkv_store_cnt = {}
        for b in range(BS):
            for tile in range(NT):
                BAR()
                colix = b * 16 + tile
                par = tile % 2
                pl.waitp(QA, xfb_free[par])
                a_x = pl.dma(QA, xfb[par][:, :], x_in[b, tile * 128:(tile + 1) * 128, :])
                pl.wait("V", QA, a_x)
                pl.wait("V", QA, a_eps)
                pl.vred(lambda e, xp=xfb[par]: e.reduce_sum(col[:, 0:1], xp[:, :], axis=AX.X))
                v_nm = pl.op("V", lambda e, c=colix: e.tensor_scalar_mul(
                    nm_st[:, c:c + 1], col[:, 0:1], -1.0 / D), inc=True)
                pl.wait("S", "V", v_nm)
                s_xm = pl.op("S", lambda e, c=colix, xp=xfb[par]: e.add(
                    wk[0][:, :], xp[:, :], nm_st[:, c:c + 1]), inc=True)
                xfb_free[par] = ("S", s_xm)
                pl.wait("V", "S", s_xm)
                pl.op("V", lambda e: e.tensor_mul(wk[1][:, :], wk[0][:, :], wk[0][:, :]))
                v_sq = pl.op("V", lambda e: e.reduce_sum(col[:, 1:2], wk[1][:, :], axis=AX.X), inc=True)
                pl.wait("S", "V", v_sq)
                s_si = pl.op("S", lambda e: e.activation(
                    col[:, 2:3], col[:, 1:2], AF.Sqrt, bias=epss[:, 0:1], scale=1.0 / D), inc=True)
                pl.wait("V", "S", s_si)
                pl.op("V", lambda e, c=colix: e.reciprocal(rs_st[:, c:c + 1], col[:, 2:3]))
                pl.wait("V", "T", last_t_tr)
                last_v_xn = pl.op("V", lambda e, c=colix: e.tensor_scalar_mul(
                    wk[2][:, :], wk[0][:, :], rs_st[:, c:c + 1]), inc=True)
                for dc in range(8):
                    slot = dc % 2
                    psl = P0[:, slot * 512:slot * 512 + 128]
                    pl.wait("T", "V", last_v_xn)
                    pl.wait("T", "S", s_xnT_last.get((slot,)))
                    last_t_tr = pl.op("T", lambda e, p=psl, src=wk[2][:, dc * 128:(dc + 1) * 128]:
                                      e.transpose(p, src, idf[:, :]), inc=True)
                    dst = xnT_sb[:, dc * 2048 + tile * 128: dc * 2048 + (tile + 1) * 128]
                    pl.wait("S", "T", last_t_tr)
                    s_xnT_last[(slot,)] = pl.op("S", lambda e, d=dst, p=psl: e.copy(d, p), inc=True)
            s_xnT_all = pl.ctr["S"]
            pl.wait(QA, "S", s_xnT_all)
            for dc in range(8):
                xnt_store_cnt[b] = pl.dma(QA, xnT_sc[b, dc], xnT_sb[:, dc * 2048:(dc + 1) * 2048])

            ps_free = {0: None, 1: None}
            pl.wait("T", "S", s_xnT_all)
            for oc in range(16):
                BAR()
                ps = P0 if oc % 2 == 0 else P1
                pl.waitp("T", ps_free[oc % 2])
                t_mm = None
                for dc in range(8):
                    for tc in range(4):
                        lhs = WT[:, dc * 3072 + oc * 128: dc * 3072 + (oc + 1) * 128]
                        rhs = xnT_sb[:, dc * 2048 + tc * 512: dc * 2048 + tc * 512 + 512]
                        t_mm = pl.op("T", lambda e, o=ps[:, tc * 512:tc * 512 + 512], l=lhs, r=rhs,
                                     st=(dc == 0), sp=(dc == 7): e.matmul(o, l, r, start=st, stop=sp),
                                     inc=(dc == 7 and tc == 3))
                dst_sc = qT_sc if oc < 8 else kT_sc
                occ = oc % 8
                sev = None
                for hh in range(2):
                    pl.wait("S", "T", t_mm)
                    pl.waitp("S", qkev_free[hh])
                    sev = pl.op("S", lambda e, d=qkev[hh][:, :],
                                p=ps[:, hh * 1024:hh * 1024 + 1024],
                                bcol=qkb[:, oc:oc + 1]:
                                e.add(d, p, bcol), inc=True)
                    pl.wait(QA, "S", sev)
                    adma = pl.dma(QA, dst_sc[b, occ, :, hh * 1024:hh * 1024 + 1024], qkev[hh][:, :])
                    qkev_free[hh] = (QA, adma)
                ps_free[oc % 2] = ("S", sev)
            for tile in range(NT):
                BAR()
                ps = P0 if tile % 2 == 0 else P1
                pl.waitp("T", ps_free[tile % 2])
                for dc in range(8):
                    for oc2 in range(2):
                        lhs = xnT_sb[:, dc * 2048 + tile * 128: dc * 2048 + (tile + 1) * 128]
                        rhs = WT[:, dc * 3072 + 2048 + oc2 * 512: dc * 3072 + 2048 + oc2 * 512 + 512]
                        t_mm = pl.op("T", lambda e, o=ps[:, oc2 * 512:oc2 * 512 + 512], l=lhs, r=rhs,
                                     st=(dc == 0), sp=(dc == 7): e.matmul(o, l, r, start=st, stop=sp),
                                     inc=(dc == 7 and oc2 == 1))
                hh = tile % 2
                pl.wait("S", "T", t_mm)
                pl.waitp("S", qkev_free[hh])
                sev = pl.op("S", lambda e, d=qkev[hh][:, :], p=ps[:, 0:1024]: e.copy(d, p), inc=True)
                pl.wait(QA, "S", sev)
                adma = pl.dma(QA, v_sc[b, tile], qkev[hh][:, :])
                qkev_free[hh] = (QA, adma)
                ps_free[tile % 2] = ("S", sev)
            qkv_store_cnt[b] = pl.ctr[QA]

        conv_jobs = []
        for e_ in range(E):
            for dc in range(8):
                for q4 in range(4):
                    off = e_ * D * F + dc * 128 * F + q4 * 1024
                    conv_jobs.append((RawAP(w1.tensor, off, [[F, 128], [1, 1024]]),
                                      RawAP(w1b.tensor, off, [[F, 128], [1, 1024]])))
        for e_ in range(E):
            for fc in range(32):
                off = e_ * F * D + fc * 128 * D
                conv_jobs.append((RawAP(w2.tensor, off, [[D, 128], [1, 1024]]),
                                  RawAP(w2b.tensor, off, [[D, 128], [1, 1024]])))
        for dc in range(8):
            for hh in range(2):
                off = dc * 128 * F2 + hh * 1024
                conv_jobs.append((RawAP(sw1.tensor, off, [[F2, 128], [1, 1024]]),
                                  RawAP(sw1b.tensor, off, [[F2, 128], [1, 1024]])))
        for fc in range(16):
            off = fc * 128 * D
            conv_jobs.append((RawAP(sw2.tensor, off, [[D, 128], [1, 1024]]),
                              RawAP(sw2b.tensor, off, [[D, 128], [1, 1024]])))
        conv = {"i": 0, "load": [None, None], "cast": [None, None]}

        def emit_conv(njobs):
            for _ in range(njobs):
                i = conv["i"]
                if i >= len(conv_jobs):
                    return
                conv["i"] = i + 1
                par = i % 2
                eng = "S" if par == 0 else "V"
                src, dst = conv_jobs[i]
                pl.waitp(QA, cvst_free[par])
                a_l = pl.dma(QA, cvst[par][:, 0:1024], src)
                conv["load"][par] = a_l
                pl.wait(eng, QA, a_l)
                pl.waitp(eng, qkev_free[par])
                if eng == "S":
                    cinc = pl.op("S", lambda e, d=qkev[par][:, :], s_=cvst[par][:, 0:1024]:
                                 e.copy(d, s_), inc=True)
                else:
                    cinc = pl.op("V", lambda e, d=qkev[par][:, :], s_=cvst[par][:, 0:1024]:
                                 e.tensor_copy(d, s_), inc=True)
                cvst_free[par] = (eng, cinc)
                conv["cast"][par] = (eng, cinc)
                pl.wait(QA, eng, cinc)
                a_st = pl.dma(QA, dst, qkev[par][:, :])
                qkev_free[par] = (QA, a_st)

        qk_free = [None, None]
        v_free = [None, None]
        bias_free = None
        stt_last = [None] * 4
        tr_free = [None, None]
        omm_free = None
        v_sdone_hist = []
        s_ss_cnt = {}
        t_pm_last = None
        for b in range(BS):
            s_oev = {}
            a_oTh = {}
            ost_free = None
            pl.wait(QA, QA, qkv_store_cnt[b])
            for h in range(H):
                BAR()
                par = h % 2
                pl.waitp(QA, qk_free[par])
                a_q = pl.dma(QA, qtb[par][:, :], qT_sc[b, h // 2, (h % 2) * 64:(h % 2) * 64 + 64, :])
                a_k = pl.dma(QA, ktb[par][:, :], kT_sc[b, h // 2, (h % 2) * 64:(h % 2) * 64 + 64, :])
                pl.waitp(QA, v_free[par])
                a_v = None
                for vq in range(4):
                    srcv = RawAP(v_sc.tensor, (b * NT + vq * 4) * 128 * D + h * 64,
                                 [[D, 128], [128 * D, 4], [1, 64]])
                    a_v = pl.dma(QA, vsb[par][:, vq * 256:(vq + 1) * 256], srcv)
                pl.waitp(QA, bias_free)
                a_bi = pl.dma(QA, bias_h[:, :], biasz[h, :, :])
                emit_conv(9)
                s_pn = v_pn = None
                for ic in range(4):
                    BAR()
                    w = (ic + 1) * 512
                    for qt in range(4):
                        BAR()
                        i0 = ic * 512 + qt * 128
                        boff = 1920 - i0
                        pl.wait("T", QA, a_k)
                        for jb in range(ic + 1):
                            pl.waitp("T", ("V", stt_last[jb]) if stt_last[jb] else None)
                            t_mm = pl.op("T", lambda e, o=P0[:, jb * 512:jb * 512 + 512],
                                         l=qtb[par][:, i0:i0 + 128],
                                         r=ktb[par][:, jb * 512:jb * 512 + 512]:
                                         e.matmul(o, l, r, start=True, stop=True), inc=True)
                            pl.wait("V", "T", t_mm)
                            pl.wait("V", QA, a_bi)
                            stt_last[jb] = pl.op(
                                "V", lambda e, d=sc_sb[:, jb * 512:jb * 512 + 512],
                                p=P0[:, jb * 512:jb * 512 + 512],
                                bsl=bias_h[:, boff + jb * 512: boff + jb * 512 + 512]:
                                e.scalar_tensor_tensor(d, p, 1.0 / 32.0, bsl,
                                                       op0=ALU.mult, op1=ALU.add), inc=True)
                        v_mx = pl.op("V", lambda e, w_=w: e.reduce_max(
                            col[:, 4:5], sc_sb[:, 0:w_], axis=AX.X, negate=True), inc=True)
                        pl.wait("S", "V", v_mx)
                        pl.wait("S", "V", v_pn)
                        s_ex = pl.op("S", lambda e, w_=w: e.activation(
                            pe_sb[:, 0:w_], sc_sb[:, 0:w_], AF.Exp, bias=col[:, 4:5]), inc=True)
                        pl.wait("V", "S", s_ex)
                        pl.vred(lambda e, w_=w: e.reduce_sum(col[:, 5:6], pe_sb[:, 0:w_], axis=AX.X))
                        v_rc = pl.op("V", lambda e: e.reciprocal(col[:, 6:7], col[:, 5:6]), inc=True)
                        wh = w // 2
                        tr_done_t = pl.ctr["T"]
                        pl.wait("S", "V", v_rc)
                        pl.wait("S", "T", (s_pn is not None) and tr_done_t)
                        s_pn = pl.op("S", lambda e, w_=wh: e.mul(
                            pn_sb[:, 0:w_], pe_sb[:, 0:w_], col[:, 6:7]), inc=True)
                        pl.wait("V", "T", (v_pn is not None) and tr_done_t)
                        v_pn = pl.op("V", lambda e, w_=wh, wf=w: e.tensor_scalar_mul(
                            pn_sb[:, w_:wf], pe_sb[:, w_:wf], col[:, 6:7]), inc=True)
                        njt = 4 * (ic + 1)
                        for jt in range(njt):
                            slot = jt % 2
                            psl = P1[:, slot * 512:slot * 512 + 128]
                            pl.wait("T", "S", s_pn)
                            pl.wait("T", "V", v_pn)
                            pl.waitp("T", tr_free[slot])
                            t_tr = pl.op("T", lambda e, p=psl, src=pn_sb[:, jt * 128:(jt + 1) * 128]:
                                         e.transpose(p, src, idf[:, :]), inc=True)
                            ceng = "S" if jt % 2 == 0 else "V"
                            pl.wait(ceng, "T", t_tr)
                            dstp = pT[:, jt * 512 + qt * 128: jt * 512 + (qt + 1) * 128]
                            if ceng == "S":
                                cv = pl.op("S", lambda e, d=dstp, p=psl: e.copy(d, p), inc=True)
                            else:
                                cv = pl.op("V", lambda e, d=dstp, p=psl: e.tensor_copy(d, p), inc=True)
                            tr_free[slot] = (ceng, cv)
                    pl.wait("T", "S", pl.lastw.get(("dummy",), 0) or None)
                    pl.wait("T", "S", tr_free[0][1] if tr_free[0] and tr_free[0][0] == "S" else None)
                    pl.wait("T", "S", pl.ctr["S"])
                    pl.wait("T", "V", pl.ctr["V"])
                    pl.waitp("T", omm_free)
                    pl.wait("T", QA, a_v)
                    njt = 4 * (ic + 1)
                    t_omm = None
                    for jt in range(njt):
                        t_omm = pl.op("T", lambda e, o=P1[0:64, 1024:1536],
                                      l=vsb[par][:, jt * 64:(jt + 1) * 64],
                                      r=pT[:, jt * 512:(jt + 1) * 512],
                                      st=(jt == 0), sp=(jt == njt - 1):
                                      e.matmul(o, l, r, start=st, stop=sp), inc=(jt == njt - 1))
                    qk_free[par] = ("T", pl.ctr["T"])
                    v_free[par] = ("T", pl.ctr["T"])
                    pl.wait("S", "T", t_omm)
                    pl.wait("S", QA, a_vb)
                    if h % 2 == 0:
                        dsto = oTpack[0:64, (h // 2) * 2048 + ic * 512:(h // 2) * 2048 + ic * 512 + 512]
                    else:
                        if ic == 0:
                            pl.waitp("S", ost_free)
                        dsto = oTstage[0:64, ic * 512: ic * 512 + 512]
                    s_oev[(h, ic)] = pl.op("S", lambda e, d=dsto, p=P1[0:64, 1024:1536],
                                           bcol=vb64[0:64, h:h + 1]:
                                           e.add(d, p, bcol), inc=True)
                    omm_free = ("S", s_oev[(h, ic)])
                bias_free = ("V", pl.ctr["V"])
                if h % 2 == 1:
                    pl.wait(QA, "S", s_oev[(h, 3)])
                    a_oTh[h] = pl.dma(QA, AR2[64:128, (h // 2) * 2048:(h // 2 + 1) * 2048],
                                      oTstage[0:64, :])
                    ost_free = (QA, a_oTh[h])
            s_pjc_last = [None, None]
            a_pj_last = None
            pThi3 = pT.rearrange("p (a c) -> p a c", c=1024)
            for ocp in range(8):
                BAR()
                par = ocp % 2
                pl.waitp(QA, cvst_free[par])
                a_pw = pl.dma(QA, cvst[par][:, 0:1024], pw[ocp * 128:(ocp + 1) * 128, :])
                if ocp == 0:
                    pl.wait("T", "V", pl.ctr["V"])
                for dhalf in range(16):
                    dc, hl = dhalf // 2, dhalf % 2
                    slot = dhalf % 2
                    psl = P0[0:64, slot * 512:slot * 512 + 128]
                    pl.wait("T", QA, a_pw)
                    if s_pjc_last[slot]:
                        pl.wait("T", "S", s_pjc_last[slot])
                    t_tr = pl.op("T", lambda e, p=psl,
                                 src=cvst[par][:, dc * 128 + hl * 64: dc * 128 + hl * 64 + 64]:
                                 e.transpose(p, src, idf[:, :]), inc=True)
                    pl.wait("S", "T", t_tr)
                    if hl == 0:
                        dstp = pT[0:64, dc * 1024 + ocp * 128: dc * 1024 + (ocp + 1) * 128]
                    else:
                        if a_pj_last:
                            pl.wait("S", QA, a_pj_last)
                        dstp = pTstage[0:64, dc * 128:(dc + 1) * 128]
                    s_pjc_last[slot] = pl.op("S", lambda e, d=dstp, p=psl: e.copy(d, p), inc=True)
                cvst_free[par] = ("T", pl.ctr["T"])
                pl.wait(QA, "S", s_pjc_last[1])
                dhi = pThi3[64:128, :, ocp * 128:(ocp + 1) * 128]
                shi = pTstage[0:64, :].rearrange("p (a c) -> p a c", c=128)
                a_pj_last = pl.dma(QA, dhi, shi)
            if v_sdone_hist:
                pl.wait(QA, "V", v_sdone_hist[-1])
            pl.wait(QA, "S", s_b2)
            a_lnag = pl.dma(QA, bcA[:, 0:1024], RawAP(lnAg.tensor, 0, [[0, 128], [1, 1024]]))
            a_lnab = pl.dma(QA, bcA[:, 1024:2048], RawAP(lnAb.tensor, 0, [[0, 128], [1, 1024]]))
            a_pjb = pl.dma(QA, bcA[:, 2048:3072], RawAP(pb.tensor, 0, [[0, 128], [1, 1024]]))
            ps_free4 = {0: None, 1: None}
            pl.waitp("T", tr_free[0])
            pl.waitp("T", tr_free[1])
            pl.waitp("T", omm_free)
            for tile in range(NT):
                BAR()
                par = tile % 2
                colix = b * 16 + tile
                pl.waitp(QA, xfb_free[par])
                a_x = pl.dma(QA, xfb[par][:, :], x_in[b, tile * 128:(tile + 1) * 128, :])
                ps = P0 if tile % 2 == 0 else P1
                pl.wait("T", QA, a_pj_last)
                pl.wait("T", QA, a_oTh.get(15))
                pl.wait("T", "S", s_pjc_last[1])
                pl.wait("T", "S", s_oev[(14, 3)])
                pl.waitp("T", ps_free4[tile % 2])
                t_pm = None
                for dc in range(8):
                    for ec in range(2):
                        lhs = oTpack[:, dc * 2048 + tile * 128: dc * 2048 + (tile + 1) * 128]
                        rhs = pT[:, dc * 1024 + ec * 512: dc * 1024 + ec * 512 + 512]
                        t_pm = pl.op("T", lambda e, o=ps[:, ec * 512:ec * 512 + 512], l=lhs, r=rhs,
                                     st=(dc == 0), sp=(dc == 7): e.matmul(o, l, r, start=st, stop=sp),
                                     inc=(dc == 7 and ec == 1))
                t_pm_last = t_pm
                pl.wait("V", "T", t_pm)
                pl.wait("V", QA, max(a_x, a_pjb))
                pl.waitp("V", wk_free[0])
                pl.op("V", lambda e, p=ps[:, 0:1024]: e.tensor_add(wk[0][:, :], p, bcA[:, 2048:3072]))
                pl.op("V", lambda e, c=colix, xp=xfb[par]: e.tensor_scalar(
                    wk[1][:, :], xp[:, :], nm_st[:, c:c + 1], rs_st[:, c:c + 1],
                    op0=ALU.add, op1=ALU.mult))
                v_y0 = pl.op("V", lambda e: e.tensor_add(wk[0][:, :], wk[0][:, :], wk[1][:, :]), inc=True)
                xfb_free[par] = ("V", v_y0)
                ps_free4[tile % 2] = ("V", v_y0)
                pl.vred(lambda e: e.reduce_sum(col[:, 7:8], wk[0][:, :], axis=AX.X))
                pl.op("V", lambda e: e.tensor_scalar_mul(col[:, 8:9], col[:, 7:8], -1.0 / D))
                pl.op("V", lambda e: e.tensor_scalar_add(wk[1][:, :], wk[0][:, :], col[:, 8:9]))
                pl.op("V", lambda e: e.tensor_mul(wk[2][:, :], wk[1][:, :], wk[1][:, :]))
                v_s2 = pl.op("V", lambda e: e.reduce_sum(col[:, 9:10], wk[2][:, :], axis=AX.X), inc=True)
                pl.wait("S", "V", v_s2)
                s_q2 = pl.op("S", lambda e: e.activation(
                    col[:, 10:11], col[:, 9:10], AF.Sqrt, bias=epss[:, 0:1], scale=1.0 / D), inc=True)
                pl.wait("V", "S", s_q2)
                pl.op("V", lambda e: e.reciprocal(col[:, 11:12], col[:, 10:11]))
                pl.op("V", lambda e: e.tensor_scalar_mul(wk[1][:, :], wk[1][:, :], col[:, 11:12]))
                pl.op("V", lambda e: e.tensor_mul(wk[1][:, :], wk[1][:, :], bcA[:, 0:1024]))
                pl.waitp("V", sfb_free[par])
                v_sd = pl.op("V", lambda e, d=sfb[par][:, :]: e.tensor_add(
                    d, wk[1][:, :], bcA[:, 1024:2048]), inc=True)
                wk_free[0] = ("V", v_sd)
                pl.wait(QA, "V", v_sd)
                a_ss = pl.dma(QA, s_sc[b, tile], sfb[par][:, :])
                s_ss_cnt[(b, tile)] = a_ss
                sfb_free[par] = (QA, a_ss)
                v_sdone_hist.append(v_sd)

        emit_conv(10 ** 9)
        BAR()
        conv_all_store = pl.ctr[QA]
        conv_tail_loads = max(x for x in conv["load"] if x) if any(conv["load"]) else 0
        conv_tail_casts = [conv["cast"][0], conv["cast"][1]]

        pl.wait(QA, "V", v_sdone_hist[-1])
        a_lnfg = pl.dma(QA, bcA[:, 0:1024], RawAP(lnFg.tensor, 0, [[0, 128], [1, 1024]]))
        a_lnfb = pl.dma(QA, bcA[:, 1024:2048], RawAP(lnFb.tensor, 0, [[0, 128], [1, 1024]]))
        a_sb2 = pl.dma(QA, bcA[:, 2048:3072], RawAP(sb2.tensor, 0, [[0, 128], [1, 1024]]))
        pl.wait("V", QA, max(a_sb2, a_sigc))
        v_sb2 = pl.op("V", lambda e: e.tensor_scalar_mul(
            bcA[:, 2048:3072], bcA[:, 2048:3072], sigc[:, 0:1]), inc=True)

        wcnt = {"n": 0}
        w1t_free = [None, None, None]
        psh_free = [None, None]
        psy_free = [None, None]
        w2_free = ("T", t_pm_last)
        xnt_free = ("T", t_pm_last)
        h2_free = ("T", pl.ctr["T"])
        yacc_freeV = pl.ctr["V"]
        yacc_freeS = pl.ctr["S"]
        a_out = None
        out_free = [None, None]
        v_out_hist = [None, None]
        for tb in range(NTB):
            BAR()
            b, half = tb // 2, tb % 2
            pl.waitp(QA, xnt_free)
            pl.wait(QA, QA, xnt_store_cnt[b])
            a_xt = None
            for dc in range(8):
                a_xt = pl.dma(QA, xnT_tb[:, dc * 1024:(dc + 1) * 1024],
                              xnT_sc[b, dc, :, half * 1024:half * 1024 + 1024])
            pl.dfence()
            v_comb = {}
            s_ctc = {}
            for ts_ in range(TSUB):
                BAR()
                pl.wait("T", QA, a_xt)
                if tb == 0 and ts_ == 0:
                    pl.wait("T", "V", v_sdone_hist[-1])
                    pl.wait("T", "V", yacc_freeV)
                pl.waitp("T", v_comb.get(ts_ - 1) and ("V", v_comb[ts_ - 1]))
                t_rt = None
                for dc in range(8):
                    t_rt = pl.op("T", lambda e, o=P0[:, 0:8],
                                 l=xnT_tb[:, dc * 1024 + ts_ * 128: dc * 1024 + (ts_ + 1) * 128],
                                 r=rw_sb[:, dc * 8:(dc + 1) * 8],
                                 st=(dc == 0), sp=(dc == 7): e.matmul(o, l, r, start=st, stop=sp),
                                 inc=(dc == 7))
                pl.wait("V", "T", t_rt)
                pl.wait("V", QA, a_rb)
                pl.op("V", lambda e: e.tensor_add(pr0[:, :], P0[:, 0:8], rb_b[:, :]))
                v_sm = pl.op("V", lambda e: e.reduce_max(
                    col[:, 12:13], pr0[:, :], axis=AX.X, negate=True), inc=True)
                pl.wait("S", "V", v_sm)
                s_e8 = pl.op("S", lambda e: e.activation(
                    pre[:, :], pr0[:, :], AF.Exp, bias=col[:, 12:13]), inc=True)
                pl.wait("V", "S", s_e8)
                pl.vred(lambda e: e.reduce_sum(col[:, 13:14], pre[:, :], axis=AX.X))
                pl.op("V", lambda e: e.reciprocal(col[:, 13:14], col[:, 13:14]))
                pl.op("V", lambda e: e.tensor_scalar_mul(prb[:, :], pre[:, :], col[:, 13:14]))
                pl.vred(lambda e: e.reduce_max(col[:, 14:15], prb[:, :], axis=AX.X))
                pl.op("V", lambda e: e.tensor_scalar(msk8[:, :], prb[:, :], col[:, 14:15],
                                                     None, op0=ALU.is_lt))
                pl.op("V", lambda e: e.tensor_mul(msk8[:, :], msk8[:, :], prb[:, :]))
                pl.vred(lambda e: e.reduce_max(col[:, 15:16], msk8[:, :], axis=AX.X))
                pl.op("V", lambda e: e.tensor_scalar(msk8[:, :], prb[:, :], col[:, 15:16],
                                                     None, op0=ALU.is_ge))
                pl.op("V", lambda e: e.tensor_mul(msk8[:, :], msk8[:, :], prb[:, :]))
                pl.op("V", lambda e: e.tensor_tensor(col[:, 14:15], col[:, 14:15],
                                                     col[:, 15:16], op=ALU.add))
                pl.op("V", lambda e: e.reciprocal(col[:, 14:15], col[:, 14:15]))
                v_comb[ts_] = pl.op("V", lambda e, d=comb_sb[:, ts_ * 8:(ts_ + 1) * 8]:
                                    e.tensor_scalar_mul(d, msk8[:, :], col[:, 14:15]), inc=True)
                pl.wait("T", "V", v_comb[ts_])
                pl.waitp("T", s_ctc.get(ts_ - 1) and ("S", s_ctc[ts_ - 1]))
                t_ct = pl.op("T", lambda e, o=P1[0:8, 512:640],
                             src=comb_sb[:, ts_ * 8:(ts_ + 1) * 8]:
                             e.transpose(o, src, idf[:, :]), inc=True)
                pl.wait("S", "T", t_ct)
                s_ctc[ts_] = pl.op("S", lambda e, d=combT[0:8, ts_ * 128:(ts_ + 1) * 128],
                                   p=P1[0:8, 512:640]: e.copy(d, p), inc=True)
            v_yinit = None
            for ts_ in range(TSUB):
                BAR()
                pl.wait("T", "S", max(s_ctc[ts_], s_b2))
                if ts_ > 0:
                    pl.wait("T", "V", v_yinit)
                elif tb == 0:
                    pl.wait("T", "V", yacc_freeV)
                    pl.wait("T", "S", yacc_freeS)
                t_b2 = None
                for dh in range(2):
                    t_b2 = pl.op("T", lambda e, o=P1[:, dh * 512:dh * 512 + 512],
                                 l=combT[0:8, ts_ * 128:(ts_ + 1) * 128],
                                 r=b2_bf[0:8, dh * 512:dh * 512 + 512]:
                                 e.matmul(o, l, r, start=True, stop=True), inc=(dh == 1))
                pl.wait("V", "T", t_b2)
                pl.wait("V", "V", 0 if v_sb2 is None else v_sb2)
                if tb == 0 and ts_ >= 6:
                    pl.wait("V", QA, conv_tail_loads)
                    for cc in conv_tail_casts:
                        pl.waitp("V", cc)
                v_yinit = pl.op("V", lambda e, d=y_acc[:, ts_ * 1024:(ts_ + 1) * 1024]:
                                e.tensor_add(d, P1[:, 0:1024], bcA[:, 2048:3072]), inc=True)
            v_yinit_all = v_yinit
            s_h2 = {}
            v_yev_last = {}
            for e_ in range(9):
                Fh = 8 if e_ == 8 else 16
                shared = (e_ == 8)
                for fh in range(2):
                    a_w2h = None
                    w2q = 0
                    for fc in range(Fh):
                        BAR()
                        fcg = fh * Fh + fc
                        slot = wcnt["n"] % 3
                        pl.waitp(QW, w1t_free[slot])
                        if wcnt["n"] == 0:
                            pl.wait(QW, QA, conv_tail_loads)
                            for cc in conv_tail_casts:
                                pl.waitp(QW, cc)
                            pl.wait(QW, QA, pl.ctr[QA])
                        if shared:
                            srcw = RawAP(sw1b.tensor, fcg * 128, [[F2, 128], [128 * F2, 8], [1, 128]])
                        else:
                            srcw = RawAP(w1b.tensor, e_ * D * F + fcg * 128,
                                         [[F, 128], [128 * F, 8], [1, 128]])
                        a_w1 = pl.dma(QW, w1t[slot][:, :], srcw)
                        wcnt["n"] += 1
                        psh = P0[:, (fc % 2) * 1024:(fc % 2) * 1024 + 1024]
                        pl.wait("T", QW, a_w1)
                        if tb == 0 and e_ == 0 and fh == 0 and fc < 2:
                            pl.wait("T", "V", v_comb[TSUB - 1])
                        pl.waitp("T", psh_free[fc % 2])
                        if fc == 0 and fh == 0:
                            pl.waitp("T", w2_free)
                        t_h = None
                        for t5 in range(2):
                            for dc in range(8):
                                t_h = pl.op("T", lambda e, o=psh[:, t5 * 512:t5 * 512 + 512],
                                            l=w1t[slot][:, dc * 128:(dc + 1) * 128],
                                            r=xnT_tb[:, dc * 1024 + t5 * 512: dc * 1024 + t5 * 512 + 512],
                                            st=(dc == 0), sp=(dc == 7):
                                            e.matmul(o, l, r, start=st, stop=sp),
                                            inc=(dc == 7 and t5 == 1))
                        w1t_free[slot] = ("T", t_h)
                        pl.wait("S", "T", t_h)
                        pl.wait("S", "V", v_bgs if shared else v_bg)
                        if shared:
                            bcol, gcol = bgs[:, fcg:fcg + 1], gts[:, fcg:fcg + 1]
                        else:
                            bcol = bg_all[:, e_ * 32 + fcg:e_ * 32 + fcg + 1]
                            gcol = gt_all[:, e_ * 32 + fcg:e_ * 32 + fcg + 1]
                        s_h2[fc] = pl.op(
                            "S", lambda e, d=h2h[:, fc * 1024:(fc + 1) * 1024], p=psh,
                            bc=bcol, gc=gcol:
                            e.activation(d, p, AF.Gelu, bias=bc, scale=gc), inc=True)
                        psh_free[fc % 2] = ("S", s_h2[fc])
                        nw2 = Fh // 2
                        if fc < nw2:
                            for sub in range(2):
                                fcl = fc * 2 + sub
                                if tb == 0 and e_ == 0 and fh == 0:
                                    pl.wait(QA, QA, conv_all_store)
                                pl.waitp(QA, w2_free)
                                if shared:
                                    src2 = RawAP(sw2b.tensor, (fh * Fh + fcl) * 128 * D,
                                                 [[D, 128], [1, D]])
                                else:
                                    src2 = RawAP(w2b.tensor, e_ * F * D + (fh * Fh + fcl) * 128 * D,
                                                 [[D, 128], [1, D]])
                                a_w2h = pl.dma(QA, w2h[:, fcl * 1024:(fcl + 1) * 1024], src2)
                    t_y = None
                    for ts_ in range(TSUB):
                        BAR()
                        psy = P1[:, (ts_ % 2) * 1024:(ts_ % 2) * 1024 + 1024]
                        pl.wait("T", "S", s_h2[Fh - 1])
                        pl.wait("T", QA, a_w2h)
                        if tb == 0 and e_ == 0 and fh == 0 and ts_ < 2:
                            pl.wait("T", "V", v_yinit_all)
                        pl.waitp("T", psy_free[ts_ % 2])
                        for dh in range(2):
                            for fc in range(Fh):
                                t_y = pl.op("T", lambda e, o=psy[:, dh * 512:dh * 512 + 512],
                                            l=h2h[:, fc * 1024 + ts_ * 128: fc * 1024 + (ts_ + 1) * 128],
                                            r=w2h[:, fc * 1024 + dh * 512: fc * 1024 + dh * 512 + 512],
                                            st=(fc == 0), sp=(fc == Fh - 1):
                                            e.matmul(o, l, r, start=st, stop=sp),
                                            inc=(fc == Fh - 1 and dh == 1))
                        ccol = sigc[:, 0:1] if shared else comb_sb[:, ts_ * 8 + e_: ts_ * 8 + e_ + 1]
                        wkb = wk[ts_ % 2]
                        pl.wait("S", "T", t_y)
                        pl.waitp("S", wk_free[ts_ % 2])
                        s_yt = pl.op("S", lambda e, d=wkb[:, :], p=psy, c=ccol:
                                     e.mul(d, p, c), inc=True)
                        pl.wait("V", "S", s_yt)
                        v_ye = pl.op("V", lambda e, d=y_acc[:, ts_ * 1024:(ts_ + 1) * 1024],
                                     w_=wkb: e.tensor_add(d, d, w_[:, :]), inc=True)
                        wk_free[ts_ % 2] = ("V", v_ye)
                        psy_free[ts_ % 2] = ("V", v_ye)
                        v_yev_last[ts_] = v_ye
                    w2_free = ("T", t_y)
            for ts_ in range(TSUB):
                BAR()
                par = ts_ % 2
                tile = half * 8 + ts_
                pl.waitp(QA, xfb_free[par])
                a_x = pl.dma(QA, xfb[par][:, :], x_in[b, tile * 128:(tile + 1) * 128, :])
                pl.waitp(QA, sfb_free[par])
                pl.wait(QA, QA, s_ss_cnt[(b, tile)])
                a_s = pl.dma(QA, sfb[par][:, :], s_sc[b, tile])
                ysl = y_acc[:, ts_ * 1024:(ts_ + 1) * 1024]
                pl.wait("V", "V", v_yev_last[ts_])
                pl.waitp("V", wk_free[2])
                pl.waitp("V", wk_free[0])
                pl.vred(lambda e, y=ysl: e.reduce_sum(col[:, 7:8], y, axis=AX.X))
                pl.op("V", lambda e: e.tensor_scalar_mul(col[:, 8:9], col[:, 7:8], -1.0 / D))
                pl.op("V", lambda e, y=ysl: e.tensor_scalar_add(wk[2][:, :], y, col[:, 8:9]))
                pl.op("V", lambda e: e.tensor_mul(wk[0][:, :], wk[2][:, :], wk[2][:, :]))
                v_ss = pl.op("V", lambda e: e.reduce_sum(col[:, 9:10], wk[0][:, :], axis=AX.X), inc=True)
                pl.wait("S", "V", v_ss)
                s_sq = pl.op("S", lambda e: e.activation(
                    col[:, 10:11], col[:, 9:10], AF.Sqrt, bias=epss[:, 0:1], scale=1.0 / D), inc=True)
                pl.wait("V", "S", s_sq)
                pl.op("V", lambda e: e.reciprocal(col[:, 11:12], col[:, 10:11]))
                pl.op("V", lambda e: e.tensor_scalar_mul(wk[2][:, :], wk[2][:, :], col[:, 11:12]))
                pl.op("V", lambda e: e.tensor_mul(wk[2][:, :], wk[2][:, :], bcA[:, 0:1024]))
                v_t2 = pl.op("V", lambda e: e.tensor_add(wk[0][:, :], wk[2][:, :],
                                                         bcA[:, 1024:2048]), inc=True)
                pl.wait("S", "V", v_t2)
                s_g = pl.op("S", lambda e: e.activation(wk[2][:, :], wk[0][:, :], AF.Silu), inc=True)
                pl.wait("V", "S", s_g)
                pl.wait("V", QA, max(a_s, a_x))
                pl.op("V", lambda e, sp=sfb[par]: e.tensor_mul(wk[0][:, :], wk[2][:, :], sp[:, :]))
                v_out = pl.op("V", lambda e, xp=xfb[par]: e.tensor_add(
                    wk[0][:, :], wk[0][:, :], xp[:, :]), inc=True)
                xfb_free[par] = ("V", v_out)
                sfb_free[par] = ("V", v_out)
                wk_free[2] = ("V", v_out)
                pl.wait(QA, "V", v_out)
                a_out = pl.dma(QA, out_d[b, tile * 128:(tile + 1) * 128, :], wk[0][:, :])
                wk_free[0] = (QA, a_out)
        BAR()
        pl.wait(QA, QA, pl.ctr[QA])
        pl.wait(QW, QA, pl.ctr[QA])

        for oplist in pl.ops.values():
            for f in oplist:
                if hasattr(f, "semname"):
                    f.sem = sems[f.semname]
        ctx.enter_context(nc.allow_non_contiguous_dma(
            reason="small one-time column-layout constant loads"))

        @block.sync
        def _(e):
            for f in pl.ops["W"]:
                f(e)

        @block.gpsimd
        def _(e):
            for f in pl.ops["A"]:
                f(e)

        @block.tensor
        def _(e):
            for f in pl.ops["T"]:
                f(e)

        @block.scalar
        def _(e):
            for f in pl.ops["S"]:
                f(e)

        @block.vector
        def _(e):
            for f in pl.ops["V"]:
                f(e)

    return nc


_NC_CACHE = None


def kernel(**inputs):
    global _NC_CACHE
    x = np.ascontiguousarray(np.asarray(inputs["L0_attn_x"], dtype=np.float32))
    rpb = np.asarray(inputs["L0_attn_rpb"], dtype=np.float32)

    p = np.arange(128)[:, None]
    c = np.arange(2432)[None, :]
    idx = np.clip(c + 127 - p, 0, 4094)
    biasz = rpb[idx, :]
    biasz = np.transpose(biasz, (2, 0, 1)).copy()
    biasz[:, (c > p + 1920)] = -1e30
    biasz = np.ascontiguousarray(biasz.astype(np.float32))

    identf = np.eye(128, dtype=np.float32)
    epsc = np.array([1e-5, 0.0, 0.0, 0.0], dtype=np.float32)

    if _NC_CACHE is None:
        _NC_CACHE = build_program()
    nc = _NC_CACHE

    wnames = ["L0_attn_W_w", "L0_attn_W_b", "L0_attn_proj_w", "L0_attn_proj_b",
              "L0_attn_ln_g", "L0_attn_ln_b", "L0_ffn_router_w", "L0_ffn_router_b",
              "L0_ffn_w1", "L0_ffn_b1", "L0_ffn_gate", "L0_ffn_w2", "L0_ffn_b2",
              "L0_ffn_sw1", "L0_ffn_sb1", "L0_ffn_sgate", "L0_ffn_sw2", "L0_ffn_sb2",
              "L0_ffn_ln_g", "L0_ffn_ln_b"]
    base = {k: np.ascontiguousarray(np.asarray(inputs[k], dtype=np.float32)) for k in wnames}
    base["L0_ffn_shared_weight"] = np.asarray(inputs["L0_ffn_shared_weight"],
                                              dtype=np.float32).reshape(1)
    base["biasz"] = biasz
    base["identf"] = identf
    base["epsc"] = epsc

    in_maps = []
    for ci in range(8):
        m = dict(base)
        m["x"] = np.ascontiguousarray(x[ci * 2:(ci + 1) * 2])
        in_maps.append(m)

    res = run_bass_kernel_spmd(nc, in_maps, core_ids=list(range(8)))
    return np.concatenate([r["out"] for r in res.results], axis=0)



# version 04
# speedup vs baseline: 1.4187x; 1.4043x over previous
import numpy as np
from contextlib import ExitStack

import concourse.bass as bass
import concourse.mybir as mybir
from concourse.bass_utils import run_bass_kernel_spmd
from concourse.ap import AP as RawAP

F32 = mybir.dt.float32
BF16 = mybir.dt.bfloat16
AF = mybir.ActivationFunctionType
ALU = mybir.AluOpType
AX = mybir.AxisListType

BS = 2
S = 2048
D = 1024
H = 16
E = 8
F = 4096
F2 = 2048
NT = 16
NTB = 4
TSUB = 8


class Planner:
    def __init__(self):
        self.ops = {k: [] for k in ("W", "A", "T", "S", "V")}
        self.ctr = {k: 0 for k in ("W", "A", "T", "S", "V", "F")}
        self.lastw = {}
        self.vspacer = None

    def dfence(self):
        self.ctr["F"] += 1

        def f(e):
            e.drain().then_inc(f.sem, 1)
        f.semname = "F"
        self.ops["A"].append(f)
        return self.ctr["F"]

    def dma(self, q, out, in_):
        self.ctr[q] += 16

        def f(e, out=out, in_=in_):
            e.dma_start(out=out, in_=in_).then_inc(f.sem, 16)
        f.semname = q
        self.ops[q].append(f)
        return self.ctr[q]

    def op(self, eng, fn, inc=False):
        if eng == "V":
            inc = True
        ret = None
        if inc:
            self.ctr[eng] += 1
            ret = self.ctr[eng]

            def g(e, fn=fn):
                fn(e).then_inc(g.sem, 1)
            g.semname = eng
            self.ops[eng].append(g)
        else:
            self.ops[eng].append(fn)
        if eng == "V":
            self.wait("V", "V", ret)
        return ret

    def vred(self, fn):
        v = self.op("V", fn, inc=True)
        self.wait("V", "V", v)
        return v

    def wait(self, eng, sem, val):
        if not val:
            return
        key = (eng, sem)
        if self.lastw.get(key, 0) >= val:
            return
        self.lastw[key] = val

        def w(e, val=val):
            e.wait_ge(w.sem, val)
        w.semname = sem
        self.ops[eng].append(w)

    def waitp(self, eng, pair):
        if pair:
            self.wait(eng, pair[0], pair[1])


def build_program():
    nc = bass.Bass("TRN2", target_bir_lowering=False)

    def din(name, shape, dt=F32):
        return nc.dram_tensor(name, shape, dt, kind="ExternalInput").ap()

    x_in = din("x", [BS, S, D])
    Ww = din("L0_attn_W_w", [3 * D, D])
    Wb = din("L0_attn_W_b", [3 * D])
    pw = din("L0_attn_proj_w", [D, D])
    pb = din("L0_attn_proj_b", [D])
    lnAg = din("L0_attn_ln_g", [D])
    lnAb = din("L0_attn_ln_b", [D])
    rtw = din("L0_ffn_router_w", [D, E])
    rtb = din("L0_ffn_router_b", [E])
    w1 = din("L0_ffn_w1", [E, D, F])
    b1 = din("L0_ffn_b1", [E, F])
    gt = din("L0_ffn_gate", [E, F])
    w2 = din("L0_ffn_w2", [E, F, D])
    b2 = din("L0_ffn_b2", [E, D])
    sw1 = din("L0_ffn_sw1", [D, F2])
    sb1 = din("L0_ffn_sb1", [F2])
    sgt = din("L0_ffn_sgate", [F2])
    sw2 = din("L0_ffn_sw2", [F2, D])
    sb2 = din("L0_ffn_sb2", [D])
    shw = din("L0_ffn_shared_weight", [1])
    lnFg = din("L0_ffn_ln_g", [D])
    lnFb = din("L0_ffn_ln_b", [D])
    biasz = din("biasz", [H, 128, 2432])
    identf = din("identf", [128, 128])
    epsc = din("epsc", [4])

    out_d = nc.dram_tensor("out", [BS, S, D], F32, kind="ExternalOutput").ap()

    def scr(name, shape, dt):
        return nc.dram_tensor(name, shape, dt, kind="Internal").ap()
    xnT_sc = scr("xnT_sc", [BS, 8, 128, S], BF16)
    qT_sc = scr("qT_sc", [BS, 8, 128, S], BF16)
    kT_sc = scr("kT_sc", [BS, 8, 128, S], BF16)
    v_sc = scr("v_sc", [BS, NT, 128, D], BF16)
    s_sc = scr("s_sc", [BS, NT, 128, D], F32)
    w1b = scr("w1b", [E, D, F], BF16)
    w2b = scr("w2b", [E, F, D], BF16)
    sw1b = scr("sw1b", [D, F2], BF16)
    sw2b = scr("sw2b", [F2, D], BF16)
    sig_rt = scr("sig_rt", [1], F32)

    pl = Planner()
    QW, QA = "W", "A"

    with ExitStack() as ctx:
        sb = lambda n, s, d: ctx.enter_context(nc.sbuf_tensor(n, s, d))
        idf = sb("idf", [128, 128], F32)
        qkb = sb("qkb", [128, 16], F32)
        vb64 = sb("vb64", [64, 16], F32)
        bcA = sb("bcA", [128, 3072], F32)
        rb_b = sb("rb_b", [128, 8], F32)
        rw_f = sb("rw_f", [128, 64], F32)
        rw_sb = sb("rw_sb", [128, 64], BF16)
        b2_bf = sb("b2_bf", [8, 1024], BF16)
        bg_all = sb("bg_all", [128, 256], F32)
        gt_all = sb("gt_all", [128, 256], F32)
        bgs = sb("bgs", [128, 16], F32)
        gts = sb("gts", [128, 16], F32)
        nm_st = sb("nm_st", [128, 32], F32)
        rs_st = sb("rs_st", [128, 32], F32)
        epss = sb("epss", [128, 4], F32)
        sig1 = sb("sig1", [1, 1], F32)
        sigc = sb("sigc", [128, 1], F32)
        AR1 = sb("AR1", [128, 16384], BF16)
        AR2 = sb("AR2", [128, 24576], BF16)
        AR3 = sb("AR3", [128, 8192], F32)
        qtb = [sb("qtb0", [64, 2048], BF16), sb("qtb1", [64, 2048], BF16)]
        ktb = [sb("ktb0", [64, 2048], BF16), sb("ktb1", [64, 2048], BF16)]
        vsb = [sb("vsb0", [128, 1024], BF16), sb("vsb1", [128, 1024], BF16)]
        bias_h = sb("bias_h", [128, 2432], F32)
        oTstage = sb("oTstage", [64, 2048], BF16)
        pTstage = sb("pTstage", [64, 1024], BF16)
        w1t = [sb("w1t0", [128, 1024], BF16), sb("w1t1", [128, 1024], BF16),
               sb("w1t2", [128, 1024], BF16)]
        qkev = [sb("qkev0", [128, 1024], BF16), sb("qkev1", [128, 1024], BF16)]
        xfb = [sb("xfb0", [128, 1024], F32), sb("xfb1", [128, 1024], F32)]
        sfb = [sb("sfb0", [128, 1024], F32), sb("sfb1", [128, 1024], F32)]
        wk = [sb("wk0", [128, 1024], F32), sb("wk1", [128, 1024], F32),
              sb("wk2", [128, 1024], F32)]
        col = sb("colb", [128, 16], F32)
        dum = sb("dum", [1, 1], F32)
        dum2 = sb("dum2", [1, 1], F32)
        pr0 = sb("pr0", [128, 8], F32)
        pre = sb("pre", [128, 8], F32)
        prb = sb("prb", [128, 8], F32)
        msk8 = sb("msk8", [128, 8], F32)
        comb_sb = sb("comb_sb", [128, 64], F32)
        combT = pTstage[0:8, :]
        b2f = bcA[0:8, 0:1024]

        P0 = ctx.enter_context(nc.psum_tensor("P0", [128, 2048], F32))
        P1 = ctx.enter_context(nc.psum_tensor("P1", [128, 2048], F32))

        sems = {k: ctx.enter_context(nc.semaphore("sem" + k))
                for k in ("W", "A", "T", "S", "V", "F")}
        block = ctx.enter_context(nc.Block())

        def BAR():
            pl.op("S", lambda e: e.copy(dum[:, :], dum[:, :]), inc=True)
            pl.op("V", lambda e: e.tensor_copy(dum[:, :], dum[:, :]), inc=True)
            vals = dict(pl.ctr)
            for eng in ("T", "S", "V", "W", "A"):
                for sem_, v_ in vals.items():
                    if sem_ != eng and v_ > 0:
                        pl.wait(eng, sem_, v_)

        xnT_sb = AR1
        h2h = AR1
        WT = AR2
        oTpack = AR2[:, 0:16384]
        pT = AR2[:, 16384:24576]
        w2h = AR2[:, 0:16384]
        xnT_tb = AR2[:, 16384:24576]
        sc_sb = AR3[:, 0:2048]
        pe_sb = AR3[:, 2048:4096]
        pn_sb = AR3[:, 4096:6144]
        cvst = [AR3[:, 6144:7168], AR3[:, 7168:8192]]
        y_acc = AR3

        pl.dma(QA, idf[:, :], identf[:, :])
        a_qkb = pl.dma(QA, qkb[:, :], RawAP(Wb.tensor, 0, [[1, 128], [128, 16]]))
        a_vb = pl.dma(QA, vb64[:, :], RawAP(Wb.tensor, 2048, [[1, 64], [64, 16]]))
        a_rb = pl.dma(QA, rb_b[:, :], RawAP(rtb.tensor, 0, [[0, 128], [1, 8]]))
        a_rw = pl.dma(QA, rw_f[:, :], RawAP(rtw.tensor, 0, [[8, 128], [1024, 8], [1, 8]]))
        a_b2 = pl.dma(QA, b2f[:, :], b2[:, :])
        for e_ in range(E):
            a_bg = pl.dma(QA, bg_all[:, e_ * 32:(e_ + 1) * 32],
                          RawAP(b1.tensor, e_ * 4096, [[1, 128], [128, 32]]))
            a_gt = pl.dma(QA, gt_all[:, e_ * 32:(e_ + 1) * 32],
                          RawAP(gt.tensor, e_ * 4096, [[1, 128], [128, 32]]))
        a_bgs = pl.dma(QA, bgs[:, :], RawAP(sb1.tensor, 0, [[1, 128], [128, 16]]))
        a_gts = pl.dma(QA, gts[:, :], RawAP(sgt.tensor, 0, [[1, 128], [128, 16]]))
        a_eps = pl.dma(QA, epss[:, :], RawAP(epsc.tensor, 0, [[0, 128], [1, 4]]))
        a_sh = pl.dma(QA, sig1[:, :], shw[None, :])
        F1 = pl.dfence()
        pl.wait("S", "F", F1)
        pl.wait("V", "F", F1)
        pl.wait("S", QA, a_rw)
        pl.op("S", lambda e: e.copy(rw_sb[:, :], rw_f[:, :]))
        pl.wait("S", QA, a_b2)
        s_b2 = pl.op("S", lambda e: e.copy(b2_bf[:, :], b2f[:, :]), inc=True)
        pl.wait("S", QA, a_sh)
        s_sig = pl.op("S", lambda e: e.activation(sig1[:, :], sig1[:, :], AF.Sigmoid), inc=True)
        pl.wait(QA, "S", s_sig)
        a_sr = pl.dma(QA, sig_rt[None, :], sig1[:, :])
        pl.wait(QA, QA, a_sr)
        pl.dfence()
        a_sigc = pl.dma(QA, sigc[:, :], RawAP(sig_rt.tensor, 0, [[0, 128], [1, 1]]))
        pl.dfence()
        pl.wait("V", QA, max(a_bg, a_gt))
        v_bg = pl.op("V", lambda e: e.tensor_mul(bg_all[:, :], bg_all[:, :], gt_all[:, :]), inc=True)
        pl.wait("V", QA, max(a_bgs, a_gts))
        v_bgs = pl.op("V", lambda e: e.tensor_mul(bgs[:, :], bgs[:, :], gts[:, :]), inc=True)

        cvst_free = [None, None]
        qkev_free = [None, None]
        xfb_free = [None, None]
        sfb_free = [None, None]
        wk_free = [None] * 3

        s_wt = None
        for oc in range(24):
            BAR()
            par = oc % 2
            pl.waitp(QA, cvst_free[par])
            a_w = pl.dma(QA, cvst[par][:, 0:1024], Ww[oc * 128:(oc + 1) * 128, :])
            for dc in range(8):
                slot = dc % 2
                psl = P0[:, slot * 512:slot * 512 + 128]
                pl.wait("T", QA, a_w)
                if dc >= 2 or oc > 0:
                    pl.wait("T", "S", (s_wt or 0) - 1 if dc >= 2 else s_wt)
                t_wt = pl.op("T", lambda e, p=psl, src=cvst[par][:, dc * 128:(dc + 1) * 128]:
                             e.transpose(p, src, idf[:, :]), inc=True)
                dst = WT[:, dc * 3072 + oc * 128: dc * 3072 + (oc + 1) * 128]
                pl.wait("S", "T", t_wt)
                s_wt = pl.op("S", lambda e, d=dst, p=psl: e.copy(d, p), inc=True)
            cvst_free[par] = ("T", pl.ctr["T"])

        s_xnT_last = {}
        last_t_tr = None
        last_v_xn = None
        xnt_store_cnt = {}
        qkv_store_cnt = {}
        for b in range(BS):
            for tile in range(NT):
                BAR()
                colix = b * 16 + tile
                par = tile % 2
                pl.waitp(QA, xfb_free[par])
                a_x = pl.dma(QA, xfb[par][:, :], x_in[b, tile * 128:(tile + 1) * 128, :])
                pl.wait("V", QA, a_x)
                pl.wait("V", QA, a_eps)
                pl.vred(lambda e, xp=xfb[par]: e.reduce_sum(col[:, 0:1], xp[:, :], axis=AX.X))
                v_nm = pl.op("V", lambda e, c=colix: e.tensor_scalar_mul(
                    nm_st[:, c:c + 1], col[:, 0:1], -1.0 / D), inc=True)
                pl.wait("S", "V", v_nm)
                s_xm = pl.op("S", lambda e, c=colix, xp=xfb[par]: e.add(
                    wk[0][:, :], xp[:, :], nm_st[:, c:c + 1]), inc=True)
                xfb_free[par] = ("S", s_xm)
                pl.wait("V", "S", s_xm)
                pl.op("V", lambda e: e.tensor_mul(wk[1][:, :], wk[0][:, :], wk[0][:, :]))
                v_sq = pl.op("V", lambda e: e.reduce_sum(col[:, 1:2], wk[1][:, :], axis=AX.X), inc=True)
                pl.wait("S", "V", v_sq)
                s_si = pl.op("S", lambda e: e.activation(
                    col[:, 2:3], col[:, 1:2], AF.Sqrt, bias=epss[:, 0:1], scale=1.0 / D), inc=True)
                pl.wait("V", "S", s_si)
                pl.op("V", lambda e, c=colix: e.reciprocal(rs_st[:, c:c + 1], col[:, 2:3]))
                pl.wait("V", "T", last_t_tr)
                last_v_xn = pl.op("V", lambda e, c=colix: e.tensor_scalar_mul(
                    wk[2][:, :], wk[0][:, :], rs_st[:, c:c + 1]), inc=True)
                for dc in range(8):
                    slot = dc % 2
                    psl = P0[:, slot * 512:slot * 512 + 128]
                    pl.wait("T", "V", last_v_xn)
                    pl.wait("T", "S", s_xnT_last.get((slot,)))
                    last_t_tr = pl.op("T", lambda e, p=psl, src=wk[2][:, dc * 128:(dc + 1) * 128]:
                                      e.transpose(p, src, idf[:, :]), inc=True)
                    dst = xnT_sb[:, dc * 2048 + tile * 128: dc * 2048 + (tile + 1) * 128]
                    pl.wait("S", "T", last_t_tr)
                    s_xnT_last[(slot,)] = pl.op("S", lambda e, d=dst, p=psl: e.copy(d, p), inc=True)
            s_xnT_all = pl.ctr["S"]
            pl.wait(QA, "S", s_xnT_all)
            for dc in range(8):
                xnt_store_cnt[b] = pl.dma(QA, xnT_sc[b, dc], xnT_sb[:, dc * 2048:(dc + 1) * 2048])

            ps_free = {0: None, 1: None}
            pl.wait("T", "S", s_xnT_all)
            for oc in range(16):
                BAR()
                ps = P0 if oc % 2 == 0 else P1
                pl.waitp("T", ps_free[oc % 2])
                t_mm = None
                for dc in range(8):
                    for tc in range(4):
                        lhs = WT[:, dc * 3072 + oc * 128: dc * 3072 + (oc + 1) * 128]
                        rhs = xnT_sb[:, dc * 2048 + tc * 512: dc * 2048 + tc * 512 + 512]
                        t_mm = pl.op("T", lambda e, o=ps[:, tc * 512:tc * 512 + 512], l=lhs, r=rhs,
                                     st=(dc == 0), sp=(dc == 7): e.matmul(o, l, r, start=st, stop=sp),
                                     inc=(dc == 7 and tc == 3))
                dst_sc = qT_sc if oc < 8 else kT_sc
                occ = oc % 8
                sev = None
                for hh in range(2):
                    pl.wait("S", "T", t_mm)
                    pl.waitp("S", qkev_free[hh])
                    sev = pl.op("S", lambda e, d=qkev[hh][:, :],
                                p=ps[:, hh * 1024:hh * 1024 + 1024],
                                bcol=qkb[:, oc:oc + 1]:
                                e.add(d, p, bcol), inc=True)
                    pl.wait(QA, "S", sev)
                    adma = pl.dma(QA, dst_sc[b, occ, :, hh * 1024:hh * 1024 + 1024], qkev[hh][:, :])
                    qkev_free[hh] = (QA, adma)
                ps_free[oc % 2] = ("S", sev)
            for tile in range(NT):
                BAR()
                ps = P0 if tile % 2 == 0 else P1
                pl.waitp("T", ps_free[tile % 2])
                for dc in range(8):
                    for oc2 in range(2):
                        lhs = xnT_sb[:, dc * 2048 + tile * 128: dc * 2048 + (tile + 1) * 128]
                        rhs = WT[:, dc * 3072 + 2048 + oc2 * 512: dc * 3072 + 2048 + oc2 * 512 + 512]
                        t_mm = pl.op("T", lambda e, o=ps[:, oc2 * 512:oc2 * 512 + 512], l=lhs, r=rhs,
                                     st=(dc == 0), sp=(dc == 7): e.matmul(o, l, r, start=st, stop=sp),
                                     inc=(dc == 7 and oc2 == 1))
                hh = tile % 2
                pl.wait("S", "T", t_mm)
                pl.waitp("S", qkev_free[hh])
                sev = pl.op("S", lambda e, d=qkev[hh][:, :], p=ps[:, 0:1024]: e.copy(d, p), inc=True)
                pl.wait(QA, "S", sev)
                adma = pl.dma(QA, v_sc[b, tile], qkev[hh][:, :])
                qkev_free[hh] = (QA, adma)
                ps_free[tile % 2] = ("S", sev)
            qkv_store_cnt[b] = pl.ctr[QA]

        conv_jobs = []
        for e_ in range(E):
            for dc in range(8):
                for q4 in range(4):
                    off = e_ * D * F + dc * 128 * F + q4 * 1024
                    conv_jobs.append((RawAP(w1.tensor, off, [[F, 128], [1, 1024]]),
                                      RawAP(w1b.tensor, off, [[F, 128], [1, 1024]])))
        for e_ in range(E):
            for fc in range(32):
                off = e_ * F * D + fc * 128 * D
                conv_jobs.append((RawAP(w2.tensor, off, [[D, 128], [1, 1024]]),
                                  RawAP(w2b.tensor, off, [[D, 128], [1, 1024]])))
        for dc in range(8):
            for hh in range(2):
                off = dc * 128 * F2 + hh * 1024
                conv_jobs.append((RawAP(sw1.tensor, off, [[F2, 128], [1, 1024]]),
                                  RawAP(sw1b.tensor, off, [[F2, 128], [1, 1024]])))
        for fc in range(16):
            off = fc * 128 * D
            conv_jobs.append((RawAP(sw2.tensor, off, [[D, 128], [1, 1024]]),
                              RawAP(sw2b.tensor, off, [[D, 128], [1, 1024]])))
        conv = {"i": 0, "load": [None, None], "cast": [None, None]}

        def emit_conv(njobs):
            for _ in range(njobs):
                i = conv["i"]
                if i >= len(conv_jobs):
                    return
                conv["i"] = i + 1
                par = i % 2
                eng = "S" if par == 0 else "V"
                src, dst = conv_jobs[i]
                pl.waitp(QA, cvst_free[par])
                a_l = pl.dma(QA, cvst[par][:, 0:1024], src)
                conv["load"][par] = a_l
                pl.wait(eng, QA, a_l)
                pl.waitp(eng, qkev_free[par])
                if eng == "S":
                    cinc = pl.op("S", lambda e, d=qkev[par][:, :], s_=cvst[par][:, 0:1024]:
                                 e.copy(d, s_), inc=True)
                else:
                    cinc = pl.op("V", lambda e, d=qkev[par][:, :], s_=cvst[par][:, 0:1024]:
                                 e.tensor_copy(d, s_), inc=True)
                cvst_free[par] = (eng, cinc)
                conv["cast"][par] = (eng, cinc)
                pl.wait(QA, eng, cinc)
                a_st = pl.dma(QA, dst, qkev[par][:, :])
                qkev_free[par] = (QA, a_st)

        qk_free = [None, None]
        v_free = [None, None]
        bias_free = None
        stt_last = [None] * 4
        tr_free = [None, None]
        omm_free = None
        v_sdone_hist = []
        s_ss_cnt = {}
        t_pm_last = None
        for b in range(BS):
            s_oev = {}
            a_oTh = {}
            ost_free = None
            pl.wait(QA, QA, qkv_store_cnt[b])
            for h in range(H):
                BAR()
                par = h % 2
                pl.waitp(QA, qk_free[par])
                a_q = pl.dma(QA, qtb[par][:, :], qT_sc[b, h // 2, (h % 2) * 64:(h % 2) * 64 + 64, :])
                a_k = pl.dma(QA, ktb[par][:, :], kT_sc[b, h // 2, (h % 2) * 64:(h % 2) * 64 + 64, :])
                pl.waitp(QA, v_free[par])
                a_v = None
                for vq in range(4):
                    srcv = RawAP(v_sc.tensor, (b * NT + vq * 4) * 128 * D + h * 64,
                                 [[D, 128], [128 * D, 4], [1, 64]])
                    a_v = pl.dma(QA, vsb[par][:, vq * 256:(vq + 1) * 256], srcv)
                pl.waitp(QA, bias_free)
                a_bi = pl.dma(QA, bias_h[:, :], biasz[h, :, :])
                emit_conv(9)
                s_pn = v_pn = None
                for ic in range(4):
                    w = (ic + 1) * 512
                    for qt in range(4):
                        i0 = ic * 512 + qt * 128
                        boff = 1920 - i0
                        pl.wait("T", QA, a_k)
                        for jb in range(ic + 1):
                            pl.waitp("T", ("V", stt_last[jb]) if stt_last[jb] else None)
                            t_mm = pl.op("T", lambda e, o=P0[:, jb * 512:jb * 512 + 512],
                                         l=qtb[par][:, i0:i0 + 128],
                                         r=ktb[par][:, jb * 512:jb * 512 + 512]:
                                         e.matmul(o, l, r, start=True, stop=True), inc=True)
                            pl.wait("V", "T", t_mm)
                            pl.wait("V", QA, a_bi)
                            stt_last[jb] = pl.op(
                                "V", lambda e, d=sc_sb[:, jb * 512:jb * 512 + 512],
                                p=P0[:, jb * 512:jb * 512 + 512],
                                bsl=bias_h[:, boff + jb * 512: boff + jb * 512 + 512]:
                                e.scalar_tensor_tensor(d, p, 1.0 / 32.0, bsl,
                                                       op0=ALU.mult, op1=ALU.add), inc=True)
                        v_mx = pl.op("V", lambda e, w_=w: e.reduce_max(
                            col[:, 4:5], sc_sb[:, 0:w_], axis=AX.X, negate=True), inc=True)
                        pl.wait("S", "V", v_mx)
                        pl.wait("S", "V", v_pn)
                        s_ex = pl.op("S", lambda e, w_=w: e.activation(
                            pe_sb[:, 0:w_], sc_sb[:, 0:w_], AF.Exp, bias=col[:, 4:5]), inc=True)
                        pl.wait("V", "S", s_ex)
                        pl.vred(lambda e, w_=w: e.reduce_sum(col[:, 5:6], pe_sb[:, 0:w_], axis=AX.X))
                        v_rc = pl.op("V", lambda e: e.reciprocal(col[:, 6:7], col[:, 5:6]), inc=True)
                        wh = w // 2
                        tr_done_t = pl.ctr["T"]
                        pl.wait("S", "V", v_rc)
                        pl.wait("S", "T", (s_pn is not None) and tr_done_t)
                        s_pn = pl.op("S", lambda e, w_=wh: e.mul(
                            pn_sb[:, 0:w_], pe_sb[:, 0:w_], col[:, 6:7]), inc=True)
                        pl.wait("V", "T", (v_pn is not None) and tr_done_t)
                        v_pn = pl.op("V", lambda e, w_=wh, wf=w: e.tensor_scalar_mul(
                            pn_sb[:, w_:wf], pe_sb[:, w_:wf], col[:, 6:7]), inc=True)
                        njt = 4 * (ic + 1)
                        for jt in range(njt):
                            slot = jt % 2
                            psl = P1[:, slot * 512:slot * 512 + 128]
                            pl.wait("T", "S", s_pn)
                            pl.wait("T", "V", v_pn)
                            pl.waitp("T", tr_free[slot])
                            t_tr = pl.op("T", lambda e, p=psl, src=pn_sb[:, jt * 128:(jt + 1) * 128]:
                                         e.transpose(p, src, idf[:, :]), inc=True)
                            ceng = "S" if jt % 2 == 0 else "V"
                            pl.wait(ceng, "T", t_tr)
                            dstp = pT[:, jt * 512 + qt * 128: jt * 512 + (qt + 1) * 128]
                            if ceng == "S":
                                cv = pl.op("S", lambda e, d=dstp, p=psl: e.copy(d, p), inc=True)
                            else:
                                cv = pl.op("V", lambda e, d=dstp, p=psl: e.tensor_copy(d, p), inc=True)
                            tr_free[slot] = (ceng, cv)
                    pl.wait("T", "S", pl.lastw.get(("dummy",), 0) or None)
                    pl.wait("T", "S", tr_free[0][1] if tr_free[0] and tr_free[0][0] == "S" else None)
                    pl.wait("T", "S", pl.ctr["S"])
                    pl.wait("T", "V", pl.ctr["V"])
                    pl.waitp("T", omm_free)
                    pl.wait("T", QA, a_v)
                    njt = 4 * (ic + 1)
                    t_omm = None
                    for jt in range(njt):
                        t_omm = pl.op("T", lambda e, o=P1[0:64, 1024:1536],
                                      l=vsb[par][:, jt * 64:(jt + 1) * 64],
                                      r=pT[:, jt * 512:(jt + 1) * 512],
                                      st=(jt == 0), sp=(jt == njt - 1):
                                      e.matmul(o, l, r, start=st, stop=sp), inc=(jt == njt - 1))
                    qk_free[par] = ("T", pl.ctr["T"])
                    v_free[par] = ("T", pl.ctr["T"])
                    pl.wait("S", "T", t_omm)
                    pl.wait("S", QA, a_vb)
                    if h % 2 == 0:
                        dsto = oTpack[0:64, (h // 2) * 2048 + ic * 512:(h // 2) * 2048 + ic * 512 + 512]
                    else:
                        if ic == 0:
                            pl.waitp("S", ost_free)
                        dsto = oTstage[0:64, ic * 512: ic * 512 + 512]
                    s_oev[(h, ic)] = pl.op("S", lambda e, d=dsto, p=P1[0:64, 1024:1536],
                                           bcol=vb64[0:64, h:h + 1]:
                                           e.add(d, p, bcol), inc=True)
                    omm_free = ("S", s_oev[(h, ic)])
                bias_free = ("V", pl.ctr["V"])
                if h % 2 == 1:
                    pl.wait(QA, "S", s_oev[(h, 3)])
                    a_oTh[h] = pl.dma(QA, AR2[64:128, (h // 2) * 2048:(h // 2 + 1) * 2048],
                                      oTstage[0:64, :])
                    ost_free = (QA, a_oTh[h])
            s_pjc_last = [None, None]
            a_pj_last = None
            pThi3 = pT.rearrange("p (a c) -> p a c", c=1024)
            for ocp in range(8):
                BAR()
                par = ocp % 2
                pl.waitp(QA, cvst_free[par])
                a_pw = pl.dma(QA, cvst[par][:, 0:1024], pw[ocp * 128:(ocp + 1) * 128, :])
                if ocp == 0:
                    pl.wait("T", "V", pl.ctr["V"])
                for dhalf in range(16):
                    dc, hl = dhalf // 2, dhalf % 2
                    slot = dhalf % 2
                    psl = P0[0:64, slot * 512:slot * 512 + 128]
                    pl.wait("T", QA, a_pw)
                    if s_pjc_last[slot]:
                        pl.wait("T", "S", s_pjc_last[slot])
                    t_tr = pl.op("T", lambda e, p=psl,
                                 src=cvst[par][:, dc * 128 + hl * 64: dc * 128 + hl * 64 + 64]:
                                 e.transpose(p, src, idf[:, :]), inc=True)
                    pl.wait("S", "T", t_tr)
                    if hl == 0:
                        dstp = pT[0:64, dc * 1024 + ocp * 128: dc * 1024 + (ocp + 1) * 128]
                    else:
                        if a_pj_last:
                            pl.wait("S", QA, a_pj_last)
                        dstp = pTstage[0:64, dc * 128:(dc + 1) * 128]
                    s_pjc_last[slot] = pl.op("S", lambda e, d=dstp, p=psl: e.copy(d, p), inc=True)
                cvst_free[par] = ("T", pl.ctr["T"])
                pl.wait(QA, "S", s_pjc_last[1])
                dhi = pThi3[64:128, :, ocp * 128:(ocp + 1) * 128]
                shi = pTstage[0:64, :].rearrange("p (a c) -> p a c", c=128)
                a_pj_last = pl.dma(QA, dhi, shi)
            if v_sdone_hist:
                pl.wait(QA, "V", v_sdone_hist[-1])
            pl.wait(QA, "S", s_b2)
            a_lnag = pl.dma(QA, bcA[:, 0:1024], RawAP(lnAg.tensor, 0, [[0, 128], [1, 1024]]))
            a_lnab = pl.dma(QA, bcA[:, 1024:2048], RawAP(lnAb.tensor, 0, [[0, 128], [1, 1024]]))
            a_pjb = pl.dma(QA, bcA[:, 2048:3072], RawAP(pb.tensor, 0, [[0, 128], [1, 1024]]))
            ps_free4 = {0: None, 1: None}
            pl.waitp("T", tr_free[0])
            pl.waitp("T", tr_free[1])
            pl.waitp("T", omm_free)
            for tile in range(NT):
                BAR()
                par = tile % 2
                colix = b * 16 + tile
                pl.waitp(QA, xfb_free[par])
                a_x = pl.dma(QA, xfb[par][:, :], x_in[b, tile * 128:(tile + 1) * 128, :])
                ps = P0 if tile % 2 == 0 else P1
                pl.wait("T", QA, a_pj_last)
                pl.wait("T", QA, a_oTh.get(15))
                pl.wait("T", "S", s_pjc_last[1])
                pl.wait("T", "S", s_oev[(14, 3)])
                pl.waitp("T", ps_free4[tile % 2])
                t_pm = None
                for dc in range(8):
                    for ec in range(2):
                        lhs = oTpack[:, dc * 2048 + tile * 128: dc * 2048 + (tile + 1) * 128]
                        rhs = pT[:, dc * 1024 + ec * 512: dc * 1024 + ec * 512 + 512]
                        t_pm = pl.op("T", lambda e, o=ps[:, ec * 512:ec * 512 + 512], l=lhs, r=rhs,
                                     st=(dc == 0), sp=(dc == 7): e.matmul(o, l, r, start=st, stop=sp),
                                     inc=(dc == 7 and ec == 1))
                t_pm_last = t_pm
                pl.wait("V", "T", t_pm)
                pl.wait("V", QA, max(a_x, a_pjb))
                pl.waitp("V", wk_free[0])
                pl.op("V", lambda e, p=ps[:, 0:1024]: e.tensor_add(wk[0][:, :], p, bcA[:, 2048:3072]))
                pl.op("V", lambda e, c=colix, xp=xfb[par]: e.tensor_scalar(
                    wk[1][:, :], xp[:, :], nm_st[:, c:c + 1], rs_st[:, c:c + 1],
                    op0=ALU.add, op1=ALU.mult))
                v_y0 = pl.op("V", lambda e: e.tensor_add(wk[0][:, :], wk[0][:, :], wk[1][:, :]), inc=True)
                xfb_free[par] = ("V", v_y0)
                ps_free4[tile % 2] = ("V", v_y0)
                pl.vred(lambda e: e.reduce_sum(col[:, 7:8], wk[0][:, :], axis=AX.X))
                pl.op("V", lambda e: e.tensor_scalar_mul(col[:, 8:9], col[:, 7:8], -1.0 / D))
                pl.op("V", lambda e: e.tensor_scalar_add(wk[1][:, :], wk[0][:, :], col[:, 8:9]))
                pl.op("V", lambda e: e.tensor_mul(wk[2][:, :], wk[1][:, :], wk[1][:, :]))
                v_s2 = pl.op("V", lambda e: e.reduce_sum(col[:, 9:10], wk[2][:, :], axis=AX.X), inc=True)
                pl.wait("S", "V", v_s2)
                s_q2 = pl.op("S", lambda e: e.activation(
                    col[:, 10:11], col[:, 9:10], AF.Sqrt, bias=epss[:, 0:1], scale=1.0 / D), inc=True)
                pl.wait("V", "S", s_q2)
                pl.op("V", lambda e: e.reciprocal(col[:, 11:12], col[:, 10:11]))
                pl.op("V", lambda e: e.tensor_scalar_mul(wk[1][:, :], wk[1][:, :], col[:, 11:12]))
                pl.op("V", lambda e: e.tensor_mul(wk[1][:, :], wk[1][:, :], bcA[:, 0:1024]))
                pl.waitp("V", sfb_free[par])
                v_sd = pl.op("V", lambda e, d=sfb[par][:, :]: e.tensor_add(
                    d, wk[1][:, :], bcA[:, 1024:2048]), inc=True)
                wk_free[0] = ("V", v_sd)
                pl.wait(QA, "V", v_sd)
                a_ss = pl.dma(QA, s_sc[b, tile], sfb[par][:, :])
                s_ss_cnt[(b, tile)] = a_ss
                sfb_free[par] = (QA, a_ss)
                v_sdone_hist.append(v_sd)

        emit_conv(10 ** 9)
        BAR()
        conv_all_store = pl.ctr[QA]
        conv_tail_loads = max(x for x in conv["load"] if x) if any(conv["load"]) else 0
        conv_tail_casts = [conv["cast"][0], conv["cast"][1]]

        pl.wait(QA, "V", v_sdone_hist[-1])
        a_lnfg = pl.dma(QA, bcA[:, 0:1024], RawAP(lnFg.tensor, 0, [[0, 128], [1, 1024]]))
        a_lnfb = pl.dma(QA, bcA[:, 1024:2048], RawAP(lnFb.tensor, 0, [[0, 128], [1, 1024]]))
        a_sb2 = pl.dma(QA, bcA[:, 2048:3072], RawAP(sb2.tensor, 0, [[0, 128], [1, 1024]]))
        pl.wait("V", QA, max(a_sb2, a_sigc))
        v_sb2 = pl.op("V", lambda e: e.tensor_scalar_mul(
            bcA[:, 2048:3072], bcA[:, 2048:3072], sigc[:, 0:1]), inc=True)

        wcnt = {"n": 0}
        w1t_free = [None, None, None]
        psh_free = [None, None]
        psy_free = [None, None]
        w2_free = ("T", t_pm_last)
        xnt_free = ("T", t_pm_last)
        h2_free = ("T", pl.ctr["T"])
        yacc_freeV = pl.ctr["V"]
        yacc_freeS = pl.ctr["S"]
        a_out = None
        out_free = [None, None]
        v_out_hist = [None, None]
        for tb in range(NTB):
            BAR()
            b, half = tb // 2, tb % 2
            pl.waitp(QA, xnt_free)
            pl.wait(QA, QA, xnt_store_cnt[b])
            a_xt = None
            for dc in range(8):
                a_xt = pl.dma(QA, xnT_tb[:, dc * 1024:(dc + 1) * 1024],
                              xnT_sc[b, dc, :, half * 1024:half * 1024 + 1024])
            pl.dfence()
            v_comb = {}
            s_ctc = {}
            for ts_ in range(TSUB):
                BAR()
                pl.wait("T", QA, a_xt)
                if tb == 0 and ts_ == 0:
                    pl.wait("T", "V", v_sdone_hist[-1])
                    pl.wait("T", "V", yacc_freeV)
                pl.waitp("T", v_comb.get(ts_ - 1) and ("V", v_comb[ts_ - 1]))
                t_rt = None
                for dc in range(8):
                    t_rt = pl.op("T", lambda e, o=P0[:, 0:8],
                                 l=xnT_tb[:, dc * 1024 + ts_ * 128: dc * 1024 + (ts_ + 1) * 128],
                                 r=rw_sb[:, dc * 8:(dc + 1) * 8],
                                 st=(dc == 0), sp=(dc == 7): e.matmul(o, l, r, start=st, stop=sp),
                                 inc=(dc == 7))
                pl.wait("V", "T", t_rt)
                pl.wait("V", QA, a_rb)
                pl.op("V", lambda e: e.tensor_add(pr0[:, :], P0[:, 0:8], rb_b[:, :]))
                v_sm = pl.op("V", lambda e: e.reduce_max(
                    col[:, 12:13], pr0[:, :], axis=AX.X, negate=True), inc=True)
                pl.wait("S", "V", v_sm)
                s_e8 = pl.op("S", lambda e: e.activation(
                    pre[:, :], pr0[:, :], AF.Exp, bias=col[:, 12:13]), inc=True)
                pl.wait("V", "S", s_e8)
                pl.vred(lambda e: e.reduce_sum(col[:, 13:14], pre[:, :], axis=AX.X))
                pl.op("V", lambda e: e.reciprocal(col[:, 13:14], col[:, 13:14]))
                pl.op("V", lambda e: e.tensor_scalar_mul(prb[:, :], pre[:, :], col[:, 13:14]))
                pl.vred(lambda e: e.reduce_max(col[:, 14:15], prb[:, :], axis=AX.X))
                pl.op("V", lambda e: e.tensor_scalar(msk8[:, :], prb[:, :], col[:, 14:15],
                                                     None, op0=ALU.is_lt))
                pl.op("V", lambda e: e.tensor_mul(msk8[:, :], msk8[:, :], prb[:, :]))
                pl.vred(lambda e: e.reduce_max(col[:, 15:16], msk8[:, :], axis=AX.X))
                pl.op("V", lambda e: e.tensor_scalar(msk8[:, :], prb[:, :], col[:, 15:16],
                                                     None, op0=ALU.is_ge))
                pl.op("V", lambda e: e.tensor_mul(msk8[:, :], msk8[:, :], prb[:, :]))
                pl.op("V", lambda e: e.tensor_tensor(col[:, 14:15], col[:, 14:15],
                                                     col[:, 15:16], op=ALU.add))
                pl.op("V", lambda e: e.reciprocal(col[:, 14:15], col[:, 14:15]))
                v_comb[ts_] = pl.op("V", lambda e, d=comb_sb[:, ts_ * 8:(ts_ + 1) * 8]:
                                    e.tensor_scalar_mul(d, msk8[:, :], col[:, 14:15]), inc=True)
                pl.wait("T", "V", v_comb[ts_])
                pl.waitp("T", s_ctc.get(ts_ - 1) and ("S", s_ctc[ts_ - 1]))
                t_ct = pl.op("T", lambda e, o=P1[0:8, 512:640],
                             src=comb_sb[:, ts_ * 8:(ts_ + 1) * 8]:
                             e.transpose(o, src, idf[:, :]), inc=True)
                pl.wait("S", "T", t_ct)
                s_ctc[ts_] = pl.op("S", lambda e, d=combT[0:8, ts_ * 128:(ts_ + 1) * 128],
                                   p=P1[0:8, 512:640]: e.copy(d, p), inc=True)
            v_yinit = None
            for ts_ in range(TSUB):
                BAR()
                pl.wait("T", "S", max(s_ctc[ts_], s_b2))
                if ts_ > 0:
                    pl.wait("T", "V", v_yinit)
                elif tb == 0:
                    pl.wait("T", "V", yacc_freeV)
                    pl.wait("T", "S", yacc_freeS)
                t_b2 = None
                for dh in range(2):
                    t_b2 = pl.op("T", lambda e, o=P1[:, dh * 512:dh * 512 + 512],
                                 l=combT[0:8, ts_ * 128:(ts_ + 1) * 128],
                                 r=b2_bf[0:8, dh * 512:dh * 512 + 512]:
                                 e.matmul(o, l, r, start=True, stop=True), inc=(dh == 1))
                pl.wait("V", "T", t_b2)
                pl.wait("V", "V", 0 if v_sb2 is None else v_sb2)
                if tb == 0 and ts_ >= 6:
                    pl.wait("V", QA, conv_tail_loads)
                    for cc in conv_tail_casts:
                        pl.waitp("V", cc)
                v_yinit = pl.op("V", lambda e, d=y_acc[:, ts_ * 1024:(ts_ + 1) * 1024]:
                                e.tensor_add(d, P1[:, 0:1024], bcA[:, 2048:3072]), inc=True)
            v_yinit_all = v_yinit
            s_h2 = {}
            v_yev_last = {}
            for e_ in range(9):
                Fh = 8 if e_ == 8 else 16
                shared = (e_ == 8)
                for fh in range(2):
                    a_w2h = None
                    w2q = 0
                    for fc in range(Fh):
                        fcg = fh * Fh + fc
                        slot = wcnt["n"] % 3
                        pl.waitp(QW, w1t_free[slot])
                        if wcnt["n"] == 0:
                            pl.wait(QW, QA, conv_tail_loads)
                            for cc in conv_tail_casts:
                                pl.waitp(QW, cc)
                            pl.wait(QW, QA, pl.ctr[QA])
                        if shared:
                            srcw = RawAP(sw1b.tensor, fcg * 128, [[F2, 128], [128 * F2, 8], [1, 128]])
                        else:
                            srcw = RawAP(w1b.tensor, e_ * D * F + fcg * 128,
                                         [[F, 128], [128 * F, 8], [1, 128]])
                        a_w1 = pl.dma(QW, w1t[slot][:, :], srcw)
                        wcnt["n"] += 1
                        psh = P0[:, (fc % 2) * 1024:(fc % 2) * 1024 + 1024]
                        pl.wait("T", QW, a_w1)
                        if tb == 0 and e_ == 0 and fh == 0 and fc < 2:
                            pl.wait("T", "V", v_comb[TSUB - 1])
                        pl.waitp("T", psh_free[fc % 2])
                        if fc == 0 and fh == 0:
                            pl.waitp("T", w2_free)
                        t_h = None
                        for t5 in range(2):
                            for dc in range(8):
                                t_h = pl.op("T", lambda e, o=psh[:, t5 * 512:t5 * 512 + 512],
                                            l=w1t[slot][:, dc * 128:(dc + 1) * 128],
                                            r=xnT_tb[:, dc * 1024 + t5 * 512: dc * 1024 + t5 * 512 + 512],
                                            st=(dc == 0), sp=(dc == 7):
                                            e.matmul(o, l, r, start=st, stop=sp),
                                            inc=(dc == 7 and t5 == 1))
                        w1t_free[slot] = ("T", t_h)
                        pl.wait("S", "T", t_h)
                        pl.wait("S", "V", v_bgs if shared else v_bg)
                        if shared:
                            bcol, gcol = bgs[:, fcg:fcg + 1], gts[:, fcg:fcg + 1]
                        else:
                            bcol = bg_all[:, e_ * 32 + fcg:e_ * 32 + fcg + 1]
                            gcol = gt_all[:, e_ * 32 + fcg:e_ * 32 + fcg + 1]
                        s_h2[fc] = pl.op(
                            "S", lambda e, d=h2h[:, fc * 1024:(fc + 1) * 1024], p=psh,
                            bc=bcol, gc=gcol:
                            e.activation(d, p, AF.Gelu, bias=bc, scale=gc), inc=True)
                        psh_free[fc % 2] = ("S", s_h2[fc])
                        nw2 = Fh // 2
                        if fc < nw2:
                            for sub in range(2):
                                fcl = fc * 2 + sub
                                if tb == 0 and e_ == 0 and fh == 0:
                                    pl.wait(QA, QA, conv_all_store)
                                pl.waitp(QA, w2_free)
                                if shared:
                                    src2 = RawAP(sw2b.tensor, (fh * Fh + fcl) * 128 * D,
                                                 [[D, 128], [1, D]])
                                else:
                                    src2 = RawAP(w2b.tensor, e_ * F * D + (fh * Fh + fcl) * 128 * D,
                                                 [[D, 128], [1, D]])
                                a_w2h = pl.dma(QA, w2h[:, fcl * 1024:(fcl + 1) * 1024], src2)
                    t_y = None
                    for ts_ in range(TSUB):
                        psy = P1[:, (ts_ % 2) * 1024:(ts_ % 2) * 1024 + 1024]
                        pl.wait("T", "S", s_h2[Fh - 1])
                        pl.wait("T", QA, a_w2h)
                        if tb == 0 and e_ == 0 and fh == 0 and ts_ < 2:
                            pl.wait("T", "V", v_yinit_all)
                        pl.waitp("T", psy_free[ts_ % 2])
                        for dh in range(2):
                            for fc in range(Fh):
                                t_y = pl.op("T", lambda e, o=psy[:, dh * 512:dh * 512 + 512],
                                            l=h2h[:, fc * 1024 + ts_ * 128: fc * 1024 + (ts_ + 1) * 128],
                                            r=w2h[:, fc * 1024 + dh * 512: fc * 1024 + dh * 512 + 512],
                                            st=(fc == 0), sp=(fc == Fh - 1):
                                            e.matmul(o, l, r, start=st, stop=sp),
                                            inc=(fc == Fh - 1 and dh == 1))
                        ccol = sigc[:, 0:1] if shared else comb_sb[:, ts_ * 8 + e_: ts_ * 8 + e_ + 1]
                        wkb = wk[ts_ % 2]
                        pl.wait("S", "T", t_y)
                        pl.waitp("S", wk_free[ts_ % 2])
                        s_yt = pl.op("S", lambda e, d=wkb[:, :], p=psy, c=ccol:
                                     e.mul(d, p, c), inc=True)
                        pl.wait("V", "S", s_yt)
                        v_ye = pl.op("V", lambda e, d=y_acc[:, ts_ * 1024:(ts_ + 1) * 1024],
                                     w_=wkb: e.tensor_add(d, d, w_[:, :]), inc=True)
                        wk_free[ts_ % 2] = ("V", v_ye)
                        psy_free[ts_ % 2] = ("V", v_ye)
                        v_yev_last[ts_] = v_ye
                    w2_free = ("T", t_y)
            for ts_ in range(TSUB):
                BAR()
                par = ts_ % 2
                tile = half * 8 + ts_
                pl.waitp(QA, xfb_free[par])
                a_x = pl.dma(QA, xfb[par][:, :], x_in[b, tile * 128:(tile + 1) * 128, :])
                pl.waitp(QA, sfb_free[par])
                pl.wait(QA, QA, s_ss_cnt[(b, tile)])
                a_s = pl.dma(QA, sfb[par][:, :], s_sc[b, tile])
                ysl = y_acc[:, ts_ * 1024:(ts_ + 1) * 1024]
                pl.wait("V", "V", v_yev_last[ts_])
                pl.waitp("V", wk_free[2])
                pl.waitp("V", wk_free[0])
                pl.vred(lambda e, y=ysl: e.reduce_sum(col[:, 7:8], y, axis=AX.X))
                pl.op("V", lambda e: e.tensor_scalar_mul(col[:, 8:9], col[:, 7:8], -1.0 / D))
                pl.op("V", lambda e, y=ysl: e.tensor_scalar_add(wk[2][:, :], y, col[:, 8:9]))
                pl.op("V", lambda e: e.tensor_mul(wk[0][:, :], wk[2][:, :], wk[2][:, :]))
                v_ss = pl.op("V", lambda e: e.reduce_sum(col[:, 9:10], wk[0][:, :], axis=AX.X), inc=True)
                pl.wait("S", "V", v_ss)
                s_sq = pl.op("S", lambda e: e.activation(
                    col[:, 10:11], col[:, 9:10], AF.Sqrt, bias=epss[:, 0:1], scale=1.0 / D), inc=True)
                pl.wait("V", "S", s_sq)
                pl.op("V", lambda e: e.reciprocal(col[:, 11:12], col[:, 10:11]))
                pl.op("V", lambda e: e.tensor_scalar_mul(wk[2][:, :], wk[2][:, :], col[:, 11:12]))
                pl.op("V", lambda e: e.tensor_mul(wk[2][:, :], wk[2][:, :], bcA[:, 0:1024]))
                v_t2 = pl.op("V", lambda e: e.tensor_add(wk[0][:, :], wk[2][:, :],
                                                         bcA[:, 1024:2048]), inc=True)
                pl.wait("S", "V", v_t2)
                s_g = pl.op("S", lambda e: e.activation(wk[2][:, :], wk[0][:, :], AF.Silu), inc=True)
                pl.wait("V", "S", s_g)
                pl.wait("V", QA, max(a_s, a_x))
                pl.op("V", lambda e, sp=sfb[par]: e.tensor_mul(wk[0][:, :], wk[2][:, :], sp[:, :]))
                v_out = pl.op("V", lambda e, xp=xfb[par]: e.tensor_add(
                    wk[0][:, :], wk[0][:, :], xp[:, :]), inc=True)
                xfb_free[par] = ("V", v_out)
                sfb_free[par] = ("V", v_out)
                wk_free[2] = ("V", v_out)
                pl.wait(QA, "V", v_out)
                a_out = pl.dma(QA, out_d[b, tile * 128:(tile + 1) * 128, :], wk[0][:, :])
                wk_free[0] = (QA, a_out)
        BAR()
        pl.wait(QA, QA, pl.ctr[QA])
        pl.wait(QW, QA, pl.ctr[QA])

        for oplist in pl.ops.values():
            for f in oplist:
                if hasattr(f, "semname"):
                    f.sem = sems[f.semname]
        ctx.enter_context(nc.allow_non_contiguous_dma(
            reason="small one-time column-layout constant loads"))

        @block.sync
        def _(e):
            for f in pl.ops["W"]:
                f(e)

        @block.gpsimd
        def _(e):
            for f in pl.ops["A"]:
                f(e)

        @block.tensor
        def _(e):
            for f in pl.ops["T"]:
                f(e)

        @block.scalar
        def _(e):
            for f in pl.ops["S"]:
                f(e)

        @block.vector
        def _(e):
            for f in pl.ops["V"]:
                f(e)

    return nc


_NC_CACHE = None


def kernel(**inputs):
    global _NC_CACHE
    x = np.ascontiguousarray(np.asarray(inputs["L0_attn_x"], dtype=np.float32))
    rpb = np.asarray(inputs["L0_attn_rpb"], dtype=np.float32)

    p = np.arange(128)[:, None]
    c = np.arange(2432)[None, :]
    idx = np.clip(c + 127 - p, 0, 4094)
    biasz = rpb[idx, :]
    biasz = np.transpose(biasz, (2, 0, 1)).copy()
    biasz[:, (c > p + 1920)] = -1e30
    biasz = np.ascontiguousarray(biasz.astype(np.float32))

    identf = np.eye(128, dtype=np.float32)
    epsc = np.array([1e-5, 0.0, 0.0, 0.0], dtype=np.float32)

    if _NC_CACHE is None:
        _NC_CACHE = build_program()
    nc = _NC_CACHE

    wnames = ["L0_attn_W_w", "L0_attn_W_b", "L0_attn_proj_w", "L0_attn_proj_b",
              "L0_attn_ln_g", "L0_attn_ln_b", "L0_ffn_router_w", "L0_ffn_router_b",
              "L0_ffn_w1", "L0_ffn_b1", "L0_ffn_gate", "L0_ffn_w2", "L0_ffn_b2",
              "L0_ffn_sw1", "L0_ffn_sb1", "L0_ffn_sgate", "L0_ffn_sw2", "L0_ffn_sb2",
              "L0_ffn_ln_g", "L0_ffn_ln_b"]
    base = {k: np.ascontiguousarray(np.asarray(inputs[k], dtype=np.float32)) for k in wnames}
    base["L0_ffn_shared_weight"] = np.asarray(inputs["L0_ffn_shared_weight"],
                                              dtype=np.float32).reshape(1)
    base["biasz"] = biasz
    base["identf"] = identf
    base["epsc"] = epsc

    in_maps = []
    for ci in range(8):
        m = dict(base)
        m["x"] = np.ascontiguousarray(x[ci * 2:(ci + 1) * 2])
        in_maps.append(m)

    res = run_bass_kernel_spmd(nc, in_maps, core_ids=list(range(8)))
    return np.concatenate([r["out"] for r in res.results], axis=0)



# version 05
# speedup vs baseline: 1.4259x; 1.0050x over previous
import numpy as np
from contextlib import ExitStack

import concourse.bass as bass
import concourse.mybir as mybir
from concourse.bass_utils import run_bass_kernel_spmd
from concourse.ap import AP as RawAP

F32 = mybir.dt.float32
BF16 = mybir.dt.bfloat16
AF = mybir.ActivationFunctionType
ALU = mybir.AluOpType
AX = mybir.AxisListType

BS = 2
S = 2048
D = 1024
H = 16
E = 8
F = 4096
F2 = 2048
NT = 16
NTB = 4
TSUB = 8


class Planner:
    def __init__(self):
        self.ops = {k: [] for k in ("W", "A", "T", "S", "V")}
        self.ctr = {k: 0 for k in ("W", "A", "T", "S", "V", "F")}
        self.lastw = {}
        self.vspacer = None

    def dfence(self):
        self.ctr["F"] += 1

        def f(e):
            e.drain().then_inc(f.sem, 1)
        f.semname = "F"
        self.ops["A"].append(f)
        return self.ctr["F"]

    def dma(self, q, out, in_):
        self.ctr[q] += 16

        def f(e, out=out, in_=in_):
            e.dma_start(out=out, in_=in_).then_inc(f.sem, 16)
        f.semname = q
        self.ops[q].append(f)
        return self.ctr[q]

    def op(self, eng, fn, inc=False):
        if eng == "V":
            inc = True
        ret = None
        if inc:
            self.ctr[eng] += 1
            ret = self.ctr[eng]

            def g(e, fn=fn):
                fn(e).then_inc(g.sem, 1)
            g.semname = eng
            self.ops[eng].append(g)
        else:
            self.ops[eng].append(fn)
        if eng == "V":
            self.wait("V", "V", ret)
        return ret

    def vred(self, fn):
        v = self.op("V", fn, inc=True)
        self.wait("V", "V", v)
        return v

    def wait(self, eng, sem, val):
        if not val:
            return
        key = (eng, sem)
        if self.lastw.get(key, 0) >= val:
            return
        self.lastw[key] = val

        def w(e, val=val):
            e.wait_ge(w.sem, val)
        w.semname = sem
        self.ops[eng].append(w)

    def waitp(self, eng, pair):
        if pair:
            self.wait(eng, pair[0], pair[1])


def build_program():
    nc = bass.Bass("TRN2", target_bir_lowering=False)

    def din(name, shape, dt=F32):
        return nc.dram_tensor(name, shape, dt, kind="ExternalInput").ap()

    x_in = din("x", [BS, S, D])
    Ww = din("L0_attn_W_w", [3 * D, D])
    Wb = din("L0_attn_W_b", [3 * D])
    pw = din("L0_attn_proj_w", [D, D])
    pb = din("L0_attn_proj_b", [D])
    lnAg = din("L0_attn_ln_g", [D])
    lnAb = din("L0_attn_ln_b", [D])
    rtw = din("L0_ffn_router_w", [D, E])
    rtb = din("L0_ffn_router_b", [E])
    w1 = din("L0_ffn_w1", [E, D, F])
    b1 = din("L0_ffn_b1", [E, F])
    gt = din("L0_ffn_gate", [E, F])
    w2 = din("L0_ffn_w2", [E, F, D])
    b2 = din("L0_ffn_b2", [E, D])
    sw1 = din("L0_ffn_sw1", [D, F2])
    sb1 = din("L0_ffn_sb1", [F2])
    sgt = din("L0_ffn_sgate", [F2])
    sw2 = din("L0_ffn_sw2", [F2, D])
    sb2 = din("L0_ffn_sb2", [D])
    shw = din("L0_ffn_shared_weight", [1])
    lnFg = din("L0_ffn_ln_g", [D])
    lnFb = din("L0_ffn_ln_b", [D])
    biasz = din("biasz", [H, 128, 2432])
    identf = din("identf", [128, 128])
    epsc = din("epsc", [4])

    out_d = nc.dram_tensor("out", [BS, S, D], F32, kind="ExternalOutput").ap()

    def scr(name, shape, dt):
        return nc.dram_tensor(name, shape, dt, kind="Internal").ap()
    xnT_sc = scr("xnT_sc", [BS, 8, 128, S], BF16)
    qT_sc = scr("qT_sc", [BS, 8, 128, S], BF16)
    kT_sc = scr("kT_sc", [BS, 8, 128, S], BF16)
    v_sc = scr("v_sc", [BS, NT, 128, D], BF16)
    s_sc = scr("s_sc", [BS, NT, 128, D], F32)
    w1b = scr("w1b", [E, D, F], BF16)
    w2b = scr("w2b", [E, F, D], BF16)
    sw1b = scr("sw1b", [D, F2], BF16)
    sw2b = scr("sw2b", [F2, D], BF16)
    sig_rt = scr("sig_rt", [1], F32)

    pl = Planner()
    QW, QA = "W", "A"

    with ExitStack() as ctx:
        sb = lambda n, s, d: ctx.enter_context(nc.sbuf_tensor(n, s, d))
        idf = sb("idf", [128, 128], F32)
        qkb = sb("qkb", [128, 16], F32)
        vb64 = sb("vb64", [64, 16], F32)
        bcA = sb("bcA", [128, 3072], F32)
        rb_b = sb("rb_b", [128, 8], F32)
        rw_f = sb("rw_f", [128, 64], F32)
        rw_sb = sb("rw_sb", [128, 64], BF16)
        b2_bf = sb("b2_bf", [8, 1024], BF16)
        bg_all = sb("bg_all", [128, 256], F32)
        gt_all = sb("gt_all", [128, 256], F32)
        bgs = sb("bgs", [128, 16], F32)
        gts = sb("gts", [128, 16], F32)
        nm_st = sb("nm_st", [128, 32], F32)
        rs_st = sb("rs_st", [128, 32], F32)
        epss = sb("epss", [128, 4], F32)
        sig1 = sb("sig1", [1, 1], F32)
        sigc = sb("sigc", [128, 1], F32)
        AR1 = sb("AR1", [128, 16384], BF16)
        AR2 = sb("AR2", [128, 24576], BF16)
        AR3 = sb("AR3", [128, 8192], F32)
        qtb = [sb("qtb0", [64, 2048], BF16), sb("qtb1", [64, 2048], BF16)]
        ktb = [sb("ktb0", [64, 2048], BF16), sb("ktb1", [64, 2048], BF16)]
        vsb = [sb("vsb0", [128, 1024], BF16), sb("vsb1", [128, 1024], BF16)]
        bias_h = sb("bias_h", [128, 2432], F32)
        oTstage = sb("oTstage", [64, 2048], BF16)
        pTstage = sb("pTstage", [64, 1024], BF16)
        w1t = [sb("w1t0", [128, 1024], BF16), sb("w1t1", [128, 1024], BF16),
               sb("w1t2", [128, 1024], BF16)]
        qkev = [sb("qkev0", [128, 1024], BF16), sb("qkev1", [128, 1024], BF16)]
        xfb = [sb("xfb0", [128, 1024], F32), sb("xfb1", [128, 1024], F32)]
        sfb = [sb("sfb0", [128, 1024], F32), sb("sfb1", [128, 1024], F32)]
        wk = [sb("wk0", [128, 1024], F32), sb("wk1", [128, 1024], F32),
              sb("wk2", [128, 1024], F32)]
        col = sb("colb", [128, 16], F32)
        dum = sb("dum", [1, 1], F32)
        dum2 = sb("dum2", [1, 1], F32)
        pr0 = sb("pr0", [128, 8], F32)
        pre = sb("pre", [128, 8], F32)
        prb = sb("prb", [128, 8], F32)
        msk8 = sb("msk8", [128, 8], F32)
        comb_sb = sb("comb_sb", [128, 64], F32)
        combT = pTstage[0:8, :]
        b2f = bcA[0:8, 0:1024]

        P0 = ctx.enter_context(nc.psum_tensor("P0", [128, 2048], F32))
        P1 = ctx.enter_context(nc.psum_tensor("P1", [128, 2048], F32))

        sems = {k: ctx.enter_context(nc.semaphore("sem" + k))
                for k in ("W", "A", "T", "S", "V", "F")}
        block = ctx.enter_context(nc.Block())

        def BAR():
            pl.op("S", lambda e: e.copy(dum[:, :], dum[:, :]), inc=True)
            pl.op("V", lambda e: e.tensor_copy(dum[:, :], dum[:, :]), inc=True)
            vals = dict(pl.ctr)
            for eng in ("T", "S", "V", "W", "A"):
                for sem_, v_ in vals.items():
                    if sem_ != eng and v_ > 0:
                        pl.wait(eng, sem_, v_)

        xnT_sb = AR1
        h2h = AR1
        WT = AR2
        oTpack = AR2[:, 0:16384]
        pT = AR2[:, 16384:24576]
        w2h = AR2[:, 0:16384]
        xnT_tb = AR2[:, 16384:24576]
        sc_sb = AR3[:, 0:2048]
        pe_sb = AR3[:, 2048:4096]
        pn_sb = AR3[:, 4096:6144]
        cvst = [AR3[:, 6144:7168], AR3[:, 7168:8192]]
        y_acc = AR3

        pl.dma(QA, idf[:, :], identf[:, :])
        a_qkb = pl.dma(QA, qkb[:, :], RawAP(Wb.tensor, 0, [[1, 128], [128, 16]]))
        a_vb = pl.dma(QA, vb64[:, :], RawAP(Wb.tensor, 2048, [[1, 64], [64, 16]]))
        a_rb = pl.dma(QA, rb_b[:, :], RawAP(rtb.tensor, 0, [[0, 128], [1, 8]]))
        a_rw = pl.dma(QA, rw_f[:, :], RawAP(rtw.tensor, 0, [[8, 128], [1024, 8], [1, 8]]))
        a_b2 = pl.dma(QA, b2f[:, :], b2[:, :])
        for e_ in range(E):
            a_bg = pl.dma(QA, bg_all[:, e_ * 32:(e_ + 1) * 32],
                          RawAP(b1.tensor, e_ * 4096, [[1, 128], [128, 32]]))
            a_gt = pl.dma(QA, gt_all[:, e_ * 32:(e_ + 1) * 32],
                          RawAP(gt.tensor, e_ * 4096, [[1, 128], [128, 32]]))
        a_bgs = pl.dma(QA, bgs[:, :], RawAP(sb1.tensor, 0, [[1, 128], [128, 16]]))
        a_gts = pl.dma(QA, gts[:, :], RawAP(sgt.tensor, 0, [[1, 128], [128, 16]]))
        a_eps = pl.dma(QA, epss[:, :], RawAP(epsc.tensor, 0, [[0, 128], [1, 4]]))
        a_sh = pl.dma(QA, sig1[:, :], shw[None, :])
        F1 = pl.dfence()
        pl.wait("S", "F", F1)
        pl.wait("V", "F", F1)
        pl.wait("S", QA, a_rw)
        pl.op("S", lambda e: e.copy(rw_sb[:, :], rw_f[:, :]))
        pl.wait("S", QA, a_b2)
        s_b2 = pl.op("S", lambda e: e.copy(b2_bf[:, :], b2f[:, :]), inc=True)
        pl.wait("S", QA, a_sh)
        s_sig = pl.op("S", lambda e: e.activation(sig1[:, :], sig1[:, :], AF.Sigmoid), inc=True)
        pl.wait(QA, "S", s_sig)
        a_sr = pl.dma(QA, sig_rt[None, :], sig1[:, :])
        pl.wait(QA, QA, a_sr)
        pl.dfence()
        a_sigc = pl.dma(QA, sigc[:, :], RawAP(sig_rt.tensor, 0, [[0, 128], [1, 1]]))
        pl.dfence()
        pl.wait("V", QA, max(a_bg, a_gt))
        v_bg = pl.op("V", lambda e: e.tensor_mul(bg_all[:, :], bg_all[:, :], gt_all[:, :]), inc=True)
        pl.wait("V", QA, max(a_bgs, a_gts))
        v_bgs = pl.op("V", lambda e: e.tensor_mul(bgs[:, :], bgs[:, :], gts[:, :]), inc=True)

        cvst_free = [None, None]
        qkev_free = [None, None]
        xfb_free = [None, None]
        sfb_free = [None, None]
        wk_free = [None] * 3

        s_wt = None
        for oc in range(24):
            par = oc % 2
            pl.waitp(QA, cvst_free[par])
            a_w = pl.dma(QA, cvst[par][:, 0:1024], Ww[oc * 128:(oc + 1) * 128, :])
            for dc in range(8):
                slot = dc % 2
                psl = P0[:, slot * 512:slot * 512 + 128]
                pl.wait("T", QA, a_w)
                if dc >= 2 or oc > 0:
                    pl.wait("T", "S", (s_wt or 0) - 1 if dc >= 2 else s_wt)
                t_wt = pl.op("T", lambda e, p=psl, src=cvst[par][:, dc * 128:(dc + 1) * 128]:
                             e.transpose(p, src, idf[:, :]), inc=True)
                dst = WT[:, dc * 3072 + oc * 128: dc * 3072 + (oc + 1) * 128]
                pl.wait("S", "T", t_wt)
                s_wt = pl.op("S", lambda e, d=dst, p=psl: e.copy(d, p), inc=True)
            cvst_free[par] = ("T", pl.ctr["T"])

        s_xnT_last = {}
        last_t_tr = None
        last_v_xn = None
        xnt_store_cnt = {}
        qkv_store_cnt = {}
        for b in range(BS):
            for tile in range(NT):
                BAR()
                colix = b * 16 + tile
                par = tile % 2
                pl.waitp(QA, xfb_free[par])
                a_x = pl.dma(QA, xfb[par][:, :], x_in[b, tile * 128:(tile + 1) * 128, :])
                pl.wait("V", QA, a_x)
                pl.wait("V", QA, a_eps)
                pl.vred(lambda e, xp=xfb[par]: e.reduce_sum(col[:, 0:1], xp[:, :], axis=AX.X))
                v_nm = pl.op("V", lambda e, c=colix: e.tensor_scalar_mul(
                    nm_st[:, c:c + 1], col[:, 0:1], -1.0 / D), inc=True)
                pl.wait("S", "V", v_nm)
                s_xm = pl.op("S", lambda e, c=colix, xp=xfb[par]: e.add(
                    wk[0][:, :], xp[:, :], nm_st[:, c:c + 1]), inc=True)
                xfb_free[par] = ("S", s_xm)
                pl.wait("V", "S", s_xm)
                pl.op("V", lambda e: e.tensor_mul(wk[1][:, :], wk[0][:, :], wk[0][:, :]))
                v_sq = pl.op("V", lambda e: e.reduce_sum(col[:, 1:2], wk[1][:, :], axis=AX.X), inc=True)
                pl.wait("S", "V", v_sq)
                s_si = pl.op("S", lambda e: e.activation(
                    col[:, 2:3], col[:, 1:2], AF.Sqrt, bias=epss[:, 0:1], scale=1.0 / D), inc=True)
                pl.wait("V", "S", s_si)
                pl.op("V", lambda e, c=colix: e.reciprocal(rs_st[:, c:c + 1], col[:, 2:3]))
                pl.wait("V", "T", last_t_tr)
                last_v_xn = pl.op("V", lambda e, c=colix: e.tensor_scalar_mul(
                    wk[2][:, :], wk[0][:, :], rs_st[:, c:c + 1]), inc=True)
                for dc in range(8):
                    slot = dc % 2
                    psl = P0[:, slot * 512:slot * 512 + 128]
                    pl.wait("T", "V", last_v_xn)
                    pl.wait("T", "S", s_xnT_last.get((slot,)))
                    last_t_tr = pl.op("T", lambda e, p=psl, src=wk[2][:, dc * 128:(dc + 1) * 128]:
                                      e.transpose(p, src, idf[:, :]), inc=True)
                    dst = xnT_sb[:, dc * 2048 + tile * 128: dc * 2048 + (tile + 1) * 128]
                    pl.wait("S", "T", last_t_tr)
                    s_xnT_last[(slot,)] = pl.op("S", lambda e, d=dst, p=psl: e.copy(d, p), inc=True)
            s_xnT_all = pl.ctr["S"]
            pl.wait(QA, "S", s_xnT_all)
            for dc in range(8):
                xnt_store_cnt[b] = pl.dma(QA, xnT_sc[b, dc], xnT_sb[:, dc * 2048:(dc + 1) * 2048])

            ps_free = {0: None, 1: None}
            pl.wait("T", "S", s_xnT_all)
            for oc in range(16):
                ps = P0 if oc % 2 == 0 else P1
                pl.waitp("T", ps_free[oc % 2])
                t_mm = None
                for dc in range(8):
                    for tc in range(4):
                        lhs = WT[:, dc * 3072 + oc * 128: dc * 3072 + (oc + 1) * 128]
                        rhs = xnT_sb[:, dc * 2048 + tc * 512: dc * 2048 + tc * 512 + 512]
                        t_mm = pl.op("T", lambda e, o=ps[:, tc * 512:tc * 512 + 512], l=lhs, r=rhs,
                                     st=(dc == 0), sp=(dc == 7): e.matmul(o, l, r, start=st, stop=sp),
                                     inc=(dc == 7 and tc == 3))
                dst_sc = qT_sc if oc < 8 else kT_sc
                occ = oc % 8
                sev = None
                for hh in range(2):
                    pl.wait("S", "T", t_mm)
                    pl.waitp("S", qkev_free[hh])
                    sev = pl.op("S", lambda e, d=qkev[hh][:, :],
                                p=ps[:, hh * 1024:hh * 1024 + 1024],
                                bcol=qkb[:, oc:oc + 1]:
                                e.add(d, p, bcol), inc=True)
                    pl.wait(QA, "S", sev)
                    adma = pl.dma(QA, dst_sc[b, occ, :, hh * 1024:hh * 1024 + 1024], qkev[hh][:, :])
                    qkev_free[hh] = (QA, adma)
                ps_free[oc % 2] = ("S", sev)
            for tile in range(NT):
                ps = P0 if tile % 2 == 0 else P1
                pl.waitp("T", ps_free[tile % 2])
                for dc in range(8):
                    for oc2 in range(2):
                        lhs = xnT_sb[:, dc * 2048 + tile * 128: dc * 2048 + (tile + 1) * 128]
                        rhs = WT[:, dc * 3072 + 2048 + oc2 * 512: dc * 3072 + 2048 + oc2 * 512 + 512]
                        t_mm = pl.op("T", lambda e, o=ps[:, oc2 * 512:oc2 * 512 + 512], l=lhs, r=rhs,
                                     st=(dc == 0), sp=(dc == 7): e.matmul(o, l, r, start=st, stop=sp),
                                     inc=(dc == 7 and oc2 == 1))
                hh = tile % 2
                pl.wait("S", "T", t_mm)
                pl.waitp("S", qkev_free[hh])
                sev = pl.op("S", lambda e, d=qkev[hh][:, :], p=ps[:, 0:1024]: e.copy(d, p), inc=True)
                pl.wait(QA, "S", sev)
                adma = pl.dma(QA, v_sc[b, tile], qkev[hh][:, :])
                qkev_free[hh] = (QA, adma)
                ps_free[tile % 2] = ("S", sev)
            qkv_store_cnt[b] = pl.ctr[QA]

        conv_jobs = []
        for e_ in range(E):
            for dc in range(8):
                for q4 in range(4):
                    off = e_ * D * F + dc * 128 * F + q4 * 1024
                    conv_jobs.append((RawAP(w1.tensor, off, [[F, 128], [1, 1024]]),
                                      RawAP(w1b.tensor, off, [[F, 128], [1, 1024]])))
        for e_ in range(E):
            for fc in range(32):
                off = e_ * F * D + fc * 128 * D
                conv_jobs.append((RawAP(w2.tensor, off, [[D, 128], [1, 1024]]),
                                  RawAP(w2b.tensor, off, [[D, 128], [1, 1024]])))
        for dc in range(8):
            for hh in range(2):
                off = dc * 128 * F2 + hh * 1024
                conv_jobs.append((RawAP(sw1.tensor, off, [[F2, 128], [1, 1024]]),
                                  RawAP(sw1b.tensor, off, [[F2, 128], [1, 1024]])))
        for fc in range(16):
            off = fc * 128 * D
            conv_jobs.append((RawAP(sw2.tensor, off, [[D, 128], [1, 1024]]),
                              RawAP(sw2b.tensor, off, [[D, 128], [1, 1024]])))
        conv = {"i": 0, "load": [None, None], "cast": [None, None]}

        def emit_conv(njobs):
            for _ in range(njobs):
                i = conv["i"]
                if i >= len(conv_jobs):
                    return
                conv["i"] = i + 1
                par = i % 2
                eng = "S" if par == 0 else "V"
                src, dst = conv_jobs[i]
                pl.waitp(QA, cvst_free[par])
                a_l = pl.dma(QA, cvst[par][:, 0:1024], src)
                conv["load"][par] = a_l
                pl.wait(eng, QA, a_l)
                pl.waitp(eng, qkev_free[par])
                if eng == "S":
                    cinc = pl.op("S", lambda e, d=qkev[par][:, :], s_=cvst[par][:, 0:1024]:
                                 e.copy(d, s_), inc=True)
                else:
                    cinc = pl.op("V", lambda e, d=qkev[par][:, :], s_=cvst[par][:, 0:1024]:
                                 e.tensor_copy(d, s_), inc=True)
                cvst_free[par] = (eng, cinc)
                conv["cast"][par] = (eng, cinc)
                pl.wait(QA, eng, cinc)
                a_st = pl.dma(QA, dst, qkev[par][:, :])
                qkev_free[par] = (QA, a_st)

        qk_free = [None, None]
        v_free = [None, None]
        bias_free = None
        stt_last = [None] * 4
        tr_free = [None, None]
        omm_free = None
        v_sdone_hist = []
        s_ss_cnt = {}
        t_pm_last = None
        for b in range(BS):
            s_oev = {}
            a_oTh = {}
            ost_free = None
            pl.wait(QA, QA, qkv_store_cnt[b])
            for h in range(H):
                par = h % 2
                pl.waitp(QA, qk_free[par])
                a_q = pl.dma(QA, qtb[par][:, :], qT_sc[b, h // 2, (h % 2) * 64:(h % 2) * 64 + 64, :])
                a_k = pl.dma(QA, ktb[par][:, :], kT_sc[b, h // 2, (h % 2) * 64:(h % 2) * 64 + 64, :])
                pl.waitp(QA, v_free[par])
                a_v = None
                for vq in range(4):
                    srcv = RawAP(v_sc.tensor, (b * NT + vq * 4) * 128 * D + h * 64,
                                 [[D, 128], [128 * D, 4], [1, 64]])
                    a_v = pl.dma(QA, vsb[par][:, vq * 256:(vq + 1) * 256], srcv)
                pl.waitp(QA, bias_free)
                a_bi = pl.dma(QA, bias_h[:, :], biasz[h, :, :])
                emit_conv(9)
                s_pn = v_pn = None
                for ic in range(4):
                    w = (ic + 1) * 512
                    for qt in range(4):
                        i0 = ic * 512 + qt * 128
                        boff = 1920 - i0
                        pl.wait("T", QA, a_k)
                        for jb in range(ic + 1):
                            pl.waitp("T", ("V", stt_last[jb]) if stt_last[jb] else None)
                            t_mm = pl.op("T", lambda e, o=P0[:, jb * 512:jb * 512 + 512],
                                         l=qtb[par][:, i0:i0 + 128],
                                         r=ktb[par][:, jb * 512:jb * 512 + 512]:
                                         e.matmul(o, l, r, start=True, stop=True), inc=True)
                            pl.wait("V", "T", t_mm)
                            pl.wait("V", QA, a_bi)
                            stt_last[jb] = pl.op(
                                "V", lambda e, d=sc_sb[:, jb * 512:jb * 512 + 512],
                                p=P0[:, jb * 512:jb * 512 + 512],
                                bsl=bias_h[:, boff + jb * 512: boff + jb * 512 + 512]:
                                e.scalar_tensor_tensor(d, p, 1.0 / 32.0, bsl,
                                                       op0=ALU.mult, op1=ALU.add), inc=True)
                        v_mx = pl.op("V", lambda e, w_=w: e.reduce_max(
                            col[:, 4:5], sc_sb[:, 0:w_], axis=AX.X, negate=True), inc=True)
                        pl.wait("S", "V", v_mx)
                        pl.wait("S", "V", v_pn)
                        s_ex = pl.op("S", lambda e, w_=w: e.activation(
                            pe_sb[:, 0:w_], sc_sb[:, 0:w_], AF.Exp, bias=col[:, 4:5]), inc=True)
                        pl.wait("V", "S", s_ex)
                        pl.vred(lambda e, w_=w: e.reduce_sum(col[:, 5:6], pe_sb[:, 0:w_], axis=AX.X))
                        v_rc = pl.op("V", lambda e: e.reciprocal(col[:, 6:7], col[:, 5:6]), inc=True)
                        wh = w // 2
                        tr_done_t = pl.ctr["T"]
                        pl.wait("S", "V", v_rc)
                        pl.wait("S", "T", (s_pn is not None) and tr_done_t)
                        s_pn = pl.op("S", lambda e, w_=wh: e.mul(
                            pn_sb[:, 0:w_], pe_sb[:, 0:w_], col[:, 6:7]), inc=True)
                        pl.wait("V", "T", (v_pn is not None) and tr_done_t)
                        v_pn = pl.op("V", lambda e, w_=wh, wf=w: e.tensor_scalar_mul(
                            pn_sb[:, w_:wf], pe_sb[:, w_:wf], col[:, 6:7]), inc=True)
                        njt = 4 * (ic + 1)
                        for jt in range(njt):
                            slot = jt % 2
                            psl = P1[:, slot * 512:slot * 512 + 128]
                            pl.wait("T", "S", s_pn)
                            pl.wait("T", "V", v_pn)
                            pl.waitp("T", tr_free[slot])
                            t_tr = pl.op("T", lambda e, p=psl, src=pn_sb[:, jt * 128:(jt + 1) * 128]:
                                         e.transpose(p, src, idf[:, :]), inc=True)
                            ceng = "S" if jt % 2 == 0 else "V"
                            pl.wait(ceng, "T", t_tr)
                            dstp = pT[:, jt * 512 + qt * 128: jt * 512 + (qt + 1) * 128]
                            if ceng == "S":
                                cv = pl.op("S", lambda e, d=dstp, p=psl: e.copy(d, p), inc=True)
                            else:
                                cv = pl.op("V", lambda e, d=dstp, p=psl: e.tensor_copy(d, p), inc=True)
                            tr_free[slot] = (ceng, cv)
                    pl.wait("T", "S", pl.lastw.get(("dummy",), 0) or None)
                    pl.wait("T", "S", tr_free[0][1] if tr_free[0] and tr_free[0][0] == "S" else None)
                    pl.wait("T", "S", pl.ctr["S"])
                    pl.wait("T", "V", pl.ctr["V"])
                    pl.waitp("T", omm_free)
                    pl.wait("T", QA, a_v)
                    njt = 4 * (ic + 1)
                    t_omm = None
                    for jt in range(njt):
                        t_omm = pl.op("T", lambda e, o=P1[0:64, 1024:1536],
                                      l=vsb[par][:, jt * 64:(jt + 1) * 64],
                                      r=pT[:, jt * 512:(jt + 1) * 512],
                                      st=(jt == 0), sp=(jt == njt - 1):
                                      e.matmul(o, l, r, start=st, stop=sp), inc=(jt == njt - 1))
                    qk_free[par] = ("T", pl.ctr["T"])
                    v_free[par] = ("T", pl.ctr["T"])
                    pl.wait("S", "T", t_omm)
                    pl.wait("S", QA, a_vb)
                    if h % 2 == 0:
                        dsto = oTpack[0:64, (h // 2) * 2048 + ic * 512:(h // 2) * 2048 + ic * 512 + 512]
                    else:
                        if ic == 0:
                            pl.waitp("S", ost_free)
                        dsto = oTstage[0:64, ic * 512: ic * 512 + 512]
                    s_oev[(h, ic)] = pl.op("S", lambda e, d=dsto, p=P1[0:64, 1024:1536],
                                           bcol=vb64[0:64, h:h + 1]:
                                           e.add(d, p, bcol), inc=True)
                    omm_free = ("S", s_oev[(h, ic)])
                bias_free = ("V", pl.ctr["V"])
                if h % 2 == 1:
                    pl.wait(QA, "S", s_oev[(h, 3)])
                    a_oTh[h] = pl.dma(QA, AR2[64:128, (h // 2) * 2048:(h // 2 + 1) * 2048],
                                      oTstage[0:64, :])
                    ost_free = (QA, a_oTh[h])
            s_pjc_last = [None, None]
            a_pj_last = None
            pThi3 = pT.rearrange("p (a c) -> p a c", c=1024)
            for ocp in range(8):
                par = ocp % 2
                pl.waitp(QA, cvst_free[par])
                a_pw = pl.dma(QA, cvst[par][:, 0:1024], pw[ocp * 128:(ocp + 1) * 128, :])
                if ocp == 0:
                    pl.wait("T", "V", pl.ctr["V"])
                for dhalf in range(16):
                    dc, hl = dhalf // 2, dhalf % 2
                    slot = dhalf % 2
                    psl = P0[0:64, slot * 512:slot * 512 + 128]
                    pl.wait("T", QA, a_pw)
                    if s_pjc_last[slot]:
                        pl.wait("T", "S", s_pjc_last[slot])
                    t_tr = pl.op("T", lambda e, p=psl,
                                 src=cvst[par][:, dc * 128 + hl * 64: dc * 128 + hl * 64 + 64]:
                                 e.transpose(p, src, idf[:, :]), inc=True)
                    pl.wait("S", "T", t_tr)
                    if hl == 0:
                        dstp = pT[0:64, dc * 1024 + ocp * 128: dc * 1024 + (ocp + 1) * 128]
                    else:
                        if a_pj_last:
                            pl.wait("S", QA, a_pj_last)
                        dstp = pTstage[0:64, dc * 128:(dc + 1) * 128]
                    s_pjc_last[slot] = pl.op("S", lambda e, d=dstp, p=psl: e.copy(d, p), inc=True)
                cvst_free[par] = ("T", pl.ctr["T"])
                pl.wait(QA, "S", s_pjc_last[1])
                dhi = pThi3[64:128, :, ocp * 128:(ocp + 1) * 128]
                shi = pTstage[0:64, :].rearrange("p (a c) -> p a c", c=128)
                a_pj_last = pl.dma(QA, dhi, shi)
            if v_sdone_hist:
                pl.wait(QA, "V", v_sdone_hist[-1])
            pl.wait(QA, "S", s_b2)
            a_lnag = pl.dma(QA, bcA[:, 0:1024], RawAP(lnAg.tensor, 0, [[0, 128], [1, 1024]]))
            a_lnab = pl.dma(QA, bcA[:, 1024:2048], RawAP(lnAb.tensor, 0, [[0, 128], [1, 1024]]))
            a_pjb = pl.dma(QA, bcA[:, 2048:3072], RawAP(pb.tensor, 0, [[0, 128], [1, 1024]]))
            ps_free4 = {0: None, 1: None}
            pl.waitp("T", tr_free[0])
            pl.waitp("T", tr_free[1])
            pl.waitp("T", omm_free)
            for tile in range(NT):
                par = tile % 2
                colix = b * 16 + tile
                pl.waitp(QA, xfb_free[par])
                a_x = pl.dma(QA, xfb[par][:, :], x_in[b, tile * 128:(tile + 1) * 128, :])
                ps = P0 if tile % 2 == 0 else P1
                pl.wait("T", QA, a_pj_last)
                pl.wait("T", QA, a_oTh.get(15))
                pl.wait("T", "S", s_pjc_last[1])
                pl.wait("T", "S", s_oev[(14, 3)])
                pl.waitp("T", ps_free4[tile % 2])
                t_pm = None
                for dc in range(8):
                    for ec in range(2):
                        lhs = oTpack[:, dc * 2048 + tile * 128: dc * 2048 + (tile + 1) * 128]
                        rhs = pT[:, dc * 1024 + ec * 512: dc * 1024 + ec * 512 + 512]
                        t_pm = pl.op("T", lambda e, o=ps[:, ec * 512:ec * 512 + 512], l=lhs, r=rhs,
                                     st=(dc == 0), sp=(dc == 7): e.matmul(o, l, r, start=st, stop=sp),
                                     inc=(dc == 7 and ec == 1))
                t_pm_last = t_pm
                pl.wait("V", "T", t_pm)
                pl.wait("V", QA, max(a_x, a_pjb))
                pl.waitp("V", wk_free[0])
                pl.op("V", lambda e, p=ps[:, 0:1024]: e.tensor_add(wk[0][:, :], p, bcA[:, 2048:3072]))
                pl.op("V", lambda e, c=colix, xp=xfb[par]: e.tensor_scalar(
                    wk[1][:, :], xp[:, :], nm_st[:, c:c + 1], rs_st[:, c:c + 1],
                    op0=ALU.add, op1=ALU.mult))
                v_y0 = pl.op("V", lambda e: e.tensor_add(wk[0][:, :], wk[0][:, :], wk[1][:, :]), inc=True)
                xfb_free[par] = ("V", v_y0)
                ps_free4[tile % 2] = ("V", v_y0)
                pl.vred(lambda e: e.reduce_sum(col[:, 7:8], wk[0][:, :], axis=AX.X))
                pl.op("V", lambda e: e.tensor_scalar_mul(col[:, 8:9], col[:, 7:8], -1.0 / D))
                pl.op("V", lambda e: e.tensor_scalar_add(wk[1][:, :], wk[0][:, :], col[:, 8:9]))
                pl.op("V", lambda e: e.tensor_mul(wk[2][:, :], wk[1][:, :], wk[1][:, :]))
                v_s2 = pl.op("V", lambda e: e.reduce_sum(col[:, 9:10], wk[2][:, :], axis=AX.X), inc=True)
                pl.wait("S", "V", v_s2)
                s_q2 = pl.op("S", lambda e: e.activation(
                    col[:, 10:11], col[:, 9:10], AF.Sqrt, bias=epss[:, 0:1], scale=1.0 / D), inc=True)
                pl.wait("V", "S", s_q2)
                pl.op("V", lambda e: e.reciprocal(col[:, 11:12], col[:, 10:11]))
                pl.op("V", lambda e: e.tensor_scalar_mul(wk[1][:, :], wk[1][:, :], col[:, 11:12]))
                pl.op("V", lambda e: e.tensor_mul(wk[1][:, :], wk[1][:, :], bcA[:, 0:1024]))
                pl.waitp("V", sfb_free[par])
                v_sd = pl.op("V", lambda e, d=sfb[par][:, :]: e.tensor_add(
                    d, wk[1][:, :], bcA[:, 1024:2048]), inc=True)
                wk_free[0] = ("V", v_sd)
                pl.wait(QA, "V", v_sd)
                a_ss = pl.dma(QA, s_sc[b, tile], sfb[par][:, :])
                s_ss_cnt[(b, tile)] = a_ss
                sfb_free[par] = (QA, a_ss)
                v_sdone_hist.append(v_sd)

        emit_conv(10 ** 9)
        BAR()
        conv_all_store = pl.ctr[QA]
        conv_tail_loads = max(x for x in conv["load"] if x) if any(conv["load"]) else 0
        conv_tail_casts = [conv["cast"][0], conv["cast"][1]]

        pl.wait(QA, "V", v_sdone_hist[-1])
        a_lnfg = pl.dma(QA, bcA[:, 0:1024], RawAP(lnFg.tensor, 0, [[0, 128], [1, 1024]]))
        a_lnfb = pl.dma(QA, bcA[:, 1024:2048], RawAP(lnFb.tensor, 0, [[0, 128], [1, 1024]]))
        a_sb2 = pl.dma(QA, bcA[:, 2048:3072], RawAP(sb2.tensor, 0, [[0, 128], [1, 1024]]))
        pl.wait("V", QA, max(a_sb2, a_sigc))
        v_sb2 = pl.op("V", lambda e: e.tensor_scalar_mul(
            bcA[:, 2048:3072], bcA[:, 2048:3072], sigc[:, 0:1]), inc=True)

        wcnt = {"n": 0}
        w1t_free = [None, None, None]
        psh_free = [None, None]
        psy_free = [None, None]
        w2_free = ("T", t_pm_last)
        xnt_free = ("T", t_pm_last)
        h2_free = ("T", pl.ctr["T"])
        yacc_freeV = pl.ctr["V"]
        yacc_freeS = pl.ctr["S"]
        a_out = None
        out_free = [None, None]
        v_out_hist = [None, None]
        for tb in range(NTB):
            BAR()
            b, half = tb // 2, tb % 2
            pl.waitp(QA, xnt_free)
            pl.wait(QA, QA, xnt_store_cnt[b])
            a_xt = None
            for dc in range(8):
                a_xt = pl.dma(QA, xnT_tb[:, dc * 1024:(dc + 1) * 1024],
                              xnT_sc[b, dc, :, half * 1024:half * 1024 + 1024])
            pl.dfence()
            v_comb = {}
            s_ctc = {}
            for ts_ in range(TSUB):
                BAR()
                pl.wait("T", QA, a_xt)
                if tb == 0 and ts_ == 0:
                    pl.wait("T", "V", v_sdone_hist[-1])
                    pl.wait("T", "V", yacc_freeV)
                pl.waitp("T", v_comb.get(ts_ - 1) and ("V", v_comb[ts_ - 1]))
                t_rt = None
                for dc in range(8):
                    t_rt = pl.op("T", lambda e, o=P0[:, 0:8],
                                 l=xnT_tb[:, dc * 1024 + ts_ * 128: dc * 1024 + (ts_ + 1) * 128],
                                 r=rw_sb[:, dc * 8:(dc + 1) * 8],
                                 st=(dc == 0), sp=(dc == 7): e.matmul(o, l, r, start=st, stop=sp),
                                 inc=(dc == 7))
                pl.wait("V", "T", t_rt)
                pl.wait("V", QA, a_rb)
                pl.op("V", lambda e: e.tensor_add(pr0[:, :], P0[:, 0:8], rb_b[:, :]))
                v_sm = pl.op("V", lambda e: e.reduce_max(
                    col[:, 12:13], pr0[:, :], axis=AX.X, negate=True), inc=True)
                pl.wait("S", "V", v_sm)
                s_e8 = pl.op("S", lambda e: e.activation(
                    pre[:, :], pr0[:, :], AF.Exp, bias=col[:, 12:13]), inc=True)
                pl.wait("V", "S", s_e8)
                pl.vred(lambda e: e.reduce_sum(col[:, 13:14], pre[:, :], axis=AX.X))
                pl.op("V", lambda e: e.reciprocal(col[:, 13:14], col[:, 13:14]))
                pl.op("V", lambda e: e.tensor_scalar_mul(prb[:, :], pre[:, :], col[:, 13:14]))
                pl.vred(lambda e: e.reduce_max(col[:, 14:15], prb[:, :], axis=AX.X))
                pl.op("V", lambda e: e.tensor_scalar(msk8[:, :], prb[:, :], col[:, 14:15],
                                                     None, op0=ALU.is_lt))
                pl.op("V", lambda e: e.tensor_mul(msk8[:, :], msk8[:, :], prb[:, :]))
                pl.vred(lambda e: e.reduce_max(col[:, 15:16], msk8[:, :], axis=AX.X))
                pl.op("V", lambda e: e.tensor_scalar(msk8[:, :], prb[:, :], col[:, 15:16],
                                                     None, op0=ALU.is_ge))
                pl.op("V", lambda e: e.tensor_mul(msk8[:, :], msk8[:, :], prb[:, :]))
                pl.op("V", lambda e: e.tensor_tensor(col[:, 14:15], col[:, 14:15],
                                                     col[:, 15:16], op=ALU.add))
                pl.op("V", lambda e: e.reciprocal(col[:, 14:15], col[:, 14:15]))
                v_comb[ts_] = pl.op("V", lambda e, d=comb_sb[:, ts_ * 8:(ts_ + 1) * 8]:
                                    e.tensor_scalar_mul(d, msk8[:, :], col[:, 14:15]), inc=True)
                pl.wait("T", "V", v_comb[ts_])
                pl.waitp("T", s_ctc.get(ts_ - 1) and ("S", s_ctc[ts_ - 1]))
                t_ct = pl.op("T", lambda e, o=P1[0:8, 512:640],
                             src=comb_sb[:, ts_ * 8:(ts_ + 1) * 8]:
                             e.transpose(o, src, idf[:, :]), inc=True)
                pl.wait("S", "T", t_ct)
                s_ctc[ts_] = pl.op("S", lambda e, d=combT[0:8, ts_ * 128:(ts_ + 1) * 128],
                                   p=P1[0:8, 512:640]: e.copy(d, p), inc=True)
            v_yinit = None
            for ts_ in range(TSUB):
                pl.wait("T", "S", max(s_ctc[ts_], s_b2))
                if ts_ > 0:
                    pl.wait("T", "V", v_yinit)
                elif tb == 0:
                    pl.wait("T", "V", yacc_freeV)
                    pl.wait("T", "S", yacc_freeS)
                t_b2 = None
                for dh in range(2):
                    t_b2 = pl.op("T", lambda e, o=P1[:, dh * 512:dh * 512 + 512],
                                 l=combT[0:8, ts_ * 128:(ts_ + 1) * 128],
                                 r=b2_bf[0:8, dh * 512:dh * 512 + 512]:
                                 e.matmul(o, l, r, start=True, stop=True), inc=(dh == 1))
                pl.wait("V", "T", t_b2)
                pl.wait("V", "V", 0 if v_sb2 is None else v_sb2)
                if tb == 0 and ts_ >= 6:
                    pl.wait("V", QA, conv_tail_loads)
                    for cc in conv_tail_casts:
                        pl.waitp("V", cc)
                v_yinit = pl.op("V", lambda e, d=y_acc[:, ts_ * 1024:(ts_ + 1) * 1024]:
                                e.tensor_add(d, P1[:, 0:1024], bcA[:, 2048:3072]), inc=True)
            v_yinit_all = v_yinit
            s_h2 = {}
            v_yev_last = {}
            for e_ in range(9):
                Fh = 8 if e_ == 8 else 16
                shared = (e_ == 8)
                for fh in range(2):
                    a_w2h = None
                    w2q = 0
                    for fc in range(Fh):
                        fcg = fh * Fh + fc
                        slot = wcnt["n"] % 3
                        pl.waitp(QW, w1t_free[slot])
                        if wcnt["n"] == 0:
                            pl.wait(QW, QA, conv_tail_loads)
                            for cc in conv_tail_casts:
                                pl.waitp(QW, cc)
                            pl.wait(QW, QA, pl.ctr[QA])
                        if shared:
                            srcw = RawAP(sw1b.tensor, fcg * 128, [[F2, 128], [128 * F2, 8], [1, 128]])
                        else:
                            srcw = RawAP(w1b.tensor, e_ * D * F + fcg * 128,
                                         [[F, 128], [128 * F, 8], [1, 128]])
                        a_w1 = pl.dma(QW, w1t[slot][:, :], srcw)
                        wcnt["n"] += 1
                        psh = P0[:, (fc % 2) * 1024:(fc % 2) * 1024 + 1024]
                        pl.wait("T", QW, a_w1)
                        if tb == 0 and e_ == 0 and fh == 0 and fc < 2:
                            pl.wait("T", "V", v_comb[TSUB - 1])
                        pl.waitp("T", psh_free[fc % 2])
                        if fc == 0 and fh == 0:
                            pl.waitp("T", w2_free)
                        t_h = None
                        for t5 in range(2):
                            for dc in range(8):
                                t_h = pl.op("T", lambda e, o=psh[:, t5 * 512:t5 * 512 + 512],
                                            l=w1t[slot][:, dc * 128:(dc + 1) * 128],
                                            r=xnT_tb[:, dc * 1024 + t5 * 512: dc * 1024 + t5 * 512 + 512],
                                            st=(dc == 0), sp=(dc == 7):
                                            e.matmul(o, l, r, start=st, stop=sp),
                                            inc=(dc == 7 and t5 == 1))
                        w1t_free[slot] = ("T", t_h)
                        pl.wait("S", "T", t_h)
                        pl.wait("S", "V", v_bgs if shared else v_bg)
                        if shared:
                            bcol, gcol = bgs[:, fcg:fcg + 1], gts[:, fcg:fcg + 1]
                        else:
                            bcol = bg_all[:, e_ * 32 + fcg:e_ * 32 + fcg + 1]
                            gcol = gt_all[:, e_ * 32 + fcg:e_ * 32 + fcg + 1]
                        s_h2[fc] = pl.op(
                            "S", lambda e, d=h2h[:, fc * 1024:(fc + 1) * 1024], p=psh,
                            bc=bcol, gc=gcol:
                            e.activation(d, p, AF.Gelu, bias=bc, scale=gc), inc=True)
                        psh_free[fc % 2] = ("S", s_h2[fc])
                        nw2 = Fh // 2
                        if fc < nw2:
                            for sub in range(2):
                                fcl = fc * 2 + sub
                                if tb == 0 and e_ == 0 and fh == 0:
                                    pl.wait(QA, QA, conv_all_store)
                                pl.waitp(QA, w2_free)
                                if shared:
                                    src2 = RawAP(sw2b.tensor, (fh * Fh + fcl) * 128 * D,
                                                 [[D, 128], [1, D]])
                                else:
                                    src2 = RawAP(w2b.tensor, e_ * F * D + (fh * Fh + fcl) * 128 * D,
                                                 [[D, 128], [1, D]])
                                a_w2h = pl.dma(QA, w2h[:, fcl * 1024:(fcl + 1) * 1024], src2)
                    t_y = None
                    for ts_ in range(TSUB):
                        psy = P1[:, (ts_ % 2) * 1024:(ts_ % 2) * 1024 + 1024]
                        pl.wait("T", "S", s_h2[Fh - 1])
                        pl.wait("T", QA, a_w2h)
                        if tb == 0 and e_ == 0 and fh == 0 and ts_ < 2:
                            pl.wait("T", "V", v_yinit_all)
                        pl.waitp("T", psy_free[ts_ % 2])
                        for dh in range(2):
                            for fc in range(Fh):
                                t_y = pl.op("T", lambda e, o=psy[:, dh * 512:dh * 512 + 512],
                                            l=h2h[:, fc * 1024 + ts_ * 128: fc * 1024 + (ts_ + 1) * 128],
                                            r=w2h[:, fc * 1024 + dh * 512: fc * 1024 + dh * 512 + 512],
                                            st=(fc == 0), sp=(fc == Fh - 1):
                                            e.matmul(o, l, r, start=st, stop=sp),
                                            inc=(fc == Fh - 1 and dh == 1))
                        ccol = sigc[:, 0:1] if shared else comb_sb[:, ts_ * 8 + e_: ts_ * 8 + e_ + 1]
                        wkb = wk[ts_ % 2]
                        pl.wait("S", "T", t_y)
                        pl.waitp("S", wk_free[ts_ % 2])
                        s_yt = pl.op("S", lambda e, d=wkb[:, :], p=psy, c=ccol:
                                     e.mul(d, p, c), inc=True)
                        pl.wait("V", "S", s_yt)
                        v_ye = pl.op("V", lambda e, d=y_acc[:, ts_ * 1024:(ts_ + 1) * 1024],
                                     w_=wkb: e.tensor_add(d, d, w_[:, :]), inc=True)
                        wk_free[ts_ % 2] = ("V", v_ye)
                        psy_free[ts_ % 2] = ("V", v_ye)
                        v_yev_last[ts_] = v_ye
                    w2_free = ("T", t_y)
            for ts_ in range(TSUB):
                BAR()
                par = ts_ % 2
                tile = half * 8 + ts_
                pl.waitp(QA, xfb_free[par])
                a_x = pl.dma(QA, xfb[par][:, :], x_in[b, tile * 128:(tile + 1) * 128, :])
                pl.waitp(QA, sfb_free[par])
                pl.wait(QA, QA, s_ss_cnt[(b, tile)])
                a_s = pl.dma(QA, sfb[par][:, :], s_sc[b, tile])
                ysl = y_acc[:, ts_ * 1024:(ts_ + 1) * 1024]
                pl.wait("V", "V", v_yev_last[ts_])
                pl.waitp("V", wk_free[2])
                pl.waitp("V", wk_free[0])
                pl.vred(lambda e, y=ysl: e.reduce_sum(col[:, 7:8], y, axis=AX.X))
                pl.op("V", lambda e: e.tensor_scalar_mul(col[:, 8:9], col[:, 7:8], -1.0 / D))
                pl.op("V", lambda e, y=ysl: e.tensor_scalar_add(wk[2][:, :], y, col[:, 8:9]))
                pl.op("V", lambda e: e.tensor_mul(wk[0][:, :], wk[2][:, :], wk[2][:, :]))
                v_ss = pl.op("V", lambda e: e.reduce_sum(col[:, 9:10], wk[0][:, :], axis=AX.X), inc=True)
                pl.wait("S", "V", v_ss)
                s_sq = pl.op("S", lambda e: e.activation(
                    col[:, 10:11], col[:, 9:10], AF.Sqrt, bias=epss[:, 0:1], scale=1.0 / D), inc=True)
                pl.wait("V", "S", s_sq)
                pl.op("V", lambda e: e.reciprocal(col[:, 11:12], col[:, 10:11]))
                pl.op("V", lambda e: e.tensor_scalar_mul(wk[2][:, :], wk[2][:, :], col[:, 11:12]))
                pl.op("V", lambda e: e.tensor_mul(wk[2][:, :], wk[2][:, :], bcA[:, 0:1024]))
                v_t2 = pl.op("V", lambda e: e.tensor_add(wk[0][:, :], wk[2][:, :],
                                                         bcA[:, 1024:2048]), inc=True)
                pl.wait("S", "V", v_t2)
                s_g = pl.op("S", lambda e: e.activation(wk[2][:, :], wk[0][:, :], AF.Silu), inc=True)
                pl.wait("V", "S", s_g)
                pl.wait("V", QA, max(a_s, a_x))
                pl.op("V", lambda e, sp=sfb[par]: e.tensor_mul(wk[0][:, :], wk[2][:, :], sp[:, :]))
                v_out = pl.op("V", lambda e, xp=xfb[par]: e.tensor_add(
                    wk[0][:, :], wk[0][:, :], xp[:, :]), inc=True)
                xfb_free[par] = ("V", v_out)
                sfb_free[par] = ("V", v_out)
                wk_free[2] = ("V", v_out)
                pl.wait(QA, "V", v_out)
                a_out = pl.dma(QA, out_d[b, tile * 128:(tile + 1) * 128, :], wk[0][:, :])
                wk_free[0] = (QA, a_out)
        BAR()
        pl.wait(QA, QA, pl.ctr[QA])
        pl.wait(QW, QA, pl.ctr[QA])

        for oplist in pl.ops.values():
            for f in oplist:
                if hasattr(f, "semname"):
                    f.sem = sems[f.semname]
        ctx.enter_context(nc.allow_non_contiguous_dma(
            reason="small one-time column-layout constant loads"))

        @block.sync
        def _(e):
            for f in pl.ops["W"]:
                f(e)

        @block.gpsimd
        def _(e):
            for f in pl.ops["A"]:
                f(e)

        @block.tensor
        def _(e):
            for f in pl.ops["T"]:
                f(e)

        @block.scalar
        def _(e):
            for f in pl.ops["S"]:
                f(e)

        @block.vector
        def _(e):
            for f in pl.ops["V"]:
                f(e)

    return nc


_NC_CACHE = None


def kernel(**inputs):
    global _NC_CACHE
    x = np.ascontiguousarray(np.asarray(inputs["L0_attn_x"], dtype=np.float32))
    rpb = np.asarray(inputs["L0_attn_rpb"], dtype=np.float32)

    p = np.arange(128)[:, None]
    c = np.arange(2432)[None, :]
    idx = np.clip(c + 127 - p, 0, 4094)
    biasz = rpb[idx, :]
    biasz = np.transpose(biasz, (2, 0, 1)).copy()
    biasz[:, (c > p + 1920)] = -1e30
    biasz = np.ascontiguousarray(biasz.astype(np.float32))

    identf = np.eye(128, dtype=np.float32)
    epsc = np.array([1e-5, 0.0, 0.0, 0.0], dtype=np.float32)

    if _NC_CACHE is None:
        _NC_CACHE = build_program()
    nc = _NC_CACHE

    wnames = ["L0_attn_W_w", "L0_attn_W_b", "L0_attn_proj_w", "L0_attn_proj_b",
              "L0_attn_ln_g", "L0_attn_ln_b", "L0_ffn_router_w", "L0_ffn_router_b",
              "L0_ffn_w1", "L0_ffn_b1", "L0_ffn_gate", "L0_ffn_w2", "L0_ffn_b2",
              "L0_ffn_sw1", "L0_ffn_sb1", "L0_ffn_sgate", "L0_ffn_sw2", "L0_ffn_sb2",
              "L0_ffn_ln_g", "L0_ffn_ln_b"]
    base = {k: np.ascontiguousarray(np.asarray(inputs[k], dtype=np.float32)) for k in wnames}
    base["L0_ffn_shared_weight"] = np.asarray(inputs["L0_ffn_shared_weight"],
                                              dtype=np.float32).reshape(1)
    base["biasz"] = biasz
    base["identf"] = identf
    base["epsc"] = epsc

    in_maps = []
    for ci in range(8):
        m = dict(base)
        m["x"] = np.ascontiguousarray(x[ci * 2:(ci + 1) * 2])
        in_maps.append(m)

    res = run_bass_kernel_spmd(nc, in_maps, core_ids=list(range(8)))
    return np.concatenate([r["out"] for r in res.results], axis=0)



# version 06
# speedup vs baseline: 1.5408x; 1.0806x over previous
import numpy as np
from contextlib import ExitStack

import concourse.bass as bass
import concourse.mybir as mybir
from concourse.bass_utils import run_bass_kernel_spmd
from concourse.ap import AP as RawAP

F32 = mybir.dt.float32
BF16 = mybir.dt.bfloat16
AF = mybir.ActivationFunctionType
ALU = mybir.AluOpType
AX = mybir.AxisListType

BS = 2
S = 2048
D = 1024
H = 16
E = 8
F = 4096
F2 = 2048
NT = 16
NTB = 4
TSUB = 8


class Planner:
    def __init__(self):
        self.ops = {k: [] for k in ("W", "A", "T", "S", "V")}
        self.ctr = {k: 0 for k in ("W", "A", "T", "S", "V", "F")}
        self.lastw = {}
        self.vspacer = None

    def dfence(self):
        self.ctr["F"] += 1

        def f(e):
            e.drain().then_inc(f.sem, 1)
        f.semname = "F"
        self.ops["A"].append(f)
        return self.ctr["F"]

    def dma(self, q, out, in_):
        self.ctr[q] += 16

        def f(e, out=out, in_=in_):
            e.dma_start(out=out, in_=in_).then_inc(f.sem, 16)
        f.semname = q
        self.ops[q].append(f)
        return self.ctr[q]

    def op(self, eng, fn, inc=False, wide=False):
        if eng == "V":
            inc = True
        ret = None
        if inc:
            self.ctr[eng] += 1
            ret = self.ctr[eng]

            def g(e, fn=fn):
                fn(e).then_inc(g.sem, 1)
            g.semname = eng
            self.ops[eng].append(g)
        else:
            self.ops[eng].append(fn)
        if eng == "V" and not wide:
            self.wait("V", "V", ret)
        return ret

    def vred(self, fn):
        v = self.op("V", fn, inc=True)
        self.wait("V", "V", v)
        return v

    def wait(self, eng, sem, val):
        if not val:
            return
        key = (eng, sem)
        if self.lastw.get(key, 0) >= val:
            return
        self.lastw[key] = val

        def w(e, val=val):
            e.wait_ge(w.sem, val)
        w.semname = sem
        self.ops[eng].append(w)

    def waitp(self, eng, pair):
        if pair:
            self.wait(eng, pair[0], pair[1])


def build_program():
    nc = bass.Bass("TRN2", target_bir_lowering=False)

    def din(name, shape, dt=F32):
        return nc.dram_tensor(name, shape, dt, kind="ExternalInput").ap()

    x_in = din("x", [BS, S, D])
    Ww = din("L0_attn_W_w", [3 * D, D])
    Wb = din("L0_attn_W_b", [3 * D])
    pw = din("L0_attn_proj_w", [D, D])
    pb = din("L0_attn_proj_b", [D])
    lnAg = din("L0_attn_ln_g", [D])
    lnAb = din("L0_attn_ln_b", [D])
    rtw = din("L0_ffn_router_w", [D, E])
    rtb = din("L0_ffn_router_b", [E])
    w1 = din("L0_ffn_w1", [E, D, F])
    b1 = din("L0_ffn_b1", [E, F])
    gt = din("L0_ffn_gate", [E, F])
    w2 = din("L0_ffn_w2", [E, F, D])
    b2 = din("L0_ffn_b2", [E, D])
    sw1 = din("L0_ffn_sw1", [D, F2])
    sb1 = din("L0_ffn_sb1", [F2])
    sgt = din("L0_ffn_sgate", [F2])
    sw2 = din("L0_ffn_sw2", [F2, D])
    sb2 = din("L0_ffn_sb2", [D])
    shw = din("L0_ffn_shared_weight", [1])
    lnFg = din("L0_ffn_ln_g", [D])
    lnFb = din("L0_ffn_ln_b", [D])
    biasz = din("biasz", [H, 128, 2432])
    identf = din("identf", [128, 128])
    epsc = din("epsc", [4])

    out_d = nc.dram_tensor("out", [BS, S, D], F32, kind="ExternalOutput").ap()

    def scr(name, shape, dt):
        return nc.dram_tensor(name, shape, dt, kind="Internal").ap()
    xnT_sc = scr("xnT_sc", [BS, 8, 128, S], BF16)
    qT_sc = scr("qT_sc", [BS, 8, 128, S], BF16)
    kT_sc = scr("kT_sc", [BS, 8, 128, S], BF16)
    v_sc = scr("v_sc", [BS, NT, 128, D], BF16)
    s_sc = scr("s_sc", [BS, NT, 128, D], F32)
    w1b = scr("w1b", [E, D, F], BF16)
    w2b = scr("w2b", [E, F, D], BF16)
    sw1b = scr("sw1b", [D, F2], BF16)
    sw2b = scr("sw2b", [F2, D], BF16)
    sig_rt = scr("sig_rt", [1], F32)

    pl = Planner()
    QW, QA = "W", "A"

    with ExitStack() as ctx:
        sb = lambda n, s, d: ctx.enter_context(nc.sbuf_tensor(n, s, d))
        idf = sb("idf", [128, 128], F32)
        qkb = sb("qkb", [128, 16], F32)
        vb64 = sb("vb64", [64, 16], F32)
        bcA = sb("bcA", [128, 3072], F32)
        rb_b = sb("rb_b", [128, 8], F32)
        rw_f = sb("rw_f", [128, 64], F32)
        rw_sb = sb("rw_sb", [128, 64], BF16)
        b2_bf = sb("b2_bf", [8, 1024], BF16)
        bg_all = sb("bg_all", [128, 256], F32)
        gt_all = sb("gt_all", [128, 256], F32)
        bgs = sb("bgs", [128, 16], F32)
        gts = sb("gts", [128, 16], F32)
        nm_st = sb("nm_st", [128, 32], F32)
        rs_st = sb("rs_st", [128, 32], F32)
        epss = sb("epss", [128, 4], F32)
        sig1 = sb("sig1", [1, 1], F32)
        sigc = sb("sigc", [128, 1], F32)
        AR1 = sb("AR1", [128, 16384], BF16)
        AR2 = sb("AR2", [128, 24576], BF16)
        AR3 = sb("AR3", [128, 8192], F32)
        qtb = [sb("qtb0", [64, 2048], BF16), sb("qtb1", [64, 2048], BF16)]
        ktb = [sb("ktb0", [64, 2048], BF16), sb("ktb1", [64, 2048], BF16)]
        vsb = [sb("vsb0", [128, 1024], BF16), sb("vsb1", [128, 1024], BF16)]
        bias_h = sb("bias_h", [128, 2432], F32)
        oTstage = sb("oTstage", [64, 2048], BF16)
        pTstage = sb("pTstage", [64, 1024], BF16)
        w1t = [sb("w1t0", [128, 1024], BF16), sb("w1t1", [128, 1024], BF16),
               sb("w1t2", [128, 1024], BF16)]
        qkev = [sb("qkev0", [128, 1024], BF16), sb("qkev1", [128, 1024], BF16)]
        xfb = [sb("xfb0", [128, 1024], F32), sb("xfb1", [128, 1024], F32)]
        sfb = [sb("sfb0", [128, 1024], F32), sb("sfb1", [128, 1024], F32)]
        wk = [sb("wk0", [128, 1024], F32), sb("wk1", [128, 1024], F32),
              sb("wk2", [128, 1024], F32)]
        col = sb("colb", [128, 16], F32)
        dum = sb("dum", [1, 1], F32)
        dum2 = sb("dum2", [1, 1], F32)
        pr0 = sb("pr0", [128, 8], F32)
        pre = sb("pre", [128, 8], F32)
        prb = sb("prb", [128, 8], F32)
        msk8 = sb("msk8", [128, 8], F32)
        comb_sb = sb("comb_sb", [128, 64], F32)
        combT = pTstage[0:8, :]
        b2f = bcA[0:8, 0:1024]

        P0 = ctx.enter_context(nc.psum_tensor("P0", [128, 2048], F32))
        P1 = ctx.enter_context(nc.psum_tensor("P1", [128, 2048], F32))

        sems = {k: ctx.enter_context(nc.semaphore("sem" + k))
                for k in ("W", "A", "T", "S", "V", "F")}
        block = ctx.enter_context(nc.Block())

        def BAR():
            pl.op("S", lambda e: e.copy(dum[:, :], dum[:, :]), inc=True)
            pl.op("V", lambda e: e.tensor_copy(dum[:, :], dum[:, :]), inc=True)
            vals = dict(pl.ctr)
            for eng in ("T", "S", "V", "W", "A"):
                for sem_, v_ in vals.items():
                    if sem_ != eng and v_ > 0:
                        pl.wait(eng, sem_, v_)

        xnT_sb = AR1
        h2h = AR1
        WT = AR2
        oTpack = AR2[:, 0:16384]
        pT = AR2[:, 16384:24576]
        w2h = AR2[:, 0:16384]
        xnT_tb = AR2[:, 16384:24576]
        sc_sb = AR3[:, 0:2048]
        pe_sb = AR3[:, 2048:4096]
        pn_sb = AR3[:, 4096:6144]
        cvst = [AR3[:, 6144:7168], AR3[:, 7168:8192]]
        y_acc = AR3

        pl.dma(QA, idf[:, :], identf[:, :])
        a_qkb = pl.dma(QA, qkb[:, :], RawAP(Wb.tensor, 0, [[1, 128], [128, 16]]))
        a_vb = pl.dma(QA, vb64[:, :], RawAP(Wb.tensor, 2048, [[1, 64], [64, 16]]))
        a_rb = pl.dma(QA, rb_b[:, :], RawAP(rtb.tensor, 0, [[0, 128], [1, 8]]))
        a_rw = pl.dma(QA, rw_f[:, :], RawAP(rtw.tensor, 0, [[8, 128], [1024, 8], [1, 8]]))
        a_b2 = pl.dma(QA, b2f[:, :], b2[:, :])
        for e_ in range(E):
            a_bg = pl.dma(QA, bg_all[:, e_ * 32:(e_ + 1) * 32],
                          RawAP(b1.tensor, e_ * 4096, [[1, 128], [128, 32]]))
            a_gt = pl.dma(QA, gt_all[:, e_ * 32:(e_ + 1) * 32],
                          RawAP(gt.tensor, e_ * 4096, [[1, 128], [128, 32]]))
        a_bgs = pl.dma(QA, bgs[:, :], RawAP(sb1.tensor, 0, [[1, 128], [128, 16]]))
        a_gts = pl.dma(QA, gts[:, :], RawAP(sgt.tensor, 0, [[1, 128], [128, 16]]))
        a_eps = pl.dma(QA, epss[:, :], RawAP(epsc.tensor, 0, [[0, 128], [1, 4]]))
        a_sh = pl.dma(QA, sig1[:, :], shw[None, :])
        F1 = pl.dfence()
        pl.wait("S", "F", F1)
        pl.wait("V", "F", F1)
        pl.wait("S", QA, a_rw)
        pl.op("S", lambda e: e.copy(rw_sb[:, :], rw_f[:, :]))
        pl.wait("S", QA, a_b2)
        s_b2 = pl.op("S", lambda e: e.copy(b2_bf[:, :], b2f[:, :]), inc=True)
        pl.wait("S", QA, a_sh)
        s_sig = pl.op("S", lambda e: e.activation(sig1[:, :], sig1[:, :], AF.Sigmoid), inc=True)
        pl.wait(QA, "S", s_sig)
        a_sr = pl.dma(QA, sig_rt[None, :], sig1[:, :])
        pl.wait(QA, QA, a_sr)
        pl.dfence()
        a_sigc = pl.dma(QA, sigc[:, :], RawAP(sig_rt.tensor, 0, [[0, 128], [1, 1]]))
        pl.dfence()
        pl.wait("V", QA, max(a_bg, a_gt))
        v_bg = pl.op("V", lambda e: e.tensor_mul(bg_all[:, :], bg_all[:, :], gt_all[:, :]), inc=True)
        pl.wait("V", QA, max(a_bgs, a_gts))
        v_bgs = pl.op("V", lambda e: e.tensor_mul(bgs[:, :], bgs[:, :], gts[:, :]), inc=True)

        cvst_free = [None, None]
        qkev_free = [None, None]
        xfb_free = [None, None]
        sfb_free = [None, None]
        wk_free = [None] * 3

        s_wt = None
        for oc in range(24):
            par = oc % 2
            pl.waitp(QA, cvst_free[par])
            a_w = pl.dma(QA, cvst[par][:, 0:1024], Ww[oc * 128:(oc + 1) * 128, :])
            for dc in range(8):
                slot = dc % 2
                psl = P0[:, slot * 512:slot * 512 + 128]
                pl.wait("T", QA, a_w)
                if dc >= 2 or oc > 0:
                    pl.wait("T", "S", (s_wt or 0) - 1 if dc >= 2 else s_wt)
                t_wt = pl.op("T", lambda e, p=psl, src=cvst[par][:, dc * 128:(dc + 1) * 128]:
                             e.transpose(p, src, idf[:, :]), inc=True)
                dst = WT[:, dc * 3072 + oc * 128: dc * 3072 + (oc + 1) * 128]
                pl.wait("S", "T", t_wt)
                s_wt = pl.op("S", lambda e, d=dst, p=psl: e.copy(d, p), inc=True)
            cvst_free[par] = ("T", pl.ctr["T"])

        s_xnT_last = {}
        last_t_tr = None
        last_v_xn = None
        xnt_store_cnt = {}
        qkv_store_cnt = {}
        for b in range(BS):
            for tile in range(NT):
                BAR()
                colix = b * 16 + tile
                par = tile % 2
                pl.waitp(QA, xfb_free[par])
                a_x = pl.dma(QA, xfb[par][:, :], x_in[b, tile * 128:(tile + 1) * 128, :])
                pl.wait("V", QA, a_x)
                pl.wait("V", QA, a_eps)
                pl.vred(lambda e, xp=xfb[par]: e.reduce_sum(col[:, 0:1], xp[:, :], axis=AX.X))
                v_nm = pl.op("V", lambda e, c=colix: e.tensor_scalar_mul(
                    nm_st[:, c:c + 1], col[:, 0:1], -1.0 / D), inc=True)
                pl.wait("S", "V", v_nm)
                s_xm = pl.op("S", lambda e, c=colix, xp=xfb[par]: e.add(
                    wk[0][:, :], xp[:, :], nm_st[:, c:c + 1]), inc=True)
                xfb_free[par] = ("S", s_xm)
                pl.wait("V", "S", s_xm)
                pl.op("V", lambda e: e.tensor_mul(wk[1][:, :], wk[0][:, :], wk[0][:, :]), wide=True)
                v_sq = pl.op("V", lambda e: e.reduce_sum(col[:, 1:2], wk[1][:, :], axis=AX.X), inc=True)
                pl.wait("S", "V", v_sq)
                s_si = pl.op("S", lambda e: e.activation(
                    col[:, 2:3], col[:, 1:2], AF.Sqrt, bias=epss[:, 0:1], scale=1.0 / D), inc=True)
                pl.wait("V", "S", s_si)
                pl.op("V", lambda e, c=colix: e.reciprocal(rs_st[:, c:c + 1], col[:, 2:3]))
                pl.wait("V", "T", last_t_tr)
                last_v_xn = pl.op("V", lambda e, c=colix: e.tensor_scalar_mul(
                    wk[2][:, :], wk[0][:, :], rs_st[:, c:c + 1]), inc=True, wide=True)
                for dc in range(8):
                    slot = dc % 2
                    psl = P0[:, slot * 512:slot * 512 + 128]
                    pl.wait("T", "V", last_v_xn)
                    pl.wait("T", "S", s_xnT_last.get((slot,)))
                    last_t_tr = pl.op("T", lambda e, p=psl, src=wk[2][:, dc * 128:(dc + 1) * 128]:
                                      e.transpose(p, src, idf[:, :]), inc=True)
                    dst = xnT_sb[:, dc * 2048 + tile * 128: dc * 2048 + (tile + 1) * 128]
                    pl.wait("S", "T", last_t_tr)
                    s_xnT_last[(slot,)] = pl.op("S", lambda e, d=dst, p=psl: e.copy(d, p), inc=True)
            s_xnT_all = pl.ctr["S"]
            pl.wait(QA, "S", s_xnT_all)
            for dc in range(8):
                xnt_store_cnt[b] = pl.dma(QA, xnT_sc[b, dc], xnT_sb[:, dc * 2048:(dc + 1) * 2048])

            ps_free = {0: None, 1: None}
            pl.wait("T", "S", s_xnT_all)
            for oc in range(16):
                ps = P0 if oc % 2 == 0 else P1
                pl.waitp("T", ps_free[oc % 2])
                t_mm = None
                for dc in range(8):
                    for tc in range(4):
                        lhs = WT[:, dc * 3072 + oc * 128: dc * 3072 + (oc + 1) * 128]
                        rhs = xnT_sb[:, dc * 2048 + tc * 512: dc * 2048 + tc * 512 + 512]
                        t_mm = pl.op("T", lambda e, o=ps[:, tc * 512:tc * 512 + 512], l=lhs, r=rhs,
                                     st=(dc == 0), sp=(dc == 7): e.matmul(o, l, r, start=st, stop=sp),
                                     inc=(dc == 7 and tc == 3))
                dst_sc = qT_sc if oc < 8 else kT_sc
                occ = oc % 8
                sev = None
                for hh in range(2):
                    pl.wait("S", "T", t_mm)
                    pl.waitp("S", qkev_free[hh])
                    sev = pl.op("S", lambda e, d=qkev[hh][:, :],
                                p=ps[:, hh * 1024:hh * 1024 + 1024],
                                bcol=qkb[:, oc:oc + 1]:
                                e.add(d, p, bcol), inc=True)
                    pl.wait(QA, "S", sev)
                    adma = pl.dma(QA, dst_sc[b, occ, :, hh * 1024:hh * 1024 + 1024], qkev[hh][:, :])
                    qkev_free[hh] = (QA, adma)
                ps_free[oc % 2] = ("S", sev)
            for tile in range(NT):
                ps = P0 if tile % 2 == 0 else P1
                pl.waitp("T", ps_free[tile % 2])
                for dc in range(8):
                    for oc2 in range(2):
                        lhs = xnT_sb[:, dc * 2048 + tile * 128: dc * 2048 + (tile + 1) * 128]
                        rhs = WT[:, dc * 3072 + 2048 + oc2 * 512: dc * 3072 + 2048 + oc2 * 512 + 512]
                        t_mm = pl.op("T", lambda e, o=ps[:, oc2 * 512:oc2 * 512 + 512], l=lhs, r=rhs,
                                     st=(dc == 0), sp=(dc == 7): e.matmul(o, l, r, start=st, stop=sp),
                                     inc=(dc == 7 and oc2 == 1))
                hh = tile % 2
                pl.wait("S", "T", t_mm)
                pl.waitp("S", qkev_free[hh])
                sev = pl.op("S", lambda e, d=qkev[hh][:, :], p=ps[:, 0:1024]: e.copy(d, p), inc=True)
                pl.wait(QA, "S", sev)
                adma = pl.dma(QA, v_sc[b, tile], qkev[hh][:, :])
                qkev_free[hh] = (QA, adma)
                ps_free[tile % 2] = ("S", sev)
            qkv_store_cnt[b] = pl.ctr[QA]

        conv_jobs = []
        for e_ in range(E):
            for dc in range(8):
                for q4 in range(4):
                    off = e_ * D * F + dc * 128 * F + q4 * 1024
                    conv_jobs.append((RawAP(w1.tensor, off, [[F, 128], [1, 1024]]),
                                      RawAP(w1b.tensor, off, [[F, 128], [1, 1024]])))
        for e_ in range(E):
            for fc in range(32):
                off = e_ * F * D + fc * 128 * D
                conv_jobs.append((RawAP(w2.tensor, off, [[D, 128], [1, 1024]]),
                                  RawAP(w2b.tensor, off, [[D, 128], [1, 1024]])))
        for dc in range(8):
            for hh in range(2):
                off = dc * 128 * F2 + hh * 1024
                conv_jobs.append((RawAP(sw1.tensor, off, [[F2, 128], [1, 1024]]),
                                  RawAP(sw1b.tensor, off, [[F2, 128], [1, 1024]])))
        for fc in range(16):
            off = fc * 128 * D
            conv_jobs.append((RawAP(sw2.tensor, off, [[D, 128], [1, 1024]]),
                              RawAP(sw2b.tensor, off, [[D, 128], [1, 1024]])))
        conv = {"i": 0, "load": [None, None], "cast": [None, None]}

        def emit_conv(njobs):
            for _ in range(njobs):
                i = conv["i"]
                if i >= len(conv_jobs):
                    return
                conv["i"] = i + 1
                par = i % 2
                eng = "S" if par == 0 else "V"
                src, dst = conv_jobs[i]
                pl.waitp(QA, cvst_free[par])
                a_l = pl.dma(QA, cvst[par][:, 0:1024], src)
                conv["load"][par] = a_l
                pl.wait(eng, QA, a_l)
                pl.waitp(eng, qkev_free[par])
                if eng == "S":
                    cinc = pl.op("S", lambda e, d=qkev[par][:, :], s_=cvst[par][:, 0:1024]:
                                 e.copy(d, s_), inc=True)
                else:
                    cinc = pl.op("V", lambda e, d=qkev[par][:, :], s_=cvst[par][:, 0:1024]:
                                 e.tensor_copy(d, s_), inc=True, wide=True)
                cvst_free[par] = (eng, cinc)
                conv["cast"][par] = (eng, cinc)
                pl.wait(QA, eng, cinc)
                a_st = pl.dma(QA, dst, qkev[par][:, :])
                qkev_free[par] = (QA, a_st)

        qk_free = [None, None]
        v_free = [None, None]
        bias_free = None
        stt_last = [None] * 4
        tr_free = [None, None]
        omm_free = None
        v_sdone_hist = []
        s_ss_cnt = {}
        t_pm_last = None
        for b in range(BS):
            s_oev = {}
            a_oTh = {}
            ost_free = None
            pl.wait(QA, QA, qkv_store_cnt[b])
            for h in range(H):
                par = h % 2
                pl.waitp(QA, qk_free[par])
                a_q = pl.dma(QA, qtb[par][:, :], qT_sc[b, h // 2, (h % 2) * 64:(h % 2) * 64 + 64, :])
                a_k = pl.dma(QA, ktb[par][:, :], kT_sc[b, h // 2, (h % 2) * 64:(h % 2) * 64 + 64, :])
                pl.waitp(QA, v_free[par])
                a_v = None
                for vq in range(4):
                    srcv = RawAP(v_sc.tensor, (b * NT + vq * 4) * 128 * D + h * 64,
                                 [[D, 128], [128 * D, 4], [1, 64]])
                    a_v = pl.dma(QA, vsb[par][:, vq * 256:(vq + 1) * 256], srcv)
                pl.waitp(QA, bias_free)
                a_bi = pl.dma(QA, bias_h[:, :], biasz[h, :, :])
                emit_conv(9)
                s_pn = v_pn = None
                for ic in range(4):
                    w = (ic + 1) * 512
                    for qt in range(4):
                        i0 = ic * 512 + qt * 128
                        boff = 1920 - i0
                        pl.wait("T", QA, a_k)
                        for jb in range(ic + 1):
                            pl.waitp("T", ("V", stt_last[jb]) if stt_last[jb] else None)
                            t_mm = pl.op("T", lambda e, o=P0[:, jb * 512:jb * 512 + 512],
                                         l=qtb[par][:, i0:i0 + 128],
                                         r=ktb[par][:, jb * 512:jb * 512 + 512]:
                                         e.matmul(o, l, r, start=True, stop=True), inc=True)
                            pl.wait("V", "T", t_mm)
                            pl.wait("V", QA, a_bi)
                            stt_last[jb] = pl.op(
                                "V", lambda e, d=sc_sb[:, jb * 512:jb * 512 + 512],
                                p=P0[:, jb * 512:jb * 512 + 512],
                                bsl=bias_h[:, boff + jb * 512: boff + jb * 512 + 512]:
                                e.scalar_tensor_tensor(d, p, 1.0 / 32.0, bsl,
                                                       op0=ALU.mult, op1=ALU.add), inc=True, wide=True)
                        v_mx = pl.op("V", lambda e, w_=w: e.reduce_max(
                            col[:, 4:5], sc_sb[:, 0:w_], axis=AX.X, negate=True), inc=True)
                        pl.wait("S", "V", v_mx)
                        pl.wait("S", "V", v_pn)
                        s_ex = pl.op("S", lambda e, w_=w: e.activation(
                            pe_sb[:, 0:w_], sc_sb[:, 0:w_], AF.Exp, bias=col[:, 4:5]), inc=True)
                        pl.wait("V", "S", s_ex)
                        pl.vred(lambda e, w_=w: e.reduce_sum(col[:, 5:6], pe_sb[:, 0:w_], axis=AX.X))
                        v_rc = pl.op("V", lambda e: e.reciprocal(col[:, 6:7], col[:, 5:6]), inc=True)
                        wh = w // 2
                        tr_done_t = pl.ctr["T"]
                        pl.wait("S", "V", v_rc)
                        pl.wait("S", "T", (s_pn is not None) and tr_done_t)
                        s_pn = pl.op("S", lambda e, w_=wh: e.mul(
                            pn_sb[:, 0:w_], pe_sb[:, 0:w_], col[:, 6:7]), inc=True)
                        pl.wait("V", "T", (v_pn is not None) and tr_done_t)
                        v_pn = pl.op("V", lambda e, w_=wh, wf=w: e.tensor_scalar_mul(
                            pn_sb[:, w_:wf], pe_sb[:, w_:wf], col[:, 6:7]), inc=True)
                        njt = 4 * (ic + 1)
                        for jt in range(njt):
                            slot = jt % 2
                            psl = P1[:, slot * 512:slot * 512 + 128]
                            pl.wait("T", "S", s_pn)
                            pl.wait("T", "V", v_pn)
                            pl.waitp("T", tr_free[slot])
                            t_tr = pl.op("T", lambda e, p=psl, src=pn_sb[:, jt * 128:(jt + 1) * 128]:
                                         e.transpose(p, src, idf[:, :]), inc=True)
                            ceng = "S" if jt % 2 == 0 else "V"
                            pl.wait(ceng, "T", t_tr)
                            dstp = pT[:, jt * 512 + qt * 128: jt * 512 + (qt + 1) * 128]
                            if ceng == "S":
                                cv = pl.op("S", lambda e, d=dstp, p=psl: e.copy(d, p), inc=True)
                            else:
                                cv = pl.op("V", lambda e, d=dstp, p=psl: e.tensor_copy(d, p), inc=True)
                            tr_free[slot] = (ceng, cv)
                    pl.wait("T", "S", pl.lastw.get(("dummy",), 0) or None)
                    pl.wait("T", "S", tr_free[0][1] if tr_free[0] and tr_free[0][0] == "S" else None)
                    pl.wait("T", "S", pl.ctr["S"])
                    pl.wait("T", "V", pl.ctr["V"])
                    pl.waitp("T", omm_free)
                    pl.wait("T", QA, a_v)
                    njt = 4 * (ic + 1)
                    t_omm = None
                    for jt in range(njt):
                        t_omm = pl.op("T", lambda e, o=P1[0:64, 1024:1536],
                                      l=vsb[par][:, jt * 64:(jt + 1) * 64],
                                      r=pT[:, jt * 512:(jt + 1) * 512],
                                      st=(jt == 0), sp=(jt == njt - 1):
                                      e.matmul(o, l, r, start=st, stop=sp), inc=(jt == njt - 1))
                    qk_free[par] = ("T", pl.ctr["T"])
                    v_free[par] = ("T", pl.ctr["T"])
                    pl.wait("S", "T", t_omm)
                    pl.wait("S", QA, a_vb)
                    if h % 2 == 0:
                        dsto = oTpack[0:64, (h // 2) * 2048 + ic * 512:(h // 2) * 2048 + ic * 512 + 512]
                    else:
                        if ic == 0:
                            pl.waitp("S", ost_free)
                        dsto = oTstage[0:64, ic * 512: ic * 512 + 512]
                    s_oev[(h, ic)] = pl.op("S", lambda e, d=dsto, p=P1[0:64, 1024:1536],
                                           bcol=vb64[0:64, h:h + 1]:
                                           e.add(d, p, bcol), inc=True)
                    omm_free = ("S", s_oev[(h, ic)])
                bias_free = ("V", pl.ctr["V"])
                if h % 2 == 1:
                    pl.wait(QA, "S", s_oev[(h, 3)])
                    a_oTh[h] = pl.dma(QA, AR2[64:128, (h // 2) * 2048:(h // 2 + 1) * 2048],
                                      oTstage[0:64, :])
                    ost_free = (QA, a_oTh[h])
            s_pjc_last = [None, None]
            a_pj_last = None
            pThi3 = pT.rearrange("p (a c) -> p a c", c=1024)
            for ocp in range(8):
                par = ocp % 2
                pl.waitp(QA, cvst_free[par])
                a_pw = pl.dma(QA, cvst[par][:, 0:1024], pw[ocp * 128:(ocp + 1) * 128, :])
                if ocp == 0:
                    pl.wait("T", "V", pl.ctr["V"])
                for dhalf in range(16):
                    dc, hl = dhalf // 2, dhalf % 2
                    slot = dhalf % 2
                    psl = P0[0:64, slot * 512:slot * 512 + 128]
                    pl.wait("T", QA, a_pw)
                    if s_pjc_last[slot]:
                        pl.wait("T", "S", s_pjc_last[slot])
                    t_tr = pl.op("T", lambda e, p=psl,
                                 src=cvst[par][:, dc * 128 + hl * 64: dc * 128 + hl * 64 + 64]:
                                 e.transpose(p, src, idf[:, :]), inc=True)
                    pl.wait("S", "T", t_tr)
                    if hl == 0:
                        dstp = pT[0:64, dc * 1024 + ocp * 128: dc * 1024 + (ocp + 1) * 128]
                    else:
                        if a_pj_last:
                            pl.wait("S", QA, a_pj_last)
                        dstp = pTstage[0:64, dc * 128:(dc + 1) * 128]
                    s_pjc_last[slot] = pl.op("S", lambda e, d=dstp, p=psl: e.copy(d, p), inc=True)
                cvst_free[par] = ("T", pl.ctr["T"])
                pl.wait(QA, "S", s_pjc_last[1])
                dhi = pThi3[64:128, :, ocp * 128:(ocp + 1) * 128]
                shi = pTstage[0:64, :].rearrange("p (a c) -> p a c", c=128)
                a_pj_last = pl.dma(QA, dhi, shi)
            if v_sdone_hist:
                pl.wait(QA, "V", v_sdone_hist[-1])
            pl.wait(QA, "S", s_b2)
            a_lnag = pl.dma(QA, bcA[:, 0:1024], RawAP(lnAg.tensor, 0, [[0, 128], [1, 1024]]))
            a_lnab = pl.dma(QA, bcA[:, 1024:2048], RawAP(lnAb.tensor, 0, [[0, 128], [1, 1024]]))
            a_pjb = pl.dma(QA, bcA[:, 2048:3072], RawAP(pb.tensor, 0, [[0, 128], [1, 1024]]))
            ps_free4 = {0: None, 1: None}
            pl.waitp("T", tr_free[0])
            pl.waitp("T", tr_free[1])
            pl.waitp("T", omm_free)
            for tile in range(NT):
                par = tile % 2
                colix = b * 16 + tile
                pl.waitp(QA, xfb_free[par])
                a_x = pl.dma(QA, xfb[par][:, :], x_in[b, tile * 128:(tile + 1) * 128, :])
                ps = P0 if tile % 2 == 0 else P1
                pl.wait("T", QA, a_pj_last)
                pl.wait("T", QA, a_oTh.get(15))
                pl.wait("T", "S", s_pjc_last[1])
                pl.wait("T", "S", s_oev[(14, 3)])
                pl.waitp("T", ps_free4[tile % 2])
                t_pm = None
                for dc in range(8):
                    for ec in range(2):
                        lhs = oTpack[:, dc * 2048 + tile * 128: dc * 2048 + (tile + 1) * 128]
                        rhs = pT[:, dc * 1024 + ec * 512: dc * 1024 + ec * 512 + 512]
                        t_pm = pl.op("T", lambda e, o=ps[:, ec * 512:ec * 512 + 512], l=lhs, r=rhs,
                                     st=(dc == 0), sp=(dc == 7): e.matmul(o, l, r, start=st, stop=sp),
                                     inc=(dc == 7 and ec == 1))
                t_pm_last = t_pm
                pl.wait("V", "T", t_pm)
                pl.wait("V", QA, max(a_x, a_pjb))
                pl.waitp("V", wk_free[0])
                pl.op("V", lambda e, p=ps[:, 0:1024]: e.tensor_add(wk[0][:, :], p, bcA[:, 2048:3072]), wide=True)
                pl.op("V", lambda e, c=colix, xp=xfb[par]: e.tensor_scalar(
                    wk[1][:, :], xp[:, :], nm_st[:, c:c + 1], rs_st[:, c:c + 1],
                    op0=ALU.add, op1=ALU.mult), wide=True)
                v_y0 = pl.op("V", lambda e: e.tensor_add(wk[0][:, :], wk[0][:, :], wk[1][:, :]), inc=True, wide=True)
                xfb_free[par] = ("V", v_y0)
                ps_free4[tile % 2] = ("V", v_y0)
                pl.vred(lambda e: e.reduce_sum(col[:, 7:8], wk[0][:, :], axis=AX.X))
                pl.op("V", lambda e: e.tensor_scalar_mul(col[:, 8:9], col[:, 7:8], -1.0 / D))
                pl.op("V", lambda e: e.tensor_scalar_add(wk[1][:, :], wk[0][:, :], col[:, 8:9]), wide=True)
                pl.op("V", lambda e: e.tensor_mul(wk[2][:, :], wk[1][:, :], wk[1][:, :]), wide=True)
                v_s2 = pl.op("V", lambda e: e.reduce_sum(col[:, 9:10], wk[2][:, :], axis=AX.X), inc=True)
                pl.wait("S", "V", v_s2)
                s_q2 = pl.op("S", lambda e: e.activation(
                    col[:, 10:11], col[:, 9:10], AF.Sqrt, bias=epss[:, 0:1], scale=1.0 / D), inc=True)
                pl.wait("V", "S", s_q2)
                pl.op("V", lambda e: e.reciprocal(col[:, 11:12], col[:, 10:11]))
                pl.op("V", lambda e: e.tensor_scalar_mul(wk[1][:, :], wk[1][:, :], col[:, 11:12]), wide=True)
                pl.op("V", lambda e: e.tensor_mul(wk[1][:, :], wk[1][:, :], bcA[:, 0:1024]), wide=True)
                pl.waitp("V", sfb_free[par])
                v_sd = pl.op("V", lambda e, d=sfb[par][:, :]: e.tensor_add(
                    d, wk[1][:, :], bcA[:, 1024:2048]), inc=True, wide=True)
                wk_free[0] = ("V", v_sd)
                pl.wait(QA, "V", v_sd)
                a_ss = pl.dma(QA, s_sc[b, tile], sfb[par][:, :])
                s_ss_cnt[(b, tile)] = a_ss
                sfb_free[par] = (QA, a_ss)
                v_sdone_hist.append(v_sd)

        emit_conv(10 ** 9)
        BAR()
        conv_all_store = pl.ctr[QA]
        conv_tail_loads = max(x for x in conv["load"] if x) if any(conv["load"]) else 0
        conv_tail_casts = [conv["cast"][0], conv["cast"][1]]

        pl.wait(QA, "V", v_sdone_hist[-1])
        a_lnfg = pl.dma(QA, bcA[:, 0:1024], RawAP(lnFg.tensor, 0, [[0, 128], [1, 1024]]))
        a_lnfb = pl.dma(QA, bcA[:, 1024:2048], RawAP(lnFb.tensor, 0, [[0, 128], [1, 1024]]))
        a_sb2 = pl.dma(QA, bcA[:, 2048:3072], RawAP(sb2.tensor, 0, [[0, 128], [1, 1024]]))
        pl.wait("V", QA, max(a_sb2, a_sigc))
        v_sb2 = pl.op("V", lambda e: e.tensor_scalar_mul(
            bcA[:, 2048:3072], bcA[:, 2048:3072], sigc[:, 0:1]), inc=True, wide=True)

        wcnt = {"n": 0}
        w1t_free = [None, None, None]
        psh_free = [None, None]
        psy_free = [None, None]
        w2_free = ("T", t_pm_last)
        xnt_free = ("T", t_pm_last)
        h2_free = ("T", pl.ctr["T"])
        yacc_freeV = pl.ctr["V"]
        yacc_freeS = pl.ctr["S"]
        a_out = None
        out_free = [None, None]
        v_out_hist = [None, None]
        for tb in range(NTB):
            BAR()
            b, half = tb // 2, tb % 2
            pl.waitp(QA, xnt_free)
            pl.wait(QA, QA, xnt_store_cnt[b])
            a_xt = None
            for dc in range(8):
                a_xt = pl.dma(QA, xnT_tb[:, dc * 1024:(dc + 1) * 1024],
                              xnT_sc[b, dc, :, half * 1024:half * 1024 + 1024])
            pl.dfence()
            v_comb = {}
            s_ctc = {}
            for ts_ in range(TSUB):
                BAR()
                pl.wait("T", QA, a_xt)
                if tb == 0 and ts_ == 0:
                    pl.wait("T", "V", v_sdone_hist[-1])
                    pl.wait("T", "V", yacc_freeV)
                pl.waitp("T", v_comb.get(ts_ - 1) and ("V", v_comb[ts_ - 1]))
                t_rt = None
                for dc in range(8):
                    t_rt = pl.op("T", lambda e, o=P0[:, 0:8],
                                 l=xnT_tb[:, dc * 1024 + ts_ * 128: dc * 1024 + (ts_ + 1) * 128],
                                 r=rw_sb[:, dc * 8:(dc + 1) * 8],
                                 st=(dc == 0), sp=(dc == 7): e.matmul(o, l, r, start=st, stop=sp),
                                 inc=(dc == 7))
                pl.wait("V", "T", t_rt)
                pl.wait("V", QA, a_rb)
                pl.op("V", lambda e: e.tensor_add(pr0[:, :], P0[:, 0:8], rb_b[:, :]))
                v_sm = pl.op("V", lambda e: e.reduce_max(
                    col[:, 12:13], pr0[:, :], axis=AX.X, negate=True), inc=True)
                pl.wait("S", "V", v_sm)
                s_e8 = pl.op("S", lambda e: e.activation(
                    pre[:, :], pr0[:, :], AF.Exp, bias=col[:, 12:13]), inc=True)
                pl.wait("V", "S", s_e8)
                pl.vred(lambda e: e.reduce_sum(col[:, 13:14], pre[:, :], axis=AX.X))
                pl.op("V", lambda e: e.reciprocal(col[:, 13:14], col[:, 13:14]))
                pl.op("V", lambda e: e.tensor_scalar_mul(prb[:, :], pre[:, :], col[:, 13:14]))
                pl.vred(lambda e: e.reduce_max(col[:, 14:15], prb[:, :], axis=AX.X))
                pl.op("V", lambda e: e.tensor_scalar(msk8[:, :], prb[:, :], col[:, 14:15],
                                                     None, op0=ALU.is_lt))
                pl.op("V", lambda e: e.tensor_mul(msk8[:, :], msk8[:, :], prb[:, :]))
                pl.vred(lambda e: e.reduce_max(col[:, 15:16], msk8[:, :], axis=AX.X))
                pl.op("V", lambda e: e.tensor_scalar(msk8[:, :], prb[:, :], col[:, 15:16],
                                                     None, op0=ALU.is_ge))
                pl.op("V", lambda e: e.tensor_mul(msk8[:, :], msk8[:, :], prb[:, :]))
                pl.op("V", lambda e: e.tensor_tensor(col[:, 14:15], col[:, 14:15],
                                                     col[:, 15:16], op=ALU.add))
                pl.op("V", lambda e: e.reciprocal(col[:, 14:15], col[:, 14:15]))
                v_comb[ts_] = pl.op("V", lambda e, d=comb_sb[:, ts_ * 8:(ts_ + 1) * 8]:
                                    e.tensor_scalar_mul(d, msk8[:, :], col[:, 14:15]), inc=True)
                pl.wait("T", "V", v_comb[ts_])
                pl.waitp("T", s_ctc.get(ts_ - 1) and ("S", s_ctc[ts_ - 1]))
                t_ct = pl.op("T", lambda e, o=P1[0:8, 512:640],
                             src=comb_sb[:, ts_ * 8:(ts_ + 1) * 8]:
                             e.transpose(o, src, idf[:, :]), inc=True)
                pl.wait("S", "T", t_ct)
                s_ctc[ts_] = pl.op("S", lambda e, d=combT[0:8, ts_ * 128:(ts_ + 1) * 128],
                                   p=P1[0:8, 512:640]: e.copy(d, p), inc=True)
            v_yinit = None
            for ts_ in range(TSUB):
                pl.wait("T", "S", max(s_ctc[ts_], s_b2))
                if ts_ > 0:
                    pl.wait("T", "V", v_yinit)
                elif tb == 0:
                    pl.wait("T", "V", yacc_freeV)
                    pl.wait("T", "S", yacc_freeS)
                t_b2 = None
                for dh in range(2):
                    t_b2 = pl.op("T", lambda e, o=P1[:, dh * 512:dh * 512 + 512],
                                 l=combT[0:8, ts_ * 128:(ts_ + 1) * 128],
                                 r=b2_bf[0:8, dh * 512:dh * 512 + 512]:
                                 e.matmul(o, l, r, start=True, stop=True), inc=(dh == 1))
                pl.wait("V", "T", t_b2)
                pl.wait("V", "V", 0 if v_sb2 is None else v_sb2)
                if tb == 0 and ts_ >= 6:
                    pl.wait("V", QA, conv_tail_loads)
                    for cc in conv_tail_casts:
                        pl.waitp("V", cc)
                v_yinit = pl.op("V", lambda e, d=y_acc[:, ts_ * 1024:(ts_ + 1) * 1024]:
                                e.tensor_add(d, P1[:, 0:1024], bcA[:, 2048:3072]), inc=True, wide=True)
            v_yinit_all = v_yinit
            s_h2 = {}
            v_yev_last = {}
            for e_ in range(9):
                Fh = 8 if e_ == 8 else 16
                shared = (e_ == 8)
                for fh in range(2):
                    a_w2h = None
                    w2q = 0
                    for fc in range(Fh):
                        fcg = fh * Fh + fc
                        slot = wcnt["n"] % 3
                        pl.waitp(QW, w1t_free[slot])
                        if wcnt["n"] == 0:
                            pl.wait(QW, QA, conv_tail_loads)
                            for cc in conv_tail_casts:
                                pl.waitp(QW, cc)
                            pl.wait(QW, QA, pl.ctr[QA])
                        if shared:
                            srcw = RawAP(sw1b.tensor, fcg * 128, [[F2, 128], [128 * F2, 8], [1, 128]])
                        else:
                            srcw = RawAP(w1b.tensor, e_ * D * F + fcg * 128,
                                         [[F, 128], [128 * F, 8], [1, 128]])
                        a_w1 = pl.dma(QW, w1t[slot][:, :], srcw)
                        wcnt["n"] += 1
                        psh = P0[:, (fc % 2) * 1024:(fc % 2) * 1024 + 1024]
                        pl.wait("T", QW, a_w1)
                        if tb == 0 and e_ == 0 and fh == 0 and fc < 2:
                            pl.wait("T", "V", v_comb[TSUB - 1])
                        pl.waitp("T", psh_free[fc % 2])
                        if fc == 0 and fh == 0:
                            pl.waitp("T", w2_free)
                        t_h = None
                        for t5 in range(2):
                            for dc in range(8):
                                t_h = pl.op("T", lambda e, o=psh[:, t5 * 512:t5 * 512 + 512],
                                            l=w1t[slot][:, dc * 128:(dc + 1) * 128],
                                            r=xnT_tb[:, dc * 1024 + t5 * 512: dc * 1024 + t5 * 512 + 512],
                                            st=(dc == 0), sp=(dc == 7):
                                            e.matmul(o, l, r, start=st, stop=sp),
                                            inc=(dc == 7 and t5 == 1))
                        w1t_free[slot] = ("T", t_h)
                        pl.wait("S", "T", t_h)
                        pl.wait("S", "V", v_bgs if shared else v_bg)
                        if shared:
                            bcol, gcol = bgs[:, fcg:fcg + 1], gts[:, fcg:fcg + 1]
                        else:
                            bcol = bg_all[:, e_ * 32 + fcg:e_ * 32 + fcg + 1]
                            gcol = gt_all[:, e_ * 32 + fcg:e_ * 32 + fcg + 1]
                        s_h2[fc] = pl.op(
                            "S", lambda e, d=h2h[:, fc * 1024:(fc + 1) * 1024], p=psh,
                            bc=bcol, gc=gcol:
                            e.activation(d, p, AF.Gelu, bias=bc, scale=gc), inc=True)
                        psh_free[fc % 2] = ("S", s_h2[fc])
                        nw2 = Fh // 2
                        if fc < nw2:
                            for sub in range(2):
                                fcl = fc * 2 + sub
                                if tb == 0 and e_ == 0 and fh == 0:
                                    pl.wait(QA, QA, conv_all_store)
                                pl.waitp(QA, w2_free)
                                if shared:
                                    src2 = RawAP(sw2b.tensor, (fh * Fh + fcl) * 128 * D,
                                                 [[D, 128], [1, D]])
                                else:
                                    src2 = RawAP(w2b.tensor, e_ * F * D + (fh * Fh + fcl) * 128 * D,
                                                 [[D, 128], [1, D]])
                                a_w2h = pl.dma(QA, w2h[:, fcl * 1024:(fcl + 1) * 1024], src2)
                    t_y = None
                    for ts_ in range(TSUB):
                        psy = P1[:, (ts_ % 2) * 1024:(ts_ % 2) * 1024 + 1024]
                        pl.wait("T", "S", s_h2[Fh - 1])
                        pl.wait("T", QA, a_w2h)
                        if tb == 0 and e_ == 0 and fh == 0 and ts_ < 2:
                            pl.wait("T", "V", v_yinit_all)
                        pl.waitp("T", psy_free[ts_ % 2])
                        for dh in range(2):
                            for fc in range(Fh):
                                t_y = pl.op("T", lambda e, o=psy[:, dh * 512:dh * 512 + 512],
                                            l=h2h[:, fc * 1024 + ts_ * 128: fc * 1024 + (ts_ + 1) * 128],
                                            r=w2h[:, fc * 1024 + dh * 512: fc * 1024 + dh * 512 + 512],
                                            st=(fc == 0), sp=(fc == Fh - 1):
                                            e.matmul(o, l, r, start=st, stop=sp),
                                            inc=(fc == Fh - 1 and dh == 1))
                        ccol = sigc[:, 0:1] if shared else comb_sb[:, ts_ * 8 + e_: ts_ * 8 + e_ + 1]
                        wkb = wk[ts_ % 2]
                        pl.wait("S", "T", t_y)
                        pl.waitp("S", wk_free[ts_ % 2])
                        s_yt = pl.op("S", lambda e, d=wkb[:, :], p=psy, c=ccol:
                                     e.mul(d, p, c), inc=True)
                        pl.wait("V", "S", s_yt)
                        v_ye = pl.op("V", lambda e, d=y_acc[:, ts_ * 1024:(ts_ + 1) * 1024],
                                     w_=wkb: e.tensor_add(d, d, w_[:, :]), inc=True, wide=True)
                        wk_free[ts_ % 2] = ("V", v_ye)
                        psy_free[ts_ % 2] = ("V", v_ye)
                        v_yev_last[ts_] = v_ye
                    w2_free = ("T", t_y)
            for ts_ in range(TSUB):
                BAR()
                par = ts_ % 2
                tile = half * 8 + ts_
                pl.waitp(QA, xfb_free[par])
                a_x = pl.dma(QA, xfb[par][:, :], x_in[b, tile * 128:(tile + 1) * 128, :])
                pl.waitp(QA, sfb_free[par])
                pl.wait(QA, QA, s_ss_cnt[(b, tile)])
                a_s = pl.dma(QA, sfb[par][:, :], s_sc[b, tile])
                ysl = y_acc[:, ts_ * 1024:(ts_ + 1) * 1024]
                pl.wait("V", "V", v_yev_last[ts_])
                pl.waitp("V", wk_free[2])
                pl.waitp("V", wk_free[0])
                pl.vred(lambda e, y=ysl: e.reduce_sum(col[:, 7:8], y, axis=AX.X))
                pl.op("V", lambda e: e.tensor_scalar_mul(col[:, 8:9], col[:, 7:8], -1.0 / D))
                pl.op("V", lambda e, y=ysl: e.tensor_scalar_add(wk[2][:, :], y, col[:, 8:9]), wide=True)
                pl.op("V", lambda e: e.tensor_mul(wk[0][:, :], wk[2][:, :], wk[2][:, :]), wide=True)
                v_ss = pl.op("V", lambda e: e.reduce_sum(col[:, 9:10], wk[0][:, :], axis=AX.X), inc=True)
                pl.wait("S", "V", v_ss)
                s_sq = pl.op("S", lambda e: e.activation(
                    col[:, 10:11], col[:, 9:10], AF.Sqrt, bias=epss[:, 0:1], scale=1.0 / D), inc=True)
                pl.wait("V", "S", s_sq)
                pl.op("V", lambda e: e.reciprocal(col[:, 11:12], col[:, 10:11]))
                pl.op("V", lambda e: e.tensor_scalar_mul(wk[2][:, :], wk[2][:, :], col[:, 11:12]), wide=True)
                pl.op("V", lambda e: e.tensor_mul(wk[2][:, :], wk[2][:, :], bcA[:, 0:1024]), wide=True)
                v_t2 = pl.op("V", lambda e: e.tensor_add(wk[0][:, :], wk[2][:, :],
                                                         bcA[:, 1024:2048]), inc=True, wide=True)
                pl.wait("S", "V", v_t2)
                s_g = pl.op("S", lambda e: e.activation(wk[2][:, :], wk[0][:, :], AF.Silu), inc=True)
                pl.wait("V", "S", s_g)
                pl.wait("V", QA, max(a_s, a_x))
                pl.op("V", lambda e, sp=sfb[par]: e.tensor_mul(wk[0][:, :], wk[2][:, :], sp[:, :]), wide=True)
                v_out = pl.op("V", lambda e, xp=xfb[par]: e.tensor_add(
                    wk[0][:, :], wk[0][:, :], xp[:, :]), inc=True, wide=True)
                xfb_free[par] = ("V", v_out)
                sfb_free[par] = ("V", v_out)
                wk_free[2] = ("V", v_out)
                pl.wait(QA, "V", v_out)
                a_out = pl.dma(QA, out_d[b, tile * 128:(tile + 1) * 128, :], wk[0][:, :])
                wk_free[0] = (QA, a_out)
        BAR()
        pl.wait(QA, QA, pl.ctr[QA])
        pl.wait(QW, QA, pl.ctr[QA])

        for oplist in pl.ops.values():
            for f in oplist:
                if hasattr(f, "semname"):
                    f.sem = sems[f.semname]
        ctx.enter_context(nc.allow_non_contiguous_dma(
            reason="small one-time column-layout constant loads"))

        @block.sync
        def _(e):
            for f in pl.ops["W"]:
                f(e)

        @block.gpsimd
        def _(e):
            for f in pl.ops["A"]:
                f(e)

        @block.tensor
        def _(e):
            for f in pl.ops["T"]:
                f(e)

        @block.scalar
        def _(e):
            for f in pl.ops["S"]:
                f(e)

        @block.vector
        def _(e):
            for f in pl.ops["V"]:
                f(e)

    return nc


_NC_CACHE = None


def kernel(**inputs):
    global _NC_CACHE
    x = np.ascontiguousarray(np.asarray(inputs["L0_attn_x"], dtype=np.float32))
    rpb = np.asarray(inputs["L0_attn_rpb"], dtype=np.float32)

    p = np.arange(128)[:, None]
    c = np.arange(2432)[None, :]
    idx = np.clip(c + 127 - p, 0, 4094)
    biasz = rpb[idx, :]
    biasz = np.transpose(biasz, (2, 0, 1)).copy()
    biasz[:, (c > p + 1920)] = -1e30
    biasz = np.ascontiguousarray(biasz.astype(np.float32))

    identf = np.eye(128, dtype=np.float32)
    epsc = np.array([1e-5, 0.0, 0.0, 0.0], dtype=np.float32)

    if _NC_CACHE is None:
        _NC_CACHE = build_program()
    nc = _NC_CACHE

    wnames = ["L0_attn_W_w", "L0_attn_W_b", "L0_attn_proj_w", "L0_attn_proj_b",
              "L0_attn_ln_g", "L0_attn_ln_b", "L0_ffn_router_w", "L0_ffn_router_b",
              "L0_ffn_w1", "L0_ffn_b1", "L0_ffn_gate", "L0_ffn_w2", "L0_ffn_b2",
              "L0_ffn_sw1", "L0_ffn_sb1", "L0_ffn_sgate", "L0_ffn_sw2", "L0_ffn_sb2",
              "L0_ffn_ln_g", "L0_ffn_ln_b"]
    base = {k: np.ascontiguousarray(np.asarray(inputs[k], dtype=np.float32)) for k in wnames}
    base["L0_ffn_shared_weight"] = np.asarray(inputs["L0_ffn_shared_weight"],
                                              dtype=np.float32).reshape(1)
    base["biasz"] = biasz
    base["identf"] = identf
    base["epsc"] = epsc

    in_maps = []
    for ci in range(8):
        m = dict(base)
        m["x"] = np.ascontiguousarray(x[ci * 2:(ci + 1) * 2])
        in_maps.append(m)

    res = run_bass_kernel_spmd(nc, in_maps, core_ids=list(range(8)))
    return np.concatenate([r["out"] for r in res.results], axis=0)

